# version 06
# baseline (speedup 1.0000x reference)
from contextlib import ExitStack

import numpy as np
import concourse.bass as bass
import concourse.mybir as mybir
from concourse.bass_utils import run_bass_kernel_spmd

F32 = mybir.dt.float32
F16 = mybir.dt.float16
I32 = mybir.dt.int32
AF = mybir.ActivationFunctionType
ALU = mybir.AluOpType
AX = mybir.AxisListType

NCORES = 8
LIMIT = 99
C_BLOCKS = 32
C_STEP = 9
NB = 32
TCORE = NB * 128
D = 1024
I3 = 3072
NIC = 24
H, HS = 16, 64
QMAX = 32767.0
WQMAX = 7.0
EPS = 1e-5
NEG = -1.0e30


class Ct:
    def __init__(self, ctx, nc, name):
        self.h = ctx.enter_context(nc.semaphore(name))
        self.n = 0


class Eng:
    def __init__(self, e):
        self.e = e
        self.waited = {}

    def wait(self, deps):
        for ct, val in deps:
            if val is None or val <= 0:
                continue
            if self.waited.get(id(ct), -1) >= val:
                continue
            self.e.wait_ge(ct.h, val)
            self.waited[id(ct)] = val


def build_program():
    nc = bass.Bass("TRN2", target_bir_lowering=False, debug=False)

    x_d = nc.dram_tensor("x", [TCORE, D], F32, kind="ExternalInput").ap()
    wq_d = nc.dram_tensor("wq", [H, D, HS], F32, kind="ExternalInput").ap()
    wk_d = nc.dram_tensor("wk", [H, D, HS], F32, kind="ExternalInput").ap()
    wv_d = nc.dram_tensor("wv", [H, D, HS], F32, kind="ExternalInput").ap()
    pw_d = nc.dram_tensor("pw", [D, D], F32, kind="ExternalInput").ap()
    pb_d = nc.dram_tensor("pb", [1, D], F32, kind="ExternalInput").ap()
    wg_d = nc.dram_tensor("wg", [I3, D], F32, kind="ExternalInput").ap()
    wu_d = nc.dram_tensor("wu", [I3, D], F32, kind="ExternalInput").ap()
    wd_d = nc.dram_tensor("wd", [D, I3], F32, kind="ExternalInput").ap()
    out_d = nc.dram_tensor("out", [TCORE, D], F32, kind="ExternalOutput").ap()

    qk_s = nc.dram_tensor("qk_s", [NB, 4096, 128], F16).ap()
    v_s = nc.dram_tensor("v_s", [NB, 2048, 64], F16).ap()
    af_s = nc.dram_tensor("af_s", [NB, 128, D], F16).ap()
    up_s = nc.dram_tensor("up_s", [NIC, 128, TCORE], F16).ap()
    gt_s = nc.dram_tensor("gt_s", [NIC, 128, TCORE], F16).ap()
    o_s = nc.dram_tensor("o_s", [NIC, 128, TCORE], F16).ap()
    wdq_u = nc.dram_tensor("wdq_u", [NIC, 128, D], F16).ap()
    wdq_g = nc.dram_tensor("wdq_g", [NIC, 128, D], F16).ap()
    wdq_dd = nc.dram_tensor("wdq_dd", [8, 128, I3], F16).ap()
    wfm_u = nc.dram_tensor("wfm_u", [NIC, 128, 1024], F16).ap()
    wfm_g = nc.dram_tensor("wfm_g", [NIC, 128, 1024], F16).ap()
    wfm_d = nc.dram_tensor("wfm_d", [NIC, 128, 1024], F16).ap()
    pw16 = nc.dram_tensor("pw16", [8, 128, H, 128], F16).ap()
    l_s = nc.dram_tensor("l_s", [NB, 2048], F16).ap()
    rl_s = nc.dram_tensor("rl_s", [NB, 2048], F16).ap()
    cc_i = [nc.dram_tensor(f"cc{i}i", [1, 8], F32).ap() for i in range(3)]
    cc_o = [nc.dram_tensor(f"cc{i}o", [1, 8], F32).ap() for i in range(3)]

    ctx = ExitStack()
    scopes = {"A": ExitStack(), "B": ExitStack(), "C": ExitStack(),
              "D": ExitStack()}
    cms = {}

    def sb(name, shape, dtype, scope=None):
        if scope is None:
            cm = nc.sbuf_tensor(name, shape, dtype)
            t = ctx.enter_context(cm)
        elif scope == "manual":
            cm = nc.sbuf_tensor(name, shape, dtype)
            t = cm.__enter__()
        else:
            cm = nc.sbuf_tensor(name, shape, dtype, side="right")
            t = scopes[scope].enter_context(cm)
        cms[name] = cm
        return t

    idn32 = sb("idn32", [128, 128], F32)
    mtri = sb("mtri", [128, 128], F32)
    onesC = sb("onesC", [128, 128], F32)
    onesC16 = sb("onesC16", [128, 128], F16)
    stats = sb("stats", [128, 8, 10], F32)
    rmax = sb("rmax", [128, 8], F32)
    red8 = sb("red8", [8, 1], F32)
    sc8 = sb("sc8", [1, 16], F32)
    ss = sb("ss", [1, 16], F32)
    ssi = sb("ssi", [1, 2], I32)
    scb = sb("scb", [128, 8], F32)
    scb2 = sb("scb2", [128, 2], F32)
    cc_sb = sb("cc_sb", [1, 8], F32)
    tmpB = sb("tmpB", [128, 2], F32)
    tmpF = sb("tmpF", [128, 2], F32)
    WQKV = sb("WQKV", [128, 8, 3072], F16, "manual")
    xT = sb("xT", [128, 8, TCORE], F16, "manual")


    pbk = [ctx.enter_context(nc.psum_tensor(f"pbk{i}", [128, 1024], F32))
           for i in range(4)]

    sPE = Ct(ctx, nc, "sPE")
    sACT = Ct(ctx, nc, "sACT")
    sDVE = Ct(ctx, nc, "sDVE")
    sPOOL = Ct(ctx, nc, "sPOOL")
    sDMA = Ct(ctx, nc, "sDMA")
    sGD = Ct(ctx, nc, "sGD")
    sCC = Ct(ctx, nc, "sCC")

    prog = {k: [] for k in ("sync", "tensor", "scalar", "vector", "gpsimd")}
    ENG = {"tensor": sPE, "scalar": sACT, "vector": sDVE, "gpsimd": sPOOL}

    def emit(engine, builder, deps=()):
        sem = ENG[engine]
        sem.n += 1
        deps = [d for d in deps if d is not None and d != 0]
        def fn(E, builder=builder, deps=deps, sem=sem):
            E.wait(deps)
            ins = builder(E)
            ins.then_inc(sem.h, 1)
        prog[engine].append(fn)
        return sem.n

    dsems = {}

    def dsem(key):
        if key not in dsems:
            dsems[key] = Ct(ctx, nc, "dk_%s" % "_".join(str(x) for x in (
                key if isinstance(key, tuple) else (key,))))
        return dsems[key]

    def dma(engine, key, out, in_, deps=(), **kw):
        sem = dsem(key)
        sem.n += 16
        deps = [d for d in deps if d is not None and d != 0]
        def fn(E, out=out, in_=in_, deps=deps, kw=kw, sem=sem):
            E.wait(deps)
            ins = E.e.dma_start(out=out, in_=in_, **kw)
            ins.then_inc(sem.h, 16)
        prog[engine].append(fn)
        return (sem, sem.n)

    def dmat(key, out, in_, deps=()):
        sem = dsem(key)
        sem.n += 16
        deps = [d for d in deps if d is not None and d != 0]
        def fn(E, out=out, in_=in_, deps=deps, sem=sem):
            E.wait(deps)
            ins = E.e.dma_start_transpose(out, in_)
            ins.then_inc(sem.h, 16)
        prog["sync"].append(fn)
        return (sem, sem.n)

    def collective(cci, cco, deps):
        sCC.n += 1
        deps = [d for d in deps if d is not None]
        def fn(E, deps=deps):
            E.wait(deps)
            ins = E.e.collective_compute(
                "AllReduce", ALU.max, replica_groups=[list(range(NCORES))],
                ins=[cci.opt()], outs=[cco.opt()])
            ins.then_inc(sCC.h, 1)
        prog["gpsimd"].append(fn)
        return sCC.n

    d_iota = emit("gpsimd", lambda E: E.e.iota(
        iotaf[:], pattern=[[1, 128]], base=0, channel_multiplier=-1,
        allow_small_or_imprecise_dtypes=True))
    d_idn32 = emit("vector", lambda E: E.e.tensor_scalar(
        idn32[:], iotaf[:], 0.0, None, ALU.is_equal), deps=[(sPOOL, d_iota)])
    d_m01 = emit("vector", lambda E: E.e.tensor_scalar(
        mtri[:], iotaf[:], 0.0, None, ALU.is_ge), deps=[(sPOOL, d_iota)])
    d_mtri = emit("vector", lambda E: E.e.tensor_scalar(
        mtri[:], mtri[:], 1.0, -NEG, ALU.subtract, ALU.mult),
        deps=[(sDVE, d_m01)])
    d_ones = emit("vector", lambda E: E.e.memset(onesC[:], 1.0))
    d_ones = emit("vector", lambda E: E.e.memset(onesC16[:], 1.0),
                  deps=[(sDVE, d_ones)])
    d_rm0 = emit("vector", lambda E: E.e.memset(rmax[:], 0.0))
    d_st0 = emit("vector", lambda E: E.e.memset(stats[:], 1.0))
    d_rm1 = emit("vector", lambda E: E.e.tensor_scalar(
        rmax[:, 3:4], rmax[:, 3:4], 0.0, NEG, ALU.mult, ALU.add),
        deps=[(sDVE, d_rm0)])

    iotaf = sb("iotaf", [128, 128], F32, "A")
    xa = sb("xa", [128, 4, D], F32, "A")
    xn32 = sb("xn32", [128, 2, D], F32, "A")
    sqdmp = sb("sqdmp", [128, D], F32, "A")
    pw_f32 = sb("pw_f32", [128, 2, D], F32, "A")
    pw_cast = sb("pw_cast", [128, 2, D], F16, "A")
    WDH = 1536
    wch = sb("wch", [128, 2, WDH], F32, "A")
    dqmax = sb("dqmax", [128, 2, 48], F32, "A")
    dqs = sb("dqs", [128, 2, 48], F32, "A")
    dqsr = sb("dqsr", [128, 2, 48], F32, "A")
    dqz = sb("dqz", [128, 2, WDH], F32, "A")
    dqi = sb("dqi", [128, 2, WDH], I32, "A")
    dq16 = sb("dq16", [128, 2, WDH], F16, "A")
    ftA = sb("ftA", [128, 2, 12, 128], F16, "A")



    wqkv_done = 0
    for fc in range(8):
        for wsrc, base in ((wq_d, 0), (wk_d, 1024), (wv_d, 2048)):
            src = wsrc[:, fc * 128:(fc + 1) * 128, :].rearrange("i f d -> f i d")
            dst = WQKV[:, fc, base:base + 1024].rearrange("p (i d) -> p i d", d=HS)
            wqkv_done = dma("gpsimd", "wqkv", dst, src)

    pw_st = [0] * 8
    pw_cv = [0] * 8
    for ec in range(8):
        pg = ec % 2
        dl = dma("sync", ("pwf", pg), pw_f32[:, pg, :], pw_d[ec * 128:(ec + 1) * 128, :],
                 deps=[(sACT, pw_cv[ec - 2]) if ec >= 2 else None])
        cv = emit("scalar", lambda E, pg=pg: E.e.activation(
            pw_cast[:, pg, :], pw_f32[:, pg, :], AF.Copy),
            deps=[dl, pw_st[ec - 2] if ec >= 2 else None])
        pw_cv[ec] = cv
        pw_st[ec] = dma("sync", ("pw16", pg), pw16[ec, :, :, 0:64],
                        pw_cast[:, pg, :].rearrange("p (i d) -> p i d", d=64),
                        deps=[(sACT, cv)])

    trp = pbk[3]
    last_tr_copy = 0
    last_sq = 0
    last_xn = [0, 0]
    xa_free = [0] * 4
    amx_of = [0] * 4
    a_mx = d_rm1
    for g in range(8):
        xa_load = [0] * 4
        for j in range(4):
            b = g * 4 + j
            xa_load[j] = dma("sync", ("xa", j), xa[:, j, :], x_d[b * 128:(b + 1) * 128, :],
                             deps=[(sACT, xa_free[j]) if g > 0 else None])
        for j in range(4):
            last_sq = emit("scalar", lambda E, j=j: E.e.activation(
                sqdmp[:], xa[:, j, :], AF.Square, accum_out=stats[:, j, 0:1]),
                deps=[xa_load[j], (sACT, last_sq), (sDVE, amx_of[j])])
            emit("vector", lambda E, j=j: E.e.reduce_sum(
                stats[:, j, 1:2], xa[:, j, :], AX.X), deps=[xa_load[j]])
        vv = sDVE.n
        vv = emit("vector", lambda E: E.e.tensor_scalar(
            stats[:, :, 2:3].opt(), stats[:, :, 1:2].opt(), 1.0 / D, None,
            ALU.mult), deps=[(sDVE, vv)])
        vv = emit("vector", lambda E: E.e.tensor_scalar(
            stats[:, :, 3:4].opt(), stats[:, :, 0:1].opt(), 1.0 / D, None,
            ALU.mult), deps=[(sACT, last_sq), (sDVE, vv)])
        vv = emit("vector", lambda E: E.e.tensor_tensor(
            stats[:, :, 4:5].opt(), stats[:, :, 2:3].opt(),
            stats[:, :, 2:3].opt(), ALU.mult), deps=[(sDVE, vv)])
        vv = emit("vector", lambda E: E.e.tensor_tensor(
            stats[:, :, 5:6].opt(), stats[:, :, 3:4].opt(),
            stats[:, :, 4:5].opt(), ALU.subtract), deps=[(sDVE, vv)])
        vv = emit("vector", lambda E: E.e.tensor_scalar(
            stats[:, :, 5:6].opt(), stats[:, :, 5:6].opt(), 1.0, EPS, ALU.mult,
            ALU.add), deps=[(sDVE, vv)])
        vv = emit("vector", lambda E: E.e.memset(stats[:, :, 6:7].opt(), 1.0),
                  deps=[(sDVE, vv)])
        for it in range(7):
            vv = emit("vector", lambda E: E.e.tensor_tensor(
                stats[:, :, 7:8].opt(), stats[:, :, 6:7].opt(),
                stats[:, :, 6:7].opt(), ALU.mult), deps=[(sDVE, vv)])
            vv = emit("vector", lambda E: E.e.tensor_tensor(
                stats[:, :, 8:9].opt(), stats[:, :, 7:8].opt(),
                stats[:, :, 5:6].opt(), ALU.mult), deps=[(sDVE, vv)])
            vv = emit("vector", lambda E: E.e.tensor_scalar(
                stats[:, :, 8:9].opt(), stats[:, :, 8:9].opt(), -0.5, 1.5,
                ALU.mult, ALU.add), deps=[(sDVE, vv)])
            vv = emit("vector", lambda E: E.e.tensor_tensor(
                stats[:, :, 6:7].opt(), stats[:, :, 6:7].opt(),
                stats[:, :, 8:9].opt(), ALU.mult), deps=[(sDVE, vv)])
        vv = emit("vector", lambda E: E.e.scalar_tensor_tensor(
            stats[:, :, 9:10].opt(), stats[:, :, 2:3].opt(), -1.0,
            stats[:, :, 6:7].opt(), ALU.mult, ALU.mult), deps=[(sDVE, vv)])
        grp = vv
        for j in range(4):
            b = g * 4 + j
            pg = j % 2
            d_xn = emit("scalar", lambda E, j=j, pg=pg: E.e.activation(
                xn32[:, pg, :], xa[:, j, :], AF.Identity, bias=stats[:, j, 9:10],
                scale=stats[:, j, 6:7]),
                deps=[(sDVE, grp), (sDVE, last_xn[pg])])
            xa_free[j] = d_xn
            d_mx = emit("vector", lambda E, j=j, pg=pg: E.e.tensor_reduce(
                stats[:, j, 0:1], xn32[:, pg, :], AX.X, ALU.max,
                apply_absolute_value=True), deps=[(sACT, d_xn)])
            a_mx = emit("vector", lambda E, j=j: E.e.tensor_tensor(
                rmax[:, 0:1], rmax[:, 0:1], stats[:, j, 0:1], ALU.max),
                deps=[(sDVE, d_mx), (sDVE, a_mx)])
            amx_of[j] = a_mx
            trs = []
            for fc in range(8):
                tr = emit("tensor", lambda E, fc=fc, pg=pg: E.e.transpose(
                    trp[:, fc * 128:(fc + 1) * 128],
                    xn32[:, pg, fc * 128:(fc + 1) * 128], idn32[:]),
                    deps=[(sACT, d_xn), (sDVE, d_idn32), (sDVE, last_tr_copy)])
                trs.append(tr)
            for fc in range(8):
                last_tr_copy = emit("vector",
                    lambda E, fc=fc, b=b: E.e.tensor_copy(
                        xT[:, fc, b * 128:(b + 1) * 128],
                        trp[:, fc * 128:(fc + 1) * 128]),
                    deps=[(sPE, trs[fc])])
            last_xn[pg] = last_tr_copy
    a_xT = last_tr_copy

    dq_state = {"z": 0, "c": 0, "w": 0, "st": [0, 0], "ft": [0, 0], "fs": [0, 0]}

    def dq_chunk(src, width, wdq_row, wfm_tiles, k):
        pg = k % 2
        ng = width // 32
        nt = width // 128
        st_ = dq_state
        dl = dma("gpsimd", "wch", wch[:, pg, 0:width], src,
                 deps=[(sDVE, st_["z"])])
        r = emit("vector", lambda E: E.e.tensor_reduce(
            dqmax[:, pg, 0:ng], wch[:, pg, 0:width].rearrange(
                "p (g e) -> p g e", e=32),
            AX.X, ALU.max, apply_absolute_value=True), deps=[dl])
        r = emit("vector", lambda E: E.e.tensor_scalar(
            dqs[:, pg, 0:ng], dqmax[:, pg, 0:ng], 1.0 / WQMAX, 1e-12,
            ALU.mult, ALU.max), deps=[(sDVE, r)])
        r = emit("vector", lambda E: E.e.reciprocal(
            dqsr[:, pg, 0:ng], dqs[:, pg, 0:ng]), deps=[(sDVE, r)])
        z = emit("vector", lambda E: E.e.tensor_tensor(
            dqz[:, pg, 0:width].rearrange("p (g e) -> p g e", e=32),
            wch[:, pg, 0:width].rearrange("p (g e) -> p g e", e=32),
            dqsr[:, pg, 0:ng].unsqueeze(2).broadcast_to([128, ng, 32]),
            ALU.mult), deps=[(sDVE, r), (sACT, st_["c"])])
        st_["z"] = z
        c = emit("scalar", lambda E: E.e.activation(
            dqi[:, pg, 0:width], dqz[:, pg, 0:width], AF.Copy),
            deps=[(sDVE, z), (sDVE, st_["w"])])
        st_["c"] = c
        w = emit("vector", lambda E: E.e.tensor_tensor(
            dq16[:, pg, 0:width].rearrange("p (g e) -> p g e", e=32),
            dqi[:, pg, 0:width].rearrange("p (g e) -> p g e", e=32),
            dqs[:, pg, 0:ng].unsqueeze(2).broadcast_to([128, ng, 32]),
            ALU.mult), deps=[(sACT, c), st_["st"][pg]])
        st_["w"] = w
        stq = dma("gpsimd", ("wdq", pg), wdq_row, dq16[:, pg, 0:width],
                  deps=[(sDVE, w)])
        st_["st"][pg] = stq
        trs = []
        for t in range(nt):
            trs.append(dmat(("ftA", pg), ftA[:, pg, t, :],
                            wdq_row_tile(wdq_row, t),
                            deps=[stq, st_["fs"][pg]]))
        fst = dma("sync", ("wfm", pg), wfm_tiles,
                  ftA[:, pg, 0:nt, :], deps=[trs[-1]])
        st_["ft"][pg] = trs[-1]
        st_["fs"][pg] = fst
        return fst

    def wdq_row_tile(row_ap, t):
        return row_ap[:, t * 128:(t + 1) * 128]

    wfm_store = {}
    _k = 0
    for ic in range(NIC):
        for wt, wsrc, ddst, fdst in (("u", wu_d, wdq_u, wfm_u),
                                     ("g", wg_d, wdq_g, wfm_g)):
            wfm_store[(wt, ic)] = dq_chunk(
                wsrc[ic * 128:(ic + 1) * 128, :], 1024,
                ddst[ic, :, :],
                fdst[ic].rearrange("p (t e) -> p t e", e=128), _k)
            _k += 1
    wd_fst = {}
    for ec in range(8):
        for hv in range(2):
            wd_fst[(ec, hv)] = dq_chunk(
                wd_d[ec * 128:(ec + 1) * 128, hv * WDH:(hv + 1) * WDH], WDH,
                wdq_dd[ec, :, hv * WDH:(hv + 1) * WDH],
                wfm_d[hv * 12:(hv + 1) * 12, :,
                      ec * 128:(ec + 1) * 128].rearrange("i p e -> p i e"),
                _k)
            _k += 1

    def allreduce(idx, extra_deps):
        tr = emit("tensor", lambda E: E.e.transpose(
            pbk[3][0:8, 0:128], rmax[:, 0:8], idn32[:]),
            deps=list(extra_deps) + [(sDVE, d_idn32)])
        rd = emit("vector", lambda E: E.e.reduce_max(
            red8[:], pbk[3][0:8, 0:128], AX.X), deps=[(sPE, tr)])
        st = dma("gpsimd", "cci", cc_i[idx][0:1, 0:8].rearrange("a b -> b a"),
                 red8[:], deps=[(sDVE, rd)])
        ccv = collective(cc_i[idx], cc_o[idx], deps=[st])
        ld = dma("gpsimd", "ccsb", cc_sb[:], cc_o[idx], deps=[(sCC, ccv)])
        return tr, ld

    def bcast_cols(pairs, deps):
        last = 0
        for k, (src, dst) in enumerate(pairs):
            b1 = emit("tensor", lambda E, src=src: E.e.matmul(
                pbk[3][:, 512 + 16 * 0: 512 + 16 * 0 + 1] if False else
                pbk[3][:, 512:513], onesC[0:1, :], src.opt(),
                start=True, stop=True),
                deps=list(deps) + [(sDVE, d_ones), (sDVE, last)])
            last = emit("vector", lambda E, dst=dst: E.e.tensor_copy(
                dst, pbk[3][:, 512:513]), deps=[(sPE, b1)])
        return last

    ar1_tr, ar1_ld = allreduce(0, [(sDVE, a_mx), (sDVE, last_tr_copy)])
    v = emit("vector", lambda E: E.e.tensor_scalar(
        sc8[0:1, 8:9], cc_sb[0:1, 0:1], 1.0 / QMAX, 1e-12, ALU.mult, ALU.max),
        deps=[ar1_ld])
    v = emit("vector", lambda E: E.e.reciprocal(sc8[0:1, 9:10], sc8[0:1, 8:9]),
             deps=[(sDVE, v)])
    ar1_bc = bcast_cols([(sc8[0:1, 9:10], scb[:, 0:1]),
                         (sc8[0:1, 8:9], scb[:, 1:2])], [(sDVE, v)])

    scopes["A"].close()
    xqT = sb("xqT", [128, 8, TCORE], F16, "B")
    xq_i32 = sb("xq_i32", [128, 2, 1024], I32, "B")
    ftile = sb("ftile", [128, 2, 8, 128], F16, "B")
    stgF = sb("stgF", [128, 2, 1024], F16, "B")


    if LIMIT < 1:
        return _finish(nc, prog, scopes, ctx, dsems, Eng, cms)
    xq_done = {}
    xq_prev = [0, 0]
    _kq = 0
    for q4 in range(4):
        for fc in range(8):
            pgq = _kq % 2
            _kq += 1
            c = emit("scalar", lambda E, fc=fc, q4=q4, pgq=pgq: E.e.activation(
                xq_i32[:, pgq, :], xT[:, fc, q4 * 1024:(q4 + 1) * 1024],
                AF.Identity, scale=scb[:, 0:1]),
                deps=[(sDVE, ar1_bc), (sDVE, a_xT), (sDVE, xq_prev[pgq])])
            xq_prev[pgq] = emit("vector",
                lambda E, fc=fc, q4=q4, pgq=pgq: E.e.tensor_scalar(
                    xqT[:, fc, q4 * 1024:(q4 + 1) * 1024], xq_i32[:, pgq, :],
                    scb[:, 1:2], None, ALU.mult), deps=[(sACT, c)])
            if q4 % 2 == 1:
                xq_done[(fc, q4 // 2)] = xq_prev[pgq]

    f1_seq = []
    for ic in range(NIC):
        f1_seq.append(("u", ic))
        f1_seq.append(("g", ic))

    if LIMIT < 2:
        return _finish(nc, prog, scopes, ctx, dsems, Eng, cms)
    def psF(wt, p):
        base = 0 if wt == "u" else 2
        t = pbk[base + p // 2]
        return t[:, (p % 2) * 512:(p % 2) * 512 + 512]

    f1_store = {}
    f1_evA = {}
    f1_evV = {}
    f1_pool = {}
    ftile_rd = [0, 0]
    stg_free = {}
    first_gate_pb3 = True
    for hh in range(2):
        for k, (wt, ic) in enumerate(f1_seq):
            pg = (hh * len(f1_seq) + k) % 2
            sdram = up_s if wt == "u" else gt_s
            fsrc = wfm_u if wt == "u" else wfm_g
            ftl = dma("sync", ("ftile", pg), ftile[:, pg, :, :],
                      fsrc[ic].rearrange("p (t e) -> p t e", e=128),
                      deps=[wfm_store[(wt, ic)], (sPE, ftile_rd[pg])])
            mm = 0
            for fc in range(8):
                for p in range(4):
                    extra = []
                    if wt == "g" and first_gate_pb3 and p >= 2:
                        extra.append((sDVE, ar1_bc))
                    mm = emit("tensor", lambda E, fc=fc, p=p, pg=pg, wt=wt, hh=hh:
                        E.e.matmul(psF(wt, p), ftile[:, pg, fc, :],
                                   xqT[:, fc, hh * 2048 + p * 512:hh * 2048 + (p + 1) * 512],
                                   start=(fc == 0), stop=(fc == 7)),
                        deps=[ftl, (sDVE, xq_done[(fc, hh)]),
                              (sACT, f1_evA.get((wt, hh, ic - 1),
                                     f1_evA.get((wt, hh - 1, NIC - 1), 0))),
                              (sDVE, f1_evV.get((wt, hh, ic - 1),
                                     f1_evV.get((wt, hh - 1, NIC - 1), 0)))] + extra)
            if wt == "g":
                first_gate_pb3 = False
            ftile_rd[pg] = mm
            ev_a = ev_v = 0
            pr = f1_pool.get(0, 0)
            stv = 0
            col = 1 if wt == "u" else 2
            for half in range(2):
                sdep = [stg_free.get(half, 0),
                        (sDVE, f1_pool.get(half, 0))]
                ev_a = emit("scalar", lambda E, half=half, wt=wt:
                    E.e.activation(stgF[:, half, 0:512], psF(wt, 2 * half),
                                   AF.Copy), deps=[(sPE, mm)] + sdep)
                ev_v = emit("vector", lambda E, half=half, wt=wt:
                    E.e.tensor_copy(stgF[:, half, 512:1024],
                                    psF(wt, 2 * half + 1)),
                    deps=[(sPE, mm)] + sdep)
                pr = emit("vector", lambda E, half=half: E.e.tensor_reduce(
                    tmpB[:, 0:1], stgF[:, half, :], AX.X, ALU.max,
                    apply_absolute_value=True),
                    deps=[(sACT, ev_a), (sDVE, ev_v), (sPE, ar1_tr),
                          (sDVE, pr)])
                pr = emit("vector", lambda E, col=col: E.e.tensor_tensor(
                    rmax[:, col:col + 1], rmax[:, col:col + 1], tmpB[:, 0:1],
                    ALU.max), deps=[(sDVE, pr)])
                if wt == "g":
                    pr = emit("vector", lambda E, half=half: E.e.tensor_reduce(
                        tmpB[:, 1:2], stgF[:, half, :], AX.X, ALU.max),
                        deps=[(sDVE, pr)])
                    pr = emit("vector", lambda E: E.e.tensor_tensor(
                        rmax[:, 3:4], rmax[:, 3:4], tmpB[:, 1:2], ALU.max),
                        deps=[(sDVE, pr)])
                f1_pool[half] = pr
                stv = dma("sync", ("f1st", half),
                          sdram[ic, :, hh * 2048 + half * 1024:
                                hh * 2048 + (half + 1) * 1024],
                          stgF[:, half, :],
                          deps=[(sACT, ev_a), (sDVE, ev_v), (sDVE, pr)])
                stg_free[half] = stv
            f1_evA[(wt, hh, ic)] = ev_a
            f1_evV[(wt, hh, ic)] = ev_v
            f1_store[(wt, hh, ic)] = stv
    b_pe = sPE.n
    b_pool = sPOOL.n
    b_act = sACT.n
    b_dve = sDVE.n

    ar2_tr, ar2_ld = allreduce(1, [(sPOOL, b_pool), (sPE, b_pe),
                                   (sACT, b_act), (sDVE, b_dve)])
    v = emit("vector", lambda E: E.e.tensor_scalar(
        ss[0:1, 0:1], cc_sb[0:1, 1:2], 1.0 / QMAX, 1e-12, ALU.mult, ALU.max),
        deps=[ar2_ld])
    v = emit("vector", lambda E: E.e.tensor_scalar(
        ss[0:1, 1:2], cc_sb[0:1, 2:3], 1.0 / QMAX, 1e-12, ALU.mult, ALU.max),
        deps=[(sDVE, v)])
    v = emit("vector", lambda E: E.e.reciprocal(ss[0:1, 2:3], ss[0:1, 0:1]),
             deps=[(sDVE, v)])
    v = emit("vector", lambda E: E.e.reciprocal(ss[0:1, 3:4], ss[0:1, 1:2]),
             deps=[(sDVE, v)])
    v = emit("vector", lambda E: E.e.tensor_tensor(
        ss[0:1, 4:5], cc_sb[0:1, 3:4], ss[0:1, 3:4], ALU.mult), deps=[(sDVE, v)])
    v = emit("vector", lambda E: E.e.tensor_copy(ssi[0:1, 0:1], ss[0:1, 4:5]),
             deps=[(sDVE, v)])
    v = emit("vector", lambda E: E.e.tensor_tensor(
        ss[0:1, 5:6], ssi[0:1, 0:1], ss[0:1, 1:2], ALU.mult), deps=[(sDVE, v)])
    v = emit("scalar", lambda E: E.e.activation(
        ss[0:1, 6:7], ss[0:1, 5:6], AF.Tanh, scale=0.5), deps=[(sDVE, v)])
    v = emit("vector", lambda E: E.e.tensor_scalar(
        ssi[0:1, 1:2], ss[0:1, 6:7], 32768.0, 32768.0, ALU.mult, ALU.add),
        deps=[(sACT, v)])
    v = emit("vector", lambda E: E.e.tensor_scalar(
        ss[0:1, 7:8], ssi[0:1, 1:2], 1.0 / 65536.0, None, ALU.mult),
        deps=[(sDVE, v)])
    v = emit("vector", lambda E: E.e.tensor_tensor(
        ss[0:1, 8:9], ss[0:1, 5:6], ss[0:1, 7:8], ALU.mult), deps=[(sDVE, v)])
    v = emit("vector", lambda E: E.e.tensor_scalar(
        ss[0:1, 9:10], ss[0:1, 8:9], 1.0 / QMAX, 1e-12, ALU.mult, ALU.max),
        deps=[(sDVE, v)])
    v = emit("vector", lambda E: E.e.tensor_scalar(
        ss[0:1, 10:11], ss[0:1, 9:10], 65536.0, None, ALU.mult), deps=[(sDVE, v)])
    v = emit("vector", lambda E: E.e.reciprocal(ss[0:1, 11:12], ss[0:1, 10:11]),
             deps=[(sDVE, v)])
    v = emit("vector", lambda E: E.e.tensor_tensor(
        ss[0:1, 12:13], ss[0:1, 1:2], ss[0:1, 11:12], ALU.mult), deps=[(sDVE, v)])
    v = emit("vector", lambda E: E.e.tensor_tensor(
        ss[0:1, 13:14], ss[0:1, 9:10], ss[0:1, 0:1], ALU.mult), deps=[(sDVE, v)])
    v = emit("vector", lambda E: E.e.tensor_scalar(
        ss[0:1, 14:15], ss[0:1, 1:2], 0.5, None, ALU.mult), deps=[(sDVE, v)])
    ar2_bc = bcast_cols([(ss[0:1, 2:3], scb[:, 2:3]),
                         (ss[0:1, 3:4], scb[:, 3:4]),
                         (ss[0:1, 14:15], scb[:, 4:5]),
                         (ss[0:1, 12:13], scb[:, 5:6]),
                         (ss[0:1, 13:14], scb[:, 6:7])], [(sDVE, v)])

    scopes["B"].close()
    pbias = sb("pbias", [1, D], F32, "C")
    pwdmp = sb("pwdmp", [128, H, 8, 128], F16, "C")
    f2U = sb("f2U", [128, 2, 1024], F16, "C")
    f2G = sb("f2G", [128, 2, 1024], F16, "C")
    f2ru = sb("f2ru", [128, 1024], I32, "C")
    f2rg = sb("f2rg", [128, 1024], I32, "C")
    f2t1 = sb("f2t1", [128, 1024], F32, "C")
    f2r2 = sb("f2r2", [128, 1024], I32, "C")
    f2o = sb("f2o", [128, 2, 1024], F16, "C")
    stgQ = sb("stgQ", [128, 3072], F16, "C")
    QKT = sb("QKT", [128, 4096], F16, "C")
    Vaug = sb("Vaug", [128, H, 65], F16, "C")
    PT = sb("PT", [128, 2, 512], F16, "C")
    OTn = sb("OTn", [64, 2048], F16, "C")
    scrB = sb("scrB", [128, 2048], F16, "C")
    lcol = sb("lcol", [128, 16], F16, "C")
    rcol = sb("rcol", [128, 16], F32, "C")
    stgAF = sb("stgAF", [128, D], F16, "C")

    d_vone = emit("vector", lambda E: E.e.memset(Vaug[:, :, 64:65], 1.0),
                  deps=[(sPE, b_pe), (sACT, b_act), (sDVE, b_dve)])
    d_pb = dma("gpsimd", "pb", pbias[:], pb_d, deps=[(sDVE, d_vone)])

    if LIMIT < 3:
        return _finish(nc, prog, scopes, ctx, dsems, Eng, cms)
    pw_tr = 0
    for i in range(H):
        for ec in range(8):
            pw_tr = dmat("pwdmp", pwdmp[:, i, ec, :], pw16[ec, :, i, :],
                         deps=[pw_st[ec]])
    pw_tr_all = pw_tr

    f2_list = [(ic, q) for q in range(4) for ic in range(NIC)]
    f2_state = {"k": 0}
    f2p = {}
    f2_store = {}

    def emit_f2_tile():
        k = f2_state["k"]
        if k >= len(f2_list):
            return
        f2_state["k"] += 1
        ic, q = f2_list[k]
        hh = q // 2
        qo = q * 1024
        pg = k % 2
        dl_u = dma("gpsimd", ("f2U", pg), f2U[:, pg, :], up_s[ic, :, qo:qo + 1024],
                   deps=[f1_store[("u", hh, ic)],
                         (sACT, f2p.get(("ru", pg), 0))])
        dl_g = dma("gpsimd", ("f2G", pg), f2G[:, pg, :], gt_s[ic, :, qo:qo + 1024],
                   deps=[f1_store[("g", hh, ic)],
                         (sACT, f2p.get(("rg", pg), 0))])
        ru = emit("scalar", lambda E, pg=pg: E.e.activation(
            f2ru[:], f2U[:, pg, :], AF.Identity, scale=scb[:, 2:3]),
            deps=[dl_u, (sDVE, ar2_bc), (sDVE, f2p.get("t2", 0))])
        rg = emit("scalar", lambda E, pg=pg: E.e.activation(
            f2rg[:], f2G[:, pg, :], AF.Identity, scale=scb[:, 3:4]),
            deps=[dl_g, (sDVE, f2p.get("t1", 0))])
        f2p[("ru", pg)] = ru
        f2p[("rg", pg)] = rg
        tg = emit("scalar", lambda E: E.e.activation(
            f2t1[:], f2rg[:], AF.Tanh, scale=scb[:, 4:5]),
            deps=[(sACT, rg), (sDVE, f2p.get("mx", 0)),
                  (sDVE, f2p.get("t2", 0))])
        rs = emit("vector", lambda E: E.e.tensor_scalar(
            f2r2[:], f2t1[:], 32768.0, 32768.0, ALU.mult, ALU.add),
            deps=[(sACT, tg)])
        f2p["rs"] = rs
        t1 = emit("vector", lambda E: E.e.tensor_tensor(
            f2t1[:], f2rg[:], f2r2[:], ALU.mult), deps=[(sDVE, rs)])
        f2p["t1"] = t1
        r2 = emit("vector", lambda E: E.e.tensor_scalar(
            f2r2[:], f2t1[:], scb[:, 5:6], None, ALU.mult), deps=[(sDVE, t1)])
        t2 = emit("vector", lambda E: E.e.tensor_tensor(
            f2t1[:], f2r2[:], f2ru[:], ALU.mult), deps=[(sDVE, r2)])
        f2p["t2"] = t2
        ost = emit("vector", lambda E, pg=pg: E.e.tensor_scalar(
            f2o[:, pg, :], f2t1[:], scb[:, 6:7], None, ALU.mult),
            deps=[(sDVE, t2), f2p.get(("st_dma", pg), 0)])
        mx = emit("vector", lambda E: E.e.tensor_reduce(
            tmpF[:, 0:1], f2t1[:], AX.X, ALU.max, apply_absolute_value=True),
            deps=[(sDVE, ost), (sPE, ar2_tr)])
        mx = emit("vector", lambda E: E.e.tensor_tensor(
            rmax[:, 4:5], rmax[:, 4:5], tmpF[:, 0:1], ALU.max),
            deps=[(sDVE, mx)])
        f2p["mx"] = mx
        st = dma("gpsimd", ("f2o", pg), o_s[ic, :, qo:qo + 1024], f2o[:, pg, :],
                 deps=[(sDVE, ost), (sDVE, mx)])
        f2p[("st_dma", pg)] = st
        f2_store[(ic, q)] = st

    blk = {}
    qkv_slot = lambda p: pbk[p // 2][:, (p % 2) * 512:(p % 2) * 512 + 512]
    psS_slot = [pbk[3][:, 0:512], pbk[3][:, 512:1024], pbk[0][:, 0:512]]

    def pO_slot(p0, p1):
        if p0 < 1024:
            return pbk[1][0:65, p0:p1]
        return pbk[2][0:65, p0 - 1024:p1 - 1024]

    def pOn_slot(p):
        if p < 2:
            return pbk[1][0:64, p * 512:(p + 1) * 512]
        return pbk[2][0:64, (p - 2) * 512:(p - 1) * 512]

    for b in range(C_BLOCKS):
        pb_ = blk.get(b - 1, {})
        mm = 0
        for fc in range(8):
            for p in range(6):
                deps = [(sDVE, a_xT), wqkv_done,
                        (sACT, pb_.get("qkv_ev", 0)),
                        (sDVE, pb_.get("otn", 0)),
                        (sACT, pb_.get("af_ev", 0))]
                if b == 0:
                    deps += [(sPE, b_pe), (sACT, b_act), (sDVE, ar2_bc)]
                mm = emit("tensor", lambda E, fc=fc, p=p, b=b: E.e.matmul(
                    qkv_slot(p), xT[:, fc, b * 128:(b + 1) * 128],
                    WQKV[:, fc, p * 512:(p + 1) * 512],
                    start=(fc == 0), stop=(fc == 7)), deps=deps)
        qkv_mm = mm
        ev = 0
        for p in range(6):
            ev = emit("scalar", lambda E, p=p: E.e.activation(
                stgQ[:, p * 512:(p + 1) * 512], qkv_slot(p), AF.Copy),
                deps=[(sPE, qkv_mm), pb_.get("st_v", 0)])
        st_q = dma("sync", "qkst", qk_s[b, 0:2048, 0:64].rearrange("(t i) d -> t i d", i=H),
                   stgQ[:, 0:1024].rearrange("p (i d) -> p i d", d=HS),
                   deps=[(sACT, ev)])
        st_k = dma("sync", "qkst", qk_s[b, 2048:4096, 0:64].rearrange("(t i) d -> t i d", i=H),
                   stgQ[:, 1024:2048].rearrange("p (i d) -> p i d", d=HS),
                   deps=[(sACT, ev)])
        st_v = dma("sync", "qkst", v_s[b, :, :].rearrange("(t i) d -> t i d", i=H),
                   stgQ[:, 2048:3072].rearrange("p (i d) -> p i d", d=HS),
                   deps=[(sACT, ev)])
        blk[b] = dict(qkv_ev=ev, st_v=st_v)
        if C_STEP < 2:
            continue
        tr = dmat("QKT", QKT[:], qk_s[b],
                  deps=[st_k, st_q, (sPE, pb_.get("av", 0))])
        vrb = dma("sync", "Vaug", Vaug[:, :, 0:64],
                  v_s[b].rearrange("(j p) d -> p j d", p=128),
                  deps=[st_v, (sPE, pb_.get("av", 0)), (sDVE, d_vone)])
        blk[b]["vrb"] = vrb
        if C_STEP < 3:
            continue
        exp_done = {}
        av_ring = {}
        npc = 0
        av = 0
        for j in range(H):
            lo = j * 128
            bnds = [lo] + list(range(512 * (lo // 512 + 1), 2048, 512)) + [2048]
            bnds = sorted(set(bnds))
            for (p0, p1) in zip(bnds[:-1], bnds[1:]):
                pi = npc % 3
                pt_i = npc % 2
                sdeps = [tr, (sACT, exp_done.get(pi, 0)),
                         (sDVE, pb_.get("evb", 0))]
                if npc == 2:
                    sdeps.append((sACT, ev))
                if b == 0 and npc < 2:
                    sdeps.append((sDVE, ar2_bc))
                    sdeps.append((sDVE, b_dve))
                smm = emit("tensor", lambda E, j=j, p0=p0, p1=p1, pi=pi:
                    E.e.matmul(psS_slot[pi][:, 0:p1 - p0],
                               QKT[0:64, 2048 + j * 128:2048 + (j + 1) * 128],
                               QKT[0:64, p0:p1], start=True, stop=True),
                    deps=sdeps)
                dv = 0
                if p0 == lo:
                    dv = emit("vector", lambda E, pi=pi: E.e.tensor_tensor(
                        psS_slot[pi][:, 0:128], psS_slot[pi][:, 0:128], mtri[:],
                        ALU.add), deps=[(sPE, smm), (sDVE, d_mtri)])
                edeps = [(sPE, smm), (sPE, av_ring.get(pi, 0))]
                if dv:
                    edeps.append((sDVE, dv))
                pt_i = npc % 2
                edeps.append((sPE, av_ring.get(("pt", pt_i), 0)))
                ex = emit("scalar", lambda E, pi=pi, pt_i=pt_i, p0=p0, p1=p1:
                    E.e.activation(PT[:, pt_i, 0:p1 - p0],
                                   psS_slot[pi][:, 0:p1 - p0], AF.Exp,
                                   scale=0.125), deps=edeps)
                exp_done[pi] = ex
                avdeps = [(sACT, ex), vrb]
                if j == 0:
                    avdeps += [(sACT, ev), (sDVE, pb_.get("otn", 0)),
                               (sDVE, pb_.get("lc", 0))]
                av = emit("tensor", lambda E, j=j, p0=p0, p1=p1, pt_i=pt_i:
                    E.e.matmul(pO_slot(p0, p1), Vaug[:, j, :],
                               PT[:, pt_i, 0:p1 - p0],
                               start=(j == 0), stop=(j == H - 1)), deps=avdeps)
                av_ring[pi] = av
                av_ring[("pt", pt_i)] = av
                npc += 1
        av_done = av
        blk[b]["av"] = av_done
        if C_STEP < 4:
            continue
        lc = 0
        for half in range(2):
            lc = emit("vector", lambda E, half=half: E.e.tensor_copy(
                scrB[64:65, half * 1024:(half + 1) * 1024],
                pbk[1 + half][64:65, 0:1024]),
                deps=[(sPE, av_done), pb_.get("l_st", 0),
                      (sPE, pb_.get("bcast", 0))])
        l_st = dma("sync", "l_s", l_s[b, :].unsqueeze(0), scrB[64:65, :],
                   deps=[(sDVE, lc)])
        l_ld = dma("sync", "lcol", lcol[:],
                   l_s[b, :].rearrange("(p k) -> p k", p=128),
                   deps=[l_st, (sDVE, pb_.get("rc", 0))])
        rc = emit("vector", lambda E: E.e.reciprocal(rcol[:], lcol[:]),
                  deps=[l_ld])
        rl_st = dma("gpsimd", "rl_s", rl_s[b, :].rearrange("(p k) -> p k", p=128),
                    rcol[:], deps=[(sDVE, rc)])
        rl_ld = dma("sync", "rlrow", scrB[0:1, :], rl_s[b, :].unsqueeze(0),
                    deps=[rl_st, l_st, (sDVE, pb_.get("otn", 0))])
        blk[b].update(rc=rc, lc=lc, l_st=l_st)
        if C_STEP < 5:
            continue
        otn = 0
        bca = 0
        last_evb = 0
        evb_of = {}
        for p in range(4):
            pi = p % 2
            bb = emit("tensor", lambda E, p=p, pi=pi: E.e.matmul(
                psS_slot[pi][0:64, 0:512], onesC16[0:1, 0:64],
                scrB[0:1, p * 512:(p + 1) * 512], start=True, stop=True),
                deps=[rl_ld, (sACT, exp_done.get(pi, 0)),
                      (sDVE, evb_of.get(p - 2, 0)), (sDVE, d_ones)])
            bca = bb
            evb = emit("vector", lambda E, p=p, pi=pi: E.e.tensor_copy(
                scrB[0:64, p * 512:(p + 1) * 512], psS_slot[pi][0:64, 0:512]),
                deps=[(sPE, bb), (sDVE, pb_.get("otn", 0))])
            evb_of[p] = evb
            last_evb = evb
            otn = emit("vector", lambda E, p=p: E.e.tensor_tensor(
                OTn[:, p * 512:(p + 1) * 512], pOn_slot(p),
                scrB[0:64, p * 512:(p + 1) * 512], ALU.mult),
                deps=[(sDVE, evb), (sPE, av_done), (sPE, pb_.get("proj", 0))])
        blk[b].update(otn=otn, bcast=bca, evb=last_evb)
        if C_STEP < 6:
            continue
        pm = 0
        for p in range(2):
            pm = emit("tensor", lambda E, p=p: E.e.matmul(
                pbk[0][:, p * 512:(p + 1) * 512], onesC[0:1, :],
                pbias[0:1, p * 512:(p + 1) * 512], start=True, stop=False),
                deps=[d_pb, (sDVE, otn), (sDVE, d_ones),
                      (sACT, pb_.get("af_ev", 0)), pw_tr_all,
                      (sACT, exp_done.get(2, 0))])
        for i in range(H):
            for p in range(2):
                pm = emit("tensor", lambda E, i=i, p=p: E.e.matmul(
                    pbk[0][:, p * 512:(p + 1) * 512],
                    OTn[:].rearrange("d (t i) -> d i t", i=H)[:, i, :],
                    pwdmp[0:64, i, :, :].rearrange("d a b -> d (a b)")[:, p * 512:(p + 1) * 512],
                    start=False, stop=(i == H - 1)), deps=[(sDVE, otn)])
        prj = pm
        af_ev = emit("scalar", lambda E: E.e.activation(
            stgAF[:], pbk[0][:, :], AF.Copy),
            deps=[(sPE, prj), pb_.get("af_st", 0)])
        af_st = dma("sync", "stgAF", af_s[b], stgAF[:], deps=[(sACT, af_ev)])

        blk[b].update(proj=prj, af_ev=af_ev, af_st=af_st)

        if b >= 1:
            emit_f2_tile()
            emit_f2_tile()
            emit_f2_tile()
    while f2_state["k"] < len(f2_list):
        emit_f2_tile()
    c_pe = sPE.n
    c_dve = sDVE.n
    c_act = sACT.n
    c_pool = sPOOL.n

    ar3_tr, ar3_ld = allreduce(2, [(sDVE, c_dve), (sPE, c_pe),
                                   (sACT, c_act), (sPOOL, c_pool)])
    v = emit("vector", lambda E: E.e.tensor_tensor(
        ss[0:1, 15:16], cc_sb[0:1, 4:5], ss[0:1, 13:14], ALU.mult),
        deps=[ar3_ld])
    v = emit("vector", lambda E: E.e.tensor_scalar(
        ss[0:1, 15:16], ss[0:1, 15:16], 1.0 / QMAX, 1e-12, ALU.mult, ALU.max),
        deps=[(sDVE, v)])
    v = emit("vector", lambda E: E.e.reciprocal(sc8[0:1, 0:1], ss[0:1, 15:16]),
             deps=[(sDVE, v)])
    ar3_bc = bcast_cols([(sc8[0:1, 0:1], scb2[:, 0:1]),
                         (ss[0:1, 15:16], scb2[:, 1:2])], [(sDVE, v)])

    if LIMIT < 4:
        return _finish(nc, prog, scopes, ctx, dsems, Eng, cms)
    scopes["C"].close()
    cms["xT"].__exit__(None, None, None)
    cms["WQKV"].__exit__(None, None, None)
    WdT = sb("WdT", [128, NIC, 1024], F16, "D")
    oq = sb("oq", [128, 2, NIC, 1024], F16, "D")
    o_ld = sb("o_ld", [128, 2, 1024], F16, "D")
    o_i32 = sb("o_i32", [128, 2, 1024], I32, "D")
    outstg = sb("outstg", [128, 2, D], F32, "D")
    xres = sb("xres", [128, 2, D], F32, "D")
    afres = sb("afres", [128, 2, D], F16, "D")

    wdt_tr = 0
    for ic in range(NIC):
        wdt_tr = dma("sync", "WdT", WdT[:, ic, :], wfm_d[ic],
                     deps=[wd_fst[(7, 0)], wd_fst[(7, 1)],
                           (sPE, c_pe), (sACT, c_act), (sDVE, c_dve),
                           (sPOOL, c_pool)])
    wdt_all = wdt_tr

    oq_done = {}
    dn_done = {}
    res_done = {}
    out_dma = {}
    oq_cast = [0, 0]
    oq_sc = [0, 0]
    for g in range(4):
        gg = g % 2
        oqv = 0
        for ic in range(NIC):
            pg2 = ic % 2
            dl = dma("gpsimd", ("old", pg2), o_ld[:, pg2, :],
                     o_s[ic, :, g * 1024:(g + 1) * 1024],
                     deps=[f2_store[(ic, g)],
                           (sACT, oq_cast[pg2]), (sDVE, ar3_bc)])
            c = emit("scalar", lambda E, pg2=pg2: E.e.activation(
                o_i32[:, pg2, :], o_ld[:, pg2, :], AF.Identity, scale=scb2[:, 0:1]),
                deps=[dl, (sDVE, ar3_bc), (sDVE, oq_sc[pg2])])
            oq_cast[pg2] = c
            svv = emit("vector", lambda E, ic=ic, gg=gg, pg2=pg2: E.e.tensor_scalar(
                oq[:, gg, ic, :], o_i32[:, pg2, :], scb2[:, 1:2], None, ALU.mult),
                deps=[(sACT, c), (sPE, dn_done.get((g - 2, 7), 0))])
            oq_sc[pg2] = svv
            oqv = svv
        oq_done[g] = oqv
        for tt in range(8):
            t_abs = g * 8 + tt
            pgd = tt % 2
            mm = 0
            war = res_done.get((g, tt - 2), 0) if tt >= 2 else \
                res_done.get((g - 1, 6 + tt), 0)
            for ic in range(NIC):
                for p in range(2):
                    deps = [(sDVE, oq_done[g]), wdt_all,
                            (sDVE, war)]
                    if g == 0 and tt < 2:
                        deps += [(sACT, c_act), (sDVE, c_dve), (sPE, c_pe)]
                    mm = emit("tensor", lambda E, ic=ic, p=p, tt=tt, gg=gg, pgd=pgd:
                        E.e.matmul(pbk[pgd][:, p * 512:(p + 1) * 512],
                                   oq[:, gg, ic, tt * 128:(tt + 1) * 128],
                                   WdT[:, ic, p * 512:(p + 1) * 512],
                                   start=(ic == 0), stop=(ic == NIC - 1)),
                        deps=deps)
            dn_done[(g, tt)] = mm
            xl = dma("sync", ("xres", pgd), xres[:, pgd, :],
                     x_d[t_abs * 128:(t_abs + 1) * 128, :],
                     deps=[(sDVE, war), (sDVE, ar3_bc)])
            al = dma("sync", ("afres", pgd), afres[:, pgd, :], af_s[t_abs],
                     deps=[blk[t_abs]["af_st"], (sDVE, war),
                           (sDVE, ar3_bc)])
            r1 = emit("vector", lambda E, pgd=pgd: E.e.tensor_tensor(
                outstg[:, pgd, :], pbk[pgd][:, :], afres[:, pgd, :], ALU.add),
                deps=[(sPE, mm), al, out_dma.get((g, tt - 2),
                      out_dma.get((g - 1, 6 + tt), 0))])
            r2 = emit("vector", lambda E, pgd=pgd: E.e.tensor_tensor(
                outstg[:, pgd, :], outstg[:, pgd, :], xres[:, pgd, :], ALU.add),
                deps=[(sDVE, r1), xl])
            res_done[(g, tt)] = r2
            out_dma[(g, tt)] = dma("sync", ("out", pgd),
                out_d[t_abs * 128:(t_abs + 1) * 128, :],
                outstg[:, pgd, :], deps=[(sDVE, r2)])

    scopes["D"].close()
    return _finish(nc, prog, scopes, ctx, dsems, Eng, cms)


def _finish(nc, prog, scopes, ctx, dsems, Eng, cms):
    for sc in scopes.values():
        try:
            sc.close()
        except Exception:
            pass
    for nm in ("xT", "WQKV"):
        try:
            cms[nm].__exit__(None, None, None)
        except Exception:
            pass
    finals = [(ct, ct.n) for ct in dsems.values() if ct.n > 0]

    with nc.Block() as block:
        @block.sync
        def _(sync):
            E = Eng(sync)
            for fn in prog["sync"]:
                fn(E)
            for ct, v in finals:
                sync.wait_ge(ct.h, v)

        @block.tensor
        def _(tensor):
            E = Eng(tensor)
            for fn in prog["tensor"]:
                fn(E)

        @block.scalar
        def _(scalar):
            E = Eng(scalar)
            for fn in prog["scalar"]:
                fn(E)

        @block.vector
        def _(vector):
            E = Eng(vector)
            for fn in prog["vector"]:
                fn(E)

        @block.gpsimd
        def _(gpsimd):
            E = Eng(gpsimd)
            for fn in prog["gpsimd"]:
                fn(E)

    ctx.close()
    return nc


_CACHE = {}


def kernel(L0_attn_x, L0_attn_Wq, L0_attn_Wk, L0_attn_Wv, L0_attn_proj_w,
           L0_attn_proj_b, L0_ffn_w_gate, L0_ffn_w_up, L0_ffn_w_down):
    if "nc" not in _CACHE:
        _CACHE["nc"] = build_program()
    nc = _CACHE["nc"]
    x = np.ascontiguousarray(np.asarray(L0_attn_x, dtype=np.float32))
    shared = dict(
        wq=np.ascontiguousarray(np.asarray(L0_attn_Wq, dtype=np.float32)),
        wk=np.ascontiguousarray(np.asarray(L0_attn_Wk, dtype=np.float32)),
        wv=np.ascontiguousarray(np.asarray(L0_attn_Wv, dtype=np.float32)),
        pw=np.ascontiguousarray(np.asarray(L0_attn_proj_w, dtype=np.float32)),
        pb=np.ascontiguousarray(np.asarray(L0_attn_proj_b, dtype=np.float32)).reshape(1, D),
        wg=np.ascontiguousarray(np.asarray(L0_ffn_w_gate, dtype=np.float32)),
        wu=np.ascontiguousarray(np.asarray(L0_ffn_w_up, dtype=np.float32)),
        wd=np.ascontiguousarray(np.asarray(L0_ffn_w_down, dtype=np.float32)),
    )
    in_maps = []
    for k in range(NCORES):
        m = dict(shared)
        m["x"] = np.ascontiguousarray(x[2 * k:2 * k + 2].reshape(TCORE, D))
        in_maps.append(m)
    res = run_bass_kernel_spmd(nc, in_maps, core_ids=list(range(NCORES)))
    parts = [np.asarray(r["out"]).reshape(2, 2048, D) for r in res.results]
    return np.concatenate(parts, axis=0)


# version 11
# speedup vs baseline: 1.0185x; 1.0185x over previous
from contextlib import ExitStack

import numpy as np
import concourse.bass as bass
import concourse.mybir as mybir
from concourse.bass_utils import run_bass_kernel_spmd

F32 = mybir.dt.float32
F16 = mybir.dt.float16
I32 = mybir.dt.int32
AF = mybir.ActivationFunctionType
ALU = mybir.AluOpType
AX = mybir.AxisListType

NCORES = 8
LIMIT = 99
C_BLOCKS = 32
C_STEP = 9
NB = 32
TCORE = NB * 128
D = 1024
I3 = 3072
NIC = 24
H, HS = 16, 64
QMAX = 32767.0
WQMAX = 7.0
EPS = 1e-5
NEG = -1.0e30


class Ct:
    def __init__(self, ctx, nc, name):
        self.h = ctx.enter_context(nc.semaphore(name))
        self.n = 0


class Eng:
    def __init__(self, e):
        self.e = e
        self.waited = {}

    def wait(self, deps):
        for ct, val in deps:
            if val is None or val <= 0:
                continue
            if self.waited.get(id(ct), -1) >= val:
                continue
            self.e.wait_ge(ct.h, val)
            self.waited[id(ct)] = val


def build_program():
    nc = bass.Bass("TRN2", target_bir_lowering=False, debug=False)

    x_d = nc.dram_tensor("x", [TCORE, D], F32, kind="ExternalInput").ap()
    wq_d = nc.dram_tensor("wq", [H, D, HS], F32, kind="ExternalInput").ap()
    wk_d = nc.dram_tensor("wk", [H, D, HS], F32, kind="ExternalInput").ap()
    wv_d = nc.dram_tensor("wv", [H, D, HS], F32, kind="ExternalInput").ap()
    pw_d = nc.dram_tensor("pw", [D, D], F32, kind="ExternalInput").ap()
    pb_d = nc.dram_tensor("pb", [1, D], F32, kind="ExternalInput").ap()
    wg_d = nc.dram_tensor("wg", [I3, D], F32, kind="ExternalInput").ap()
    wu_d = nc.dram_tensor("wu", [I3, D], F32, kind="ExternalInput").ap()
    wd_d = nc.dram_tensor("wd", [D, I3], F32, kind="ExternalInput").ap()
    out_d = nc.dram_tensor("out", [TCORE, D], F32, kind="ExternalOutput").ap()

    qk_s = nc.dram_tensor("qk_s", [NB, 4096, 128], F16).ap()
    v_s = nc.dram_tensor("v_s", [NB, 2048, 64], F16).ap()
    af_s = nc.dram_tensor("af_s", [NB, 128, D], F16).ap()
    up_s = nc.dram_tensor("up_s", [NIC, 128, TCORE], F16).ap()
    gt_s = nc.dram_tensor("gt_s", [NIC, 128, TCORE], F16).ap()
    o_s = nc.dram_tensor("o_s", [NIC, 128, TCORE], F16).ap()
    wdq_u = nc.dram_tensor("wdq_u", [NIC, 128, D], F16).ap()
    wdq_g = nc.dram_tensor("wdq_g", [NIC, 128, D], F16).ap()
    wdq_dd = nc.dram_tensor("wdq_dd", [8, 128, I3], F16).ap()
    wfm_u = nc.dram_tensor("wfm_u", [NIC, 128, 1024], F16).ap()
    wfm_g = nc.dram_tensor("wfm_g", [NIC, 128, 1024], F16).ap()
    wfm_d = nc.dram_tensor("wfm_d", [NIC, 128, 1024], F16).ap()
    pw16 = nc.dram_tensor("pw16", [8, 128, H, 128], F16).ap()
    l_s = nc.dram_tensor("l_s", [NB, 2048], F16).ap()
    rl_s = nc.dram_tensor("rl_s", [NB, 2048], F16).ap()
    cc_i = [nc.dram_tensor(f"cc{i}i", [1, 8], F32).ap() for i in range(3)]
    cc_o = [nc.dram_tensor(f"cc{i}o", [1, 8], F32).ap() for i in range(3)]

    ctx = ExitStack()
    scopes = {"A": ExitStack(), "B": ExitStack(), "C": ExitStack(),
              "D": ExitStack()}
    cms = {}

    def sb(name, shape, dtype, scope=None):
        if scope is None:
            cm = nc.sbuf_tensor(name, shape, dtype)
            t = ctx.enter_context(cm)
        elif scope == "manual":
            cm = nc.sbuf_tensor(name, shape, dtype)
            t = cm.__enter__()
        else:
            cm = nc.sbuf_tensor(name, shape, dtype, side="right")
            t = scopes[scope].enter_context(cm)
        cms[name] = cm
        return t

    idn32 = sb("idn32", [128, 128], F32)
    mtri = sb("mtri", [128, 128], F32)
    onesC = sb("onesC", [128, 128], F32)
    onesC16 = sb("onesC16", [128, 128], F16)
    stats = sb("stats", [128, 8, 10], F32)
    rmax = sb("rmax", [128, 8], F32)
    red8 = sb("red8", [8, 1], F32)
    sc8 = sb("sc8", [1, 16], F32)
    ss = sb("ss", [1, 16], F32)
    ssi = sb("ssi", [1, 2], I32)
    scb = sb("scb", [128, 8], F32)
    scb2 = sb("scb2", [128, 2], F32)
    cc_sb = sb("cc_sb", [1, 8], F32)
    tmpB = sb("tmpB", [128, 2], F32)
    tmpF = sb("tmpF", [128, 2], F32)
    WQKV = sb("WQKV", [128, 8, 3072], F16, "manual")
    xT = sb("xT", [128, 8, TCORE], F16, "manual")


    pbk = [ctx.enter_context(nc.psum_tensor(f"pbk{i}", [128, 1024], F32))
           for i in range(4)]

    sPE = Ct(ctx, nc, "sPE")
    sACT = Ct(ctx, nc, "sACT")
    sDVE = Ct(ctx, nc, "sDVE")
    sPOOL = Ct(ctx, nc, "sPOOL")
    sDMA = Ct(ctx, nc, "sDMA")
    sGD = Ct(ctx, nc, "sGD")
    sCC = Ct(ctx, nc, "sCC")

    prog = {k: [] for k in ("sync", "tensor", "scalar", "vector", "gpsimd")}
    ENG = {"tensor": sPE, "scalar": sACT, "vector": sDVE, "gpsimd": sPOOL}

    def emit(engine, builder, deps=()):
        sem = ENG[engine]
        sem.n += 1
        deps = [d for d in deps if d is not None and d != 0]
        def fn(E, builder=builder, deps=deps, sem=sem):
            E.wait(deps)
            ins = builder(E)
            ins.then_inc(sem.h, 1)
        prog[engine].append(fn)
        return sem.n

    dsems = {}

    def dsem(key):
        if key not in dsems:
            dsems[key] = Ct(ctx, nc, "dk_%s" % "_".join(str(x) for x in (
                key if isinstance(key, tuple) else (key,))))
        return dsems[key]

    def dma(engine, key, out, in_, deps=(), **kw):
        sem = dsem(key)
        sem.n += 16
        deps = [d for d in deps if d is not None and d != 0]
        def fn(E, out=out, in_=in_, deps=deps, kw=kw, sem=sem):
            E.wait(deps)
            ins = E.e.dma_start(out=out, in_=in_, **kw)
            ins.then_inc(sem.h, 16)
        prog[engine].append(fn)
        return (sem, sem.n)

    def dmat(key, out, in_, deps=()):
        sem = dsem(key)
        sem.n += 16
        deps = [d for d in deps if d is not None and d != 0]
        def fn(E, out=out, in_=in_, deps=deps, sem=sem):
            E.wait(deps)
            ins = E.e.dma_start_transpose(out, in_)
            ins.then_inc(sem.h, 16)
        prog["sync"].append(fn)
        return (sem, sem.n)

    def collective(cci, cco, deps):
        sCC.n += 1
        deps = [d for d in deps if d is not None]
        def fn(E, deps=deps):
            E.wait(deps)
            ins = E.e.collective_compute(
                "AllReduce", ALU.max, replica_groups=[list(range(NCORES))],
                ins=[cci.opt()], outs=[cco.opt()])
            ins.then_inc(sCC.h, 1)
        prog["gpsimd"].append(fn)
        return sCC.n

    d_iota = emit("gpsimd", lambda E: E.e.iota(
        iotaf[:], pattern=[[1, 128]], base=0, channel_multiplier=-1,
        allow_small_or_imprecise_dtypes=True))
    d_idn32 = emit("vector", lambda E: E.e.tensor_scalar(
        idn32[:], iotaf[:], 0.0, None, ALU.is_equal), deps=[(sPOOL, d_iota)])
    d_m01 = emit("vector", lambda E: E.e.tensor_scalar(
        mtri[:], iotaf[:], 0.0, None, ALU.is_ge), deps=[(sPOOL, d_iota)])
    d_mtri = emit("vector", lambda E: E.e.tensor_scalar(
        mtri[:], mtri[:], 1.0, -NEG, ALU.subtract, ALU.mult),
        deps=[(sDVE, d_m01)])
    d_ones = emit("vector", lambda E: E.e.memset(onesC[:], 1.0))
    d_ones = emit("vector", lambda E: E.e.memset(onesC16[:], 1.0),
                  deps=[(sDVE, d_ones)])
    d_rm0 = emit("vector", lambda E: E.e.memset(rmax[:], 0.0))
    d_st0 = emit("vector", lambda E: E.e.memset(stats[:], 1.0))
    d_rm1 = emit("vector", lambda E: E.e.tensor_scalar(
        rmax[:, 3:4], rmax[:, 3:4], 0.0, NEG, ALU.mult, ALU.add),
        deps=[(sDVE, d_rm0)])

    iotaf = sb("iotaf", [128, 128], F32, "A")
    xa = sb("xa", [128, 4, D], F32, "A")
    xn32 = sb("xn32", [128, 2, D], F32, "A")
    sqdmp = sb("sqdmp", [128, D], F32, "A")
    pw_f32 = sb("pw_f32", [128, 2, D], F32, "A")
    pw_cast = sb("pw_cast", [128, 2, D], F16, "A")
    WDH = 1536
    wch = sb("wch", [128, 2, WDH], F32, "A")
    dqmax = sb("dqmax", [128, 2, 48], F32, "A")
    dqs = sb("dqs", [128, 2, 48], F32, "A")
    dqsr = sb("dqsr", [128, 2, 48], F32, "A")
    dqz = sb("dqz", [128, 2, WDH], F32, "A")
    dqi = sb("dqi", [128, 2, WDH], I32, "A")
    dq16 = sb("dq16", [128, 2, WDH], F16, "A")
    ftA = sb("ftA", [128, 2, 12, 128], F16, "A")



    wqkv_done = 0
    for fc in range(8):
        for wsrc, base in ((wq_d, 0), (wk_d, 1024), (wv_d, 2048)):
            src = wsrc[:, fc * 128:(fc + 1) * 128, :].rearrange("i f d -> f i d")
            dst = WQKV[:, fc, base:base + 1024].rearrange("p (i d) -> p i d", d=HS)
            wqkv_done = dma("gpsimd", "wqkv", dst, src)

    pw_st = [0] * 8
    pw_cv = [0] * 8
    for ec in range(8):
        pg = ec % 2
        dl = dma("sync", ("pwf", pg), pw_f32[:, pg, :], pw_d[ec * 128:(ec + 1) * 128, :],
                 deps=[(sACT, pw_cv[ec - 2]) if ec >= 2 else None])
        cv = emit("scalar", lambda E, pg=pg: E.e.activation(
            pw_cast[:, pg, :], pw_f32[:, pg, :], AF.Copy),
            deps=[dl, pw_st[ec - 2] if ec >= 2 else None])
        pw_cv[ec] = cv
        pw_st[ec] = dma("sync", ("pw16", pg), pw16[ec, :, :, 0:64],
                        pw_cast[:, pg, :].rearrange("p (i d) -> p i d", d=64),
                        deps=[(sACT, cv)])

    trp = pbk[3]
    last_tr_copy = 0
    last_sq = 0
    last_xn = [0, 0]
    xa_free = [0] * 4
    amx_of = [0] * 4
    a_mx = d_rm1
    for g in range(8):
        xa_load = [0] * 4
        for j in range(4):
            b = g * 4 + j
            xa_load[j] = dma("sync", ("xa", j), xa[:, j, :], x_d[b * 128:(b + 1) * 128, :],
                             deps=[(sACT, xa_free[j]) if g > 0 else None])
        for j in range(4):
            last_sq = emit("scalar", lambda E, j=j: E.e.activation(
                sqdmp[:], xa[:, j, :], AF.Square, accum_out=stats[:, j, 0:1]),
                deps=[xa_load[j], (sACT, last_sq), (sDVE, amx_of[j])])
            emit("vector", lambda E, j=j: E.e.reduce_sum(
                stats[:, j, 1:2], xa[:, j, :], AX.X), deps=[xa_load[j]])
        vv = sDVE.n
        vv = emit("vector", lambda E: E.e.tensor_scalar(
            stats[:, :, 2:3].opt(), stats[:, :, 1:2].opt(), 1.0 / D, None,
            ALU.mult), deps=[(sDVE, vv)])
        vv = emit("vector", lambda E: E.e.tensor_scalar(
            stats[:, :, 3:4].opt(), stats[:, :, 0:1].opt(), 1.0 / D, None,
            ALU.mult), deps=[(sACT, last_sq), (sDVE, vv)])
        vv = emit("vector", lambda E: E.e.tensor_tensor(
            stats[:, :, 4:5].opt(), stats[:, :, 2:3].opt(),
            stats[:, :, 2:3].opt(), ALU.mult), deps=[(sDVE, vv)])
        vv = emit("vector", lambda E: E.e.tensor_tensor(
            stats[:, :, 5:6].opt(), stats[:, :, 3:4].opt(),
            stats[:, :, 4:5].opt(), ALU.subtract), deps=[(sDVE, vv)])
        vv = emit("vector", lambda E: E.e.tensor_scalar(
            stats[:, :, 5:6].opt(), stats[:, :, 5:6].opt(), 1.0, EPS, ALU.mult,
            ALU.add), deps=[(sDVE, vv)])
        vv = emit("vector", lambda E: E.e.memset(stats[:, :, 6:7].opt(), 1.0),
                  deps=[(sDVE, vv)])
        for it in range(7):
            vv = emit("vector", lambda E: E.e.tensor_tensor(
                stats[:, :, 7:8].opt(), stats[:, :, 6:7].opt(),
                stats[:, :, 6:7].opt(), ALU.mult), deps=[(sDVE, vv)])
            vv = emit("vector", lambda E: E.e.tensor_tensor(
                stats[:, :, 8:9].opt(), stats[:, :, 7:8].opt(),
                stats[:, :, 5:6].opt(), ALU.mult), deps=[(sDVE, vv)])
            vv = emit("vector", lambda E: E.e.tensor_scalar(
                stats[:, :, 8:9].opt(), stats[:, :, 8:9].opt(), -0.5, 1.5,
                ALU.mult, ALU.add), deps=[(sDVE, vv)])
            vv = emit("vector", lambda E: E.e.tensor_tensor(
                stats[:, :, 6:7].opt(), stats[:, :, 6:7].opt(),
                stats[:, :, 8:9].opt(), ALU.mult), deps=[(sDVE, vv)])
        vv = emit("vector", lambda E: E.e.scalar_tensor_tensor(
            stats[:, :, 9:10].opt(), stats[:, :, 2:3].opt(), -1.0,
            stats[:, :, 6:7].opt(), ALU.mult, ALU.mult), deps=[(sDVE, vv)])
        grp = vv
        for j in range(4):
            b = g * 4 + j
            pg = j % 2
            d_xn = emit("scalar", lambda E, j=j, pg=pg: E.e.activation(
                xn32[:, pg, :], xa[:, j, :], AF.Identity, bias=stats[:, j, 9:10],
                scale=stats[:, j, 6:7]),
                deps=[(sDVE, grp), (sDVE, last_xn[pg])])
            xa_free[j] = d_xn
            d_mx = emit("vector", lambda E, j=j, pg=pg: E.e.tensor_reduce(
                stats[:, j, 0:1], xn32[:, pg, :], AX.X, ALU.max,
                apply_absolute_value=True), deps=[(sACT, d_xn)])
            a_mx = emit("vector", lambda E, j=j: E.e.tensor_tensor(
                rmax[:, 0:1], rmax[:, 0:1], stats[:, j, 0:1], ALU.max),
                deps=[(sDVE, d_mx), (sDVE, a_mx)])
            amx_of[j] = a_mx
            trs = []
            for fc in range(8):
                tr = emit("tensor", lambda E, fc=fc, pg=pg: E.e.transpose(
                    trp[:, fc * 128:(fc + 1) * 128],
                    xn32[:, pg, fc * 128:(fc + 1) * 128], idn32[:]),
                    deps=[(sACT, d_xn), (sDVE, d_idn32), (sDVE, last_tr_copy)])
                trs.append(tr)
            for fc in range(8):
                last_tr_copy = emit("vector",
                    lambda E, fc=fc, b=b: E.e.tensor_copy(
                        xT[:, fc, b * 128:(b + 1) * 128],
                        trp[:, fc * 128:(fc + 1) * 128]),
                    deps=[(sPE, trs[fc])])
            last_xn[pg] = last_tr_copy
    a_xT = last_tr_copy

    dq_state = {"z": 0, "c": 0, "w": 0, "st": [0, 0], "ft": [0, 0], "fs": [0, 0]}

    def dq_chunk(src, width, wdq_row, wfm_tiles, k):
        pg = k % 2
        ng = width // 32
        nt = width // 128
        st_ = dq_state
        dl = dma("gpsimd", "wch", wch[:, pg, 0:width], src,
                 deps=[(sDVE, st_["z"])])
        r = emit("vector", lambda E: E.e.tensor_reduce(
            dqmax[:, pg, 0:ng], wch[:, pg, 0:width].rearrange(
                "p (g e) -> p g e", e=32),
            AX.X, ALU.max, apply_absolute_value=True), deps=[dl])
        r = emit("vector", lambda E: E.e.tensor_scalar(
            dqs[:, pg, 0:ng], dqmax[:, pg, 0:ng], 1.0 / WQMAX, 1e-12,
            ALU.mult, ALU.max), deps=[(sDVE, r)])
        r = emit("vector", lambda E: E.e.reciprocal(
            dqsr[:, pg, 0:ng], dqs[:, pg, 0:ng]), deps=[(sDVE, r)])
        z = emit("vector", lambda E: E.e.tensor_tensor(
            dqz[:, pg, 0:width].rearrange("p (g e) -> p g e", e=32),
            wch[:, pg, 0:width].rearrange("p (g e) -> p g e", e=32),
            dqsr[:, pg, 0:ng].unsqueeze(2).broadcast_to([128, ng, 32]),
            ALU.mult), deps=[(sDVE, r), (sACT, st_["c"])])
        st_["z"] = z
        c = emit("scalar", lambda E: E.e.activation(
            dqi[:, pg, 0:width], dqz[:, pg, 0:width], AF.Copy),
            deps=[(sDVE, z), (sDVE, st_["w"])])
        st_["c"] = c
        w = emit("vector", lambda E: E.e.tensor_tensor(
            dq16[:, pg, 0:width].rearrange("p (g e) -> p g e", e=32),
            dqi[:, pg, 0:width].rearrange("p (g e) -> p g e", e=32),
            dqs[:, pg, 0:ng].unsqueeze(2).broadcast_to([128, ng, 32]),
            ALU.mult), deps=[(sACT, c), st_["st"][pg]])
        st_["w"] = w
        stq = dma("gpsimd", ("wdq", pg), wdq_row, dq16[:, pg, 0:width],
                  deps=[(sDVE, w)])
        st_["st"][pg] = stq
        trs = []
        for t in range(nt):
            trs.append(dmat(("ftA", pg), ftA[:, pg, t, :],
                            wdq_row_tile(wdq_row, t),
                            deps=[stq, st_["fs"][pg]]))
        fst = dma("sync", ("wfm", pg), wfm_tiles,
                  ftA[:, pg, 0:nt, :], deps=[trs[-1]])
        st_["ft"][pg] = trs[-1]
        st_["fs"][pg] = fst
        return fst

    def wdq_row_tile(row_ap, t):
        return row_ap[:, t * 128:(t + 1) * 128]

    wfm_store = {}
    _k = 0
    for ic in range(NIC):
        for wt, wsrc, ddst, fdst in (("u", wu_d, wdq_u, wfm_u),
                                     ("g", wg_d, wdq_g, wfm_g)):
            wfm_store[(wt, ic)] = dq_chunk(
                wsrc[ic * 128:(ic + 1) * 128, :], 1024,
                ddst[ic, :, :],
                fdst[ic].rearrange("p (t e) -> p t e", e=128), _k)
            _k += 1
    wd_fst = {}
    for ec in range(8):
        for hv in range(2):
            wd_fst[(ec, hv)] = dq_chunk(
                wd_d[ec * 128:(ec + 1) * 128, hv * WDH:(hv + 1) * WDH], WDH,
                wdq_dd[ec, :, hv * WDH:(hv + 1) * WDH],
                wfm_d[hv * 12:(hv + 1) * 12, :,
                      ec * 128:(ec + 1) * 128].rearrange("i p e -> p i e"),
                _k)
            _k += 1

    def allreduce(idx, extra_deps):
        tr = emit("tensor", lambda E: E.e.transpose(
            pbk[3][0:8, 0:128], rmax[:, 0:8], idn32[:]),
            deps=list(extra_deps) + [(sDVE, d_idn32)])
        rd = emit("vector", lambda E: E.e.reduce_max(
            red8[:], pbk[3][0:8, 0:128], AX.X), deps=[(sPE, tr)])
        st = dma("gpsimd", "cci", cc_i[idx][0:1, 0:8].rearrange("a b -> b a"),
                 red8[:], deps=[(sDVE, rd)])
        ccv = collective(cc_i[idx], cc_o[idx], deps=[st])
        ld = dma("gpsimd", "ccsb", cc_sb[:], cc_o[idx], deps=[(sCC, ccv)])
        return tr, ld

    def bcast_cols(pairs, deps):
        last = 0
        for k, (src, dst) in enumerate(pairs):
            b1 = emit("tensor", lambda E, src=src: E.e.matmul(
                pbk[3][:, 512 + 16 * 0: 512 + 16 * 0 + 1] if False else
                pbk[3][:, 512:513], onesC[0:1, :], src.opt(),
                start=True, stop=True),
                deps=list(deps) + [(sDVE, d_ones), (sDVE, last)])
            last = emit("vector", lambda E, dst=dst: E.e.tensor_copy(
                dst, pbk[3][:, 512:513]), deps=[(sPE, b1)])
        return last

    ar1_tr, ar1_ld = allreduce(0, [(sDVE, a_mx), (sDVE, last_tr_copy)])
    v = emit("vector", lambda E: E.e.tensor_scalar(
        sc8[0:1, 8:9], cc_sb[0:1, 0:1], 1.0 / QMAX, 1e-12, ALU.mult, ALU.max),
        deps=[ar1_ld])
    v = emit("vector", lambda E: E.e.reciprocal(sc8[0:1, 9:10], sc8[0:1, 8:9]),
             deps=[(sDVE, v)])
    ar1_bc = bcast_cols([(sc8[0:1, 9:10], scb[:, 0:1]),
                         (sc8[0:1, 8:9], scb[:, 1:2])], [(sDVE, v)])

    scopes["A"].close()
    xqT = sb("xqT", [128, 8, TCORE], F16, "B")
    xq_i32 = sb("xq_i32", [128, 2, 1024], I32, "B")
    ftile = sb("ftile", [128, 2, 8, 128], F16, "B")
    stgF = sb("stgF", [128, 2, 1024], F16, "B")


    if LIMIT < 1:
        return _finish(nc, prog, scopes, ctx, dsems, Eng, cms)
    xq_done = {}
    xq_prev = [0, 0]
    _kq = 0
    for q4 in range(4):
        for fc in range(8):
            pgq = _kq % 2
            _kq += 1
            c = emit("scalar", lambda E, fc=fc, q4=q4, pgq=pgq: E.e.activation(
                xq_i32[:, pgq, :], xT[:, fc, q4 * 1024:(q4 + 1) * 1024],
                AF.Identity, scale=scb[:, 0:1]),
                deps=[(sDVE, ar1_bc), (sDVE, a_xT), (sDVE, xq_prev[pgq])])
            xq_prev[pgq] = emit("vector",
                lambda E, fc=fc, q4=q4, pgq=pgq: E.e.tensor_scalar(
                    xqT[:, fc, q4 * 1024:(q4 + 1) * 1024], xq_i32[:, pgq, :],
                    scb[:, 1:2], None, ALU.mult), deps=[(sACT, c)])
            if q4 % 2 == 1:
                xq_done[(fc, q4 // 2)] = xq_prev[pgq]

    f1_seq = []
    for ic in range(NIC):
        f1_seq.append(("u", ic))
        f1_seq.append(("g", ic))

    if LIMIT < 2:
        return _finish(nc, prog, scopes, ctx, dsems, Eng, cms)
    def psF(wt, p):
        base = 0 if wt == "u" else 2
        t = pbk[base + p // 2]
        return t[:, (p % 2) * 512:(p % 2) * 512 + 512]

    f1_store = {}
    f1_evA = {}
    f1_evV = {}
    f1_pool = {}
    ftile_rd = [0, 0]
    stg_free = {}
    first_gate_pb3 = True
    for hh in range(2):
        for k, (wt, ic) in enumerate(f1_seq):
            pg = (hh * len(f1_seq) + k) % 2
            sdram = up_s if wt == "u" else gt_s
            fsrc = wfm_u if wt == "u" else wfm_g
            ftl = dma("sync", ("ftile", pg), ftile[:, pg, :, :],
                      fsrc[ic].rearrange("p (t e) -> p t e", e=128),
                      deps=[wfm_store[(wt, ic)], (sPE, ftile_rd[pg])])
            mm = 0
            for fc in range(8):
                for p in range(4):
                    extra = []
                    if wt == "g" and first_gate_pb3 and p >= 2:
                        extra.append((sDVE, ar1_bc))
                    mm = emit("tensor", lambda E, fc=fc, p=p, pg=pg, wt=wt, hh=hh:
                        E.e.matmul(psF(wt, p), ftile[:, pg, fc, :],
                                   xqT[:, fc, hh * 2048 + p * 512:hh * 2048 + (p + 1) * 512],
                                   start=(fc == 0), stop=(fc == 7)),
                        deps=[ftl, (sDVE, xq_done[(fc, hh)]),
                              (sACT, f1_evA.get((wt, hh, ic - 1),
                                     f1_evA.get((wt, hh - 1, NIC - 1), 0))),
                              (sDVE, f1_evV.get((wt, hh, ic - 1),
                                     f1_evV.get((wt, hh - 1, NIC - 1), 0)))] + extra)
            if wt == "g":
                first_gate_pb3 = False
            ftile_rd[pg] = mm
            ev_a = ev_v = 0
            pr = f1_pool.get(0, 0)
            stv = 0
            col = 1 if wt == "u" else 2
            for half in range(2):
                sdep = [stg_free.get(half, 0),
                        (sDVE, f1_pool.get(half, 0))]
                ev_a = emit("scalar", lambda E, half=half, wt=wt:
                    E.e.activation(stgF[:, half, 0:512], psF(wt, 2 * half),
                                   AF.Copy), deps=[(sPE, mm)] + sdep)
                ev_v = emit("vector", lambda E, half=half, wt=wt:
                    E.e.tensor_copy(stgF[:, half, 512:1024],
                                    psF(wt, 2 * half + 1)),
                    deps=[(sPE, mm)] + sdep)
                pr = emit("vector", lambda E, half=half: E.e.tensor_reduce(
                    tmpB[:, 0:1], stgF[:, half, :], AX.X, ALU.max,
                    apply_absolute_value=True),
                    deps=[(sACT, ev_a), (sDVE, ev_v), (sPE, ar1_tr),
                          (sDVE, pr)])
                pr = emit("vector", lambda E, col=col: E.e.tensor_tensor(
                    rmax[:, col:col + 1], rmax[:, col:col + 1], tmpB[:, 0:1],
                    ALU.max), deps=[(sDVE, pr)])
                if wt == "g":
                    pr = emit("vector", lambda E, half=half: E.e.tensor_reduce(
                        tmpB[:, 1:2], stgF[:, half, :], AX.X, ALU.max),
                        deps=[(sDVE, pr)])
                    pr = emit("vector", lambda E: E.e.tensor_tensor(
                        rmax[:, 3:4], rmax[:, 3:4], tmpB[:, 1:2], ALU.max),
                        deps=[(sDVE, pr)])
                f1_pool[half] = pr
                stv = dma("sync", ("f1st", half),
                          sdram[ic, :, hh * 2048 + half * 1024:
                                hh * 2048 + (half + 1) * 1024],
                          stgF[:, half, :],
                          deps=[(sACT, ev_a), (sDVE, ev_v)])
                stg_free[half] = stv
            f1_evA[(wt, hh, ic)] = ev_a
            f1_evV[(wt, hh, ic)] = ev_v
            f1_store[(wt, hh, ic)] = stv
    b_pe = sPE.n
    b_pool = sPOOL.n
    b_act = sACT.n
    b_dve = sDVE.n

    ar2_tr, ar2_ld = allreduce(1, [(sPOOL, b_pool), (sPE, b_pe),
                                   (sACT, b_act), (sDVE, b_dve)])
    v = emit("vector", lambda E: E.e.tensor_scalar(
        ss[0:1, 0:1], cc_sb[0:1, 1:2], 1.0 / QMAX, 1e-12, ALU.mult, ALU.max),
        deps=[ar2_ld])
    v = emit("vector", lambda E: E.e.tensor_scalar(
        ss[0:1, 1:2], cc_sb[0:1, 2:3], 1.0 / QMAX, 1e-12, ALU.mult, ALU.max),
        deps=[(sDVE, v)])
    v = emit("vector", lambda E: E.e.reciprocal(ss[0:1, 2:3], ss[0:1, 0:1]),
             deps=[(sDVE, v)])
    v = emit("vector", lambda E: E.e.reciprocal(ss[0:1, 3:4], ss[0:1, 1:2]),
             deps=[(sDVE, v)])
    v = emit("vector", lambda E: E.e.tensor_tensor(
        ss[0:1, 4:5], cc_sb[0:1, 3:4], ss[0:1, 3:4], ALU.mult), deps=[(sDVE, v)])
    v = emit("vector", lambda E: E.e.tensor_copy(ssi[0:1, 0:1], ss[0:1, 4:5]),
             deps=[(sDVE, v)])
    v = emit("vector", lambda E: E.e.tensor_tensor(
        ss[0:1, 5:6], ssi[0:1, 0:1], ss[0:1, 1:2], ALU.mult), deps=[(sDVE, v)])
    v = emit("scalar", lambda E: E.e.activation(
        ss[0:1, 6:7], ss[0:1, 5:6], AF.Tanh, scale=0.5), deps=[(sDVE, v)])
    v = emit("vector", lambda E: E.e.tensor_scalar(
        ssi[0:1, 1:2], ss[0:1, 6:7], 32768.0, 32768.0, ALU.mult, ALU.add),
        deps=[(sACT, v)])
    v = emit("vector", lambda E: E.e.tensor_scalar(
        ss[0:1, 7:8], ssi[0:1, 1:2], 1.0 / 65536.0, None, ALU.mult),
        deps=[(sDVE, v)])
    v = emit("vector", lambda E: E.e.tensor_tensor(
        ss[0:1, 8:9], ss[0:1, 5:6], ss[0:1, 7:8], ALU.mult), deps=[(sDVE, v)])
    v = emit("vector", lambda E: E.e.tensor_scalar(
        ss[0:1, 9:10], ss[0:1, 8:9], 1.0 / QMAX, 1e-12, ALU.mult, ALU.max),
        deps=[(sDVE, v)])
    v = emit("vector", lambda E: E.e.tensor_scalar(
        ss[0:1, 10:11], ss[0:1, 9:10], 65536.0, None, ALU.mult), deps=[(sDVE, v)])
    v = emit("vector", lambda E: E.e.reciprocal(ss[0:1, 11:12], ss[0:1, 10:11]),
             deps=[(sDVE, v)])
    v = emit("vector", lambda E: E.e.tensor_tensor(
        ss[0:1, 12:13], ss[0:1, 1:2], ss[0:1, 11:12], ALU.mult), deps=[(sDVE, v)])
    v = emit("vector", lambda E: E.e.tensor_tensor(
        ss[0:1, 13:14], ss[0:1, 9:10], ss[0:1, 0:1], ALU.mult), deps=[(sDVE, v)])
    v = emit("vector", lambda E: E.e.tensor_scalar(
        ss[0:1, 14:15], ss[0:1, 1:2], 0.5, None, ALU.mult), deps=[(sDVE, v)])
    ar2_bc = bcast_cols([(ss[0:1, 2:3], scb[:, 2:3]),
                         (ss[0:1, 3:4], scb[:, 3:4]),
                         (ss[0:1, 14:15], scb[:, 4:5]),
                         (ss[0:1, 12:13], scb[:, 5:6]),
                         (ss[0:1, 13:14], scb[:, 6:7])], [(sDVE, v)])

    scopes["B"].close()
    pbias = sb("pbias", [1, D], F32, "C")
    pwdmp = sb("pwdmp", [128, H, 8, 128], F16, "C")
    f2U = sb("f2U", [128, 2, 1024], F16, "C")
    f2G = sb("f2G", [128, 2, 1024], F16, "C")
    f2ru = sb("f2ru", [128, 1024], I32, "C")
    f2rg = sb("f2rg", [128, 1024], I32, "C")
    f2t1 = sb("f2t1", [128, 1024], F32, "C")
    f2r2 = sb("f2r2", [128, 1024], I32, "C")
    f2o = sb("f2o", [128, 2, 1024], F16, "C")
    stgQ = sb("stgQ", [128, 3072], F16, "C")
    QKT = sb("QKT", [128, 4096], F16, "C")
    Vaug = sb("Vaug", [128, H, 65], F16, "C")
    PT = sb("PT", [128, 2, 512], F16, "C")
    OTn = sb("OTn", [64, 2048], F16, "C")
    scrB = sb("scrB", [128, 2048], F16, "C")
    lcol = sb("lcol", [128, 16], F16, "C")
    rcol = sb("rcol", [128, 16], F32, "C")
    stgAF = sb("stgAF", [128, D], F16, "C")

    d_vone = emit("vector", lambda E: E.e.memset(Vaug[:, :, 64:65], 1.0),
                  deps=[(sPE, b_pe), (sACT, b_act), (sDVE, b_dve)])
    d_pb = dma("gpsimd", "pb", pbias[:], pb_d, deps=[(sDVE, d_vone)])

    if LIMIT < 3:
        return _finish(nc, prog, scopes, ctx, dsems, Eng, cms)
    pw_tr = 0
    for i in range(H):
        for ec in range(8):
            pw_tr = dmat("pwdmp", pwdmp[:, i, ec, :], pw16[ec, :, i, :],
                         deps=[pw_st[ec]])
    pw_tr_all = pw_tr

    f2_list = [(ic, q) for q in range(4) for ic in range(NIC)]
    f2_state = {"k": 0}
    f2p = {}
    f2_store = {}

    def emit_f2_tile():
        k = f2_state["k"]
        if k >= len(f2_list):
            return
        f2_state["k"] += 1
        ic, q = f2_list[k]
        hh = q // 2
        qo = q * 1024
        pg = k % 2
        dl_u = dma("gpsimd", ("f2U", pg), f2U[:, pg, :], up_s[ic, :, qo:qo + 1024],
                   deps=[f1_store[("u", hh, ic)],
                         (sACT, f2p.get(("ru", pg), 0))])
        dl_g = dma("gpsimd", ("f2G", pg), f2G[:, pg, :], gt_s[ic, :, qo:qo + 1024],
                   deps=[f1_store[("g", hh, ic)],
                         (sACT, f2p.get(("rg", pg), 0))])
        ru = emit("scalar", lambda E, pg=pg: E.e.activation(
            f2ru[:], f2U[:, pg, :], AF.Identity, scale=scb[:, 2:3]),
            deps=[dl_u, (sDVE, ar2_bc), (sDVE, f2p.get("t2", 0))])
        rg = emit("scalar", lambda E, pg=pg: E.e.activation(
            f2rg[:], f2G[:, pg, :], AF.Identity, scale=scb[:, 3:4]),
            deps=[dl_g, (sDVE, f2p.get("t1", 0))])
        f2p[("ru", pg)] = ru
        f2p[("rg", pg)] = rg
        tg = emit("scalar", lambda E: E.e.activation(
            f2t1[:], f2rg[:], AF.Tanh, scale=scb[:, 4:5]),
            deps=[(sACT, rg), (sDVE, f2p.get("mx", 0)),
                  (sDVE, f2p.get("t2", 0))])
        rs = emit("vector", lambda E: E.e.tensor_scalar(
            f2r2[:], f2t1[:], 32768.0, 32768.0, ALU.mult, ALU.add),
            deps=[(sACT, tg)])
        f2p["rs"] = rs
        t1 = emit("vector", lambda E: E.e.tensor_tensor(
            f2t1[:], f2rg[:], f2r2[:], ALU.mult), deps=[(sDVE, rs)])
        f2p["t1"] = t1
        r2 = emit("vector", lambda E: E.e.tensor_scalar(
            f2r2[:], f2t1[:], scb[:, 5:6], None, ALU.mult), deps=[(sDVE, t1)])
        t2 = emit("vector", lambda E: E.e.tensor_tensor(
            f2t1[:], f2r2[:], f2ru[:], ALU.mult), deps=[(sDVE, r2)])
        f2p["t2"] = t2
        ost = emit("vector", lambda E, pg=pg: E.e.tensor_scalar(
            f2o[:, pg, :], f2t1[:], scb[:, 6:7], None, ALU.mult),
            deps=[(sDVE, t2), f2p.get(("st_dma", pg), 0)])
        mx = emit("vector", lambda E: E.e.tensor_reduce(
            tmpF[:, 0:1], f2t1[:], AX.X, ALU.max, apply_absolute_value=True),
            deps=[(sDVE, ost), (sPE, ar2_tr)])
        mx = emit("vector", lambda E: E.e.tensor_tensor(
            rmax[:, 4:5], rmax[:, 4:5], tmpF[:, 0:1], ALU.max),
            deps=[(sDVE, mx)])
        f2p["mx"] = mx
        st = dma("gpsimd", ("f2o", pg), o_s[ic, :, qo:qo + 1024], f2o[:, pg, :],
                 deps=[(sDVE, ost), (sDVE, mx)])
        f2p[("st_dma", pg)] = st
        f2_store[(ic, q)] = st

    blk = {}
    qkv_slot = lambda p: pbk[p // 2][:, (p % 2) * 512:(p % 2) * 512 + 512]
    psS_slot = [pbk[3][:, 0:512], pbk[3][:, 512:1024], pbk[0][:, 0:512]]

    def pO_slot(p0, p1):
        if p0 < 1024:
            return pbk[1][0:65, p0:p1]
        return pbk[2][0:65, p0 - 1024:p1 - 1024]

    def pOn_slot(p):
        if p < 2:
            return pbk[1][0:64, p * 512:(p + 1) * 512]
        return pbk[2][0:64, (p - 2) * 512:(p - 1) * 512]

    for b in range(C_BLOCKS):
        pb_ = blk.get(b - 1, {})
        mm = 0
        for fc in range(8):
            for p in range(6):
                deps = [(sDVE, a_xT), wqkv_done,
                        (sACT, pb_.get("qkv_ev", 0)),
                        (sDVE, pb_.get("otn", 0)),
                        (sACT, pb_.get("af_ev", 0))]
                if b == 0:
                    deps += [(sPE, b_pe), (sACT, b_act), (sDVE, ar2_bc)]
                mm = emit("tensor", lambda E, fc=fc, p=p, b=b: E.e.matmul(
                    qkv_slot(p), xT[:, fc, b * 128:(b + 1) * 128],
                    WQKV[:, fc, p * 512:(p + 1) * 512],
                    start=(fc == 0), stop=(fc == 7)), deps=deps)
        qkv_mm = mm
        ev = 0
        for p in range(6):
            ev = emit("scalar", lambda E, p=p: E.e.activation(
                stgQ[:, p * 512:(p + 1) * 512], qkv_slot(p), AF.Copy),
                deps=[(sPE, qkv_mm), pb_.get("st_v", 0)])
        st_q = dma("sync", "qkst", qk_s[b, 0:2048, 0:64].rearrange("(t i) d -> t i d", i=H),
                   stgQ[:, 0:1024].rearrange("p (i d) -> p i d", d=HS),
                   deps=[(sACT, ev)])
        st_k = dma("sync", "qkst", qk_s[b, 2048:4096, 0:64].rearrange("(t i) d -> t i d", i=H),
                   stgQ[:, 1024:2048].rearrange("p (i d) -> p i d", d=HS),
                   deps=[(sACT, ev)])
        st_v = dma("sync", "qkst", v_s[b, :, :].rearrange("(t i) d -> t i d", i=H),
                   stgQ[:, 2048:3072].rearrange("p (i d) -> p i d", d=HS),
                   deps=[(sACT, ev)])
        blk[b] = dict(qkv_ev=ev, st_v=st_v)
        if C_STEP < 2:
            continue
        tr = dmat("QKT", QKT[:], qk_s[b],
                  deps=[st_k, st_q, (sPE, pb_.get("av", 0))])
        vrb = dma("sync", "Vaug", Vaug[:, :, 0:64],
                  v_s[b].rearrange("(j p) d -> p j d", p=128),
                  deps=[st_v, (sPE, pb_.get("av", 0)), (sDVE, d_vone)])
        blk[b]["vrb"] = vrb
        if C_STEP < 3:
            continue
        exp_done = {}
        av_ring = {}
        npc = 0
        av = 0
        for j in range(H):
            lo = j * 128
            bnds = [lo] + list(range(512 * (lo // 512 + 1), 2048, 512)) + [2048]
            bnds = sorted(set(bnds))
            for (p0, p1) in zip(bnds[:-1], bnds[1:]):
                pi = npc % 3
                pt_i = npc % 2
                sdeps = [tr, (sACT, exp_done.get(pi, 0)),
                         (sDVE, pb_.get("evb", 0))]
                if npc == 2:
                    sdeps.append((sACT, ev))
                if b == 0 and npc < 2:
                    sdeps.append((sDVE, ar2_bc))
                    sdeps.append((sDVE, b_dve))
                smm = emit("tensor", lambda E, j=j, p0=p0, p1=p1, pi=pi:
                    E.e.matmul(psS_slot[pi][:, 0:p1 - p0],
                               QKT[0:64, 2048 + j * 128:2048 + (j + 1) * 128],
                               QKT[0:64, p0:p1], start=True, stop=True),
                    deps=sdeps)
                dv = 0
                if p0 == lo:
                    dv = emit("vector", lambda E, pi=pi: E.e.tensor_tensor(
                        psS_slot[pi][:, 0:128], psS_slot[pi][:, 0:128], mtri[:],
                        ALU.add), deps=[(sPE, smm), (sDVE, d_mtri)])
                edeps = [(sPE, smm), (sPE, av_ring.get(pi, 0))]
                if dv:
                    edeps.append((sDVE, dv))
                pt_i = npc % 2
                edeps.append((sPE, av_ring.get(("pt", pt_i), 0)))
                ex = emit("scalar", lambda E, pi=pi, pt_i=pt_i, p0=p0, p1=p1:
                    E.e.activation(PT[:, pt_i, 0:p1 - p0],
                                   psS_slot[pi][:, 0:p1 - p0], AF.Exp,
                                   scale=0.125), deps=edeps)
                exp_done[pi] = ex
                avdeps = [(sACT, ex), vrb]
                if j == 0:
                    avdeps += [(sACT, ev), (sDVE, pb_.get("otn", 0)),
                               (sDVE, pb_.get("lc", 0))]
                av = emit("tensor", lambda E, j=j, p0=p0, p1=p1, pt_i=pt_i:
                    E.e.matmul(pO_slot(p0, p1), Vaug[:, j, :],
                               PT[:, pt_i, 0:p1 - p0],
                               start=(j == 0), stop=(j == H - 1)), deps=avdeps)
                av_ring[pi] = av
                av_ring[("pt", pt_i)] = av
                npc += 1
        av_done = av
        blk[b]["av"] = av_done
        if C_STEP < 4:
            continue
        lc = 0
        for half in range(2):
            lc = emit("vector", lambda E, half=half: E.e.tensor_copy(
                scrB[64:65, half * 1024:(half + 1) * 1024],
                pbk[1 + half][64:65, 0:1024]),
                deps=[(sPE, av_done), pb_.get("l_st", 0),
                      (sPE, pb_.get("bcast", 0))])
        l_st = dma("sync", "l_s", l_s[b, :].unsqueeze(0), scrB[64:65, :],
                   deps=[(sDVE, lc)])
        l_ld = dma("sync", "lcol", lcol[:],
                   l_s[b, :].rearrange("(p k) -> p k", p=128),
                   deps=[l_st, (sDVE, pb_.get("rc", 0))])
        rc = emit("vector", lambda E: E.e.reciprocal(rcol[:], lcol[:]),
                  deps=[l_ld])
        rl_st = dma("gpsimd", "rl_s", rl_s[b, :].rearrange("(p k) -> p k", p=128),
                    rcol[:], deps=[(sDVE, rc)])
        rl_ld = dma("sync", "rlrow", scrB[0:1, :], rl_s[b, :].unsqueeze(0),
                    deps=[rl_st, l_st, (sDVE, pb_.get("otn", 0))])
        blk[b].update(rc=rc, lc=lc, l_st=l_st)
        if C_STEP < 5:
            continue
        otn = 0
        bca = 0
        last_evb = 0
        evb_of = {}
        for p in range(4):
            pi = p % 2
            bb = emit("tensor", lambda E, p=p, pi=pi: E.e.matmul(
                psS_slot[pi][0:64, 0:512], onesC16[0:1, 0:64],
                scrB[0:1, p * 512:(p + 1) * 512], start=True, stop=True),
                deps=[rl_ld, (sACT, exp_done.get(pi, 0)),
                      (sDVE, evb_of.get(p - 2, 0)), (sDVE, d_ones)])
            bca = bb
            evb = emit("vector", lambda E, p=p, pi=pi: E.e.tensor_copy(
                scrB[0:64, p * 512:(p + 1) * 512], psS_slot[pi][0:64, 0:512]),
                deps=[(sPE, bb), (sDVE, pb_.get("otn", 0))])
            evb_of[p] = evb
            last_evb = evb
            otn = emit("vector", lambda E, p=p: E.e.tensor_tensor(
                OTn[:, p * 512:(p + 1) * 512], pOn_slot(p),
                scrB[0:64, p * 512:(p + 1) * 512], ALU.mult),
                deps=[(sDVE, evb), (sPE, av_done), (sPE, pb_.get("proj", 0))])
        blk[b].update(otn=otn, bcast=bca, evb=last_evb)
        if C_STEP < 6:
            continue
        pm = 0
        for p in range(2):
            pm = emit("tensor", lambda E, p=p: E.e.matmul(
                pbk[0][:, p * 512:(p + 1) * 512], onesC[0:1, :],
                pbias[0:1, p * 512:(p + 1) * 512], start=True, stop=False),
                deps=[d_pb, (sDVE, otn), (sDVE, d_ones),
                      (sACT, pb_.get("af_ev", 0)), pw_tr_all,
                      (sACT, exp_done.get(2, 0))])
        for i in range(H):
            for p in range(2):
                pm = emit("tensor", lambda E, i=i, p=p: E.e.matmul(
                    pbk[0][:, p * 512:(p + 1) * 512],
                    OTn[:].rearrange("d (t i) -> d i t", i=H)[:, i, :],
                    pwdmp[0:64, i, :, :].rearrange("d a b -> d (a b)")[:, p * 512:(p + 1) * 512],
                    start=False, stop=(i == H - 1)), deps=[(sDVE, otn)])
        prj = pm
        af_ev = emit("scalar", lambda E: E.e.activation(
            stgAF[:], pbk[0][:, :], AF.Copy),
            deps=[(sPE, prj), pb_.get("af_st", 0)])
        af_st = dma("sync", "stgAF", af_s[b], stgAF[:], deps=[(sACT, af_ev)])

        blk[b].update(proj=prj, af_ev=af_ev, af_st=af_st)

        if b >= 1:
            emit_f2_tile()
            emit_f2_tile()
            emit_f2_tile()
    while f2_state["k"] < len(f2_list):
        emit_f2_tile()
    c_pe = sPE.n
    c_dve = sDVE.n
    c_act = sACT.n
    c_pool = sPOOL.n

    ar3_tr, ar3_ld = allreduce(2, [(sDVE, c_dve), (sPE, c_pe),
                                   (sACT, c_act), (sPOOL, c_pool)])
    v = emit("vector", lambda E: E.e.tensor_tensor(
        ss[0:1, 15:16], cc_sb[0:1, 4:5], ss[0:1, 13:14], ALU.mult),
        deps=[ar3_ld])
    v = emit("vector", lambda E: E.e.tensor_scalar(
        ss[0:1, 15:16], ss[0:1, 15:16], 1.0 / QMAX, 1e-12, ALU.mult, ALU.max),
        deps=[(sDVE, v)])
    v = emit("vector", lambda E: E.e.reciprocal(sc8[0:1, 0:1], ss[0:1, 15:16]),
             deps=[(sDVE, v)])
    ar3_bc = bcast_cols([(sc8[0:1, 0:1], scb2[:, 0:1]),
                         (ss[0:1, 15:16], scb2[:, 1:2])], [(sDVE, v)])

    if LIMIT < 4:
        return _finish(nc, prog, scopes, ctx, dsems, Eng, cms)
    scopes["C"].close()
    cms["xT"].__exit__(None, None, None)
    cms["WQKV"].__exit__(None, None, None)
    WdT = sb("WdT", [128, NIC, 1024], F16, "D")
    oq = sb("oq", [128, 2, NIC, 1024], F16, "D")
    o_ld = sb("o_ld", [128, 2, 1024], F16, "D")
    o_i32 = sb("o_i32", [128, 2, 1024], I32, "D")
    outstg = sb("outstg", [128, 2, D], F32, "D")
    xres = sb("xres", [128, 2, D], F32, "D")
    afres = sb("afres", [128, 2, D], F16, "D")

    wdt_tr = 0
    for ic in range(NIC):
        wdt_tr = dma("sync", "WdT", WdT[:, ic, :], wfm_d[ic],
                     deps=[wd_fst[(7, 0)], wd_fst[(7, 1)],
                           (sPE, c_pe), (sACT, c_act), (sDVE, c_dve),
                           (sPOOL, c_pool)])
    wdt_all = wdt_tr

    oq_done = {}
    dn_done = {}
    res_done = {}
    out_dma = {}
    oq_cast = [0, 0]
    oq_sc = [0, 0]
    for g in range(4):
        gg = g % 2
        oqv = 0
        for ic in range(NIC):
            pg2 = ic % 2
            dl = dma("gpsimd", ("old", pg2), o_ld[:, pg2, :],
                     o_s[ic, :, g * 1024:(g + 1) * 1024],
                     deps=[f2_store[(ic, g)],
                           (sACT, oq_cast[pg2]), (sDVE, ar3_bc)])
            c = emit("scalar", lambda E, pg2=pg2: E.e.activation(
                o_i32[:, pg2, :], o_ld[:, pg2, :], AF.Identity, scale=scb2[:, 0:1]),
                deps=[dl, (sDVE, ar3_bc), (sDVE, oq_sc[pg2])])
            oq_cast[pg2] = c
            svv = emit("vector", lambda E, ic=ic, gg=gg, pg2=pg2: E.e.tensor_scalar(
                oq[:, gg, ic, :], o_i32[:, pg2, :], scb2[:, 1:2], None, ALU.mult),
                deps=[(sACT, c), (sPE, dn_done.get((g - 2, 7), 0))])
            oq_sc[pg2] = svv
            oqv = svv
        oq_done[g] = oqv
        for tt in range(8):
            t_abs = g * 8 + tt
            pgd = tt % 2
            mm = 0
            war = res_done.get((g, tt - 2), 0) if tt >= 2 else \
                res_done.get((g - 1, 6 + tt), 0)
            for ic in range(NIC):
                for p in range(2):
                    deps = [(sDVE, oq_done[g]), wdt_all,
                            (sDVE, war)]
                    if g == 0 and tt < 2:
                        deps += [(sACT, c_act), (sDVE, c_dve), (sPE, c_pe)]
                    mm = emit("tensor", lambda E, ic=ic, p=p, tt=tt, gg=gg, pgd=pgd:
                        E.e.matmul(pbk[pgd][:, p * 512:(p + 1) * 512],
                                   oq[:, gg, ic, tt * 128:(tt + 1) * 128],
                                   WdT[:, ic, p * 512:(p + 1) * 512],
                                   start=(ic == 0), stop=(ic == NIC - 1)),
                        deps=deps)
            dn_done[(g, tt)] = mm
            xl = dma("sync", ("xres", pgd), xres[:, pgd, :],
                     x_d[t_abs * 128:(t_abs + 1) * 128, :],
                     deps=[(sDVE, war), (sDVE, ar3_bc)])
            al = dma("sync", ("afres", pgd), afres[:, pgd, :], af_s[t_abs],
                     deps=[blk[t_abs]["af_st"], (sDVE, war),
                           (sDVE, ar3_bc)])
            r1 = emit("vector", lambda E, pgd=pgd: E.e.tensor_tensor(
                outstg[:, pgd, :], pbk[pgd][:, :], afres[:, pgd, :], ALU.add),
                deps=[(sPE, mm), al, out_dma.get((g, tt - 2),
                      out_dma.get((g - 1, 6 + tt), 0))])
            r2 = emit("vector", lambda E, pgd=pgd: E.e.tensor_tensor(
                outstg[:, pgd, :], outstg[:, pgd, :], xres[:, pgd, :], ALU.add),
                deps=[(sDVE, r1), xl])
            res_done[(g, tt)] = r2
            out_dma[(g, tt)] = dma("sync", ("out", pgd),
                out_d[t_abs * 128:(t_abs + 1) * 128, :],
                outstg[:, pgd, :], deps=[(sDVE, r2)])

    scopes["D"].close()
    return _finish(nc, prog, scopes, ctx, dsems, Eng, cms)


def _finish(nc, prog, scopes, ctx, dsems, Eng, cms):
    for sc in scopes.values():
        try:
            sc.close()
        except Exception:
            pass
    for nm in ("xT", "WQKV"):
        try:
            cms[nm].__exit__(None, None, None)
        except Exception:
            pass
    finals = [(ct, ct.n) for ct in dsems.values() if ct.n > 0]

    with nc.Block() as block:
        @block.sync
        def _(sync):
            E = Eng(sync)
            for fn in prog["sync"]:
                fn(E)
            for ct, v in finals:
                sync.wait_ge(ct.h, v)

        @block.tensor
        def _(tensor):
            E = Eng(tensor)
            for fn in prog["tensor"]:
                fn(E)

        @block.scalar
        def _(scalar):
            E = Eng(scalar)
            for fn in prog["scalar"]:
                fn(E)

        @block.vector
        def _(vector):
            E = Eng(vector)
            for fn in prog["vector"]:
                fn(E)

        @block.gpsimd
        def _(gpsimd):
            E = Eng(gpsimd)
            for fn in prog["gpsimd"]:
                fn(E)

    ctx.close()
    return nc


_CACHE = {}


def kernel(L0_attn_x, L0_attn_Wq, L0_attn_Wk, L0_attn_Wv, L0_attn_proj_w,
           L0_attn_proj_b, L0_ffn_w_gate, L0_ffn_w_up, L0_ffn_w_down):
    if "nc" not in _CACHE:
        _CACHE["nc"] = build_program()
    nc = _CACHE["nc"]
    x = np.ascontiguousarray(np.asarray(L0_attn_x, dtype=np.float32))
    shared = dict(
        wq=np.ascontiguousarray(np.asarray(L0_attn_Wq, dtype=np.float32)),
        wk=np.ascontiguousarray(np.asarray(L0_attn_Wk, dtype=np.float32)),
        wv=np.ascontiguousarray(np.asarray(L0_attn_Wv, dtype=np.float32)),
        pw=np.ascontiguousarray(np.asarray(L0_attn_proj_w, dtype=np.float32)),
        pb=np.ascontiguousarray(np.asarray(L0_attn_proj_b, dtype=np.float32)).reshape(1, D),
        wg=np.ascontiguousarray(np.asarray(L0_ffn_w_gate, dtype=np.float32)),
        wu=np.ascontiguousarray(np.asarray(L0_ffn_w_up, dtype=np.float32)),
        wd=np.ascontiguousarray(np.asarray(L0_ffn_w_down, dtype=np.float32)),
    )
    in_maps = []
    for k in range(NCORES):
        m = dict(shared)
        m["x"] = np.ascontiguousarray(x[2 * k:2 * k + 2].reshape(TCORE, D))
        in_maps.append(m)
    res = run_bass_kernel_spmd(nc, in_maps, core_ids=list(range(NCORES)))
    parts = [np.asarray(r["out"]).reshape(2, 2048, D) for r in res.results]
    return np.concatenate(parts, axis=0)


# version 14
# speedup vs baseline: 1.2504x; 1.2277x over previous
from contextlib import ExitStack

import numpy as np
import concourse.bass as bass
import concourse.mybir as mybir
from concourse.bass_utils import run_bass_kernel_spmd

F32 = mybir.dt.float32
F16 = mybir.dt.float16
I32 = mybir.dt.int32
AF = mybir.ActivationFunctionType
ALU = mybir.AluOpType
AX = mybir.AxisListType

NCORES = 8
LIMIT = 99
C_BLOCKS = 32
C_STEP = 9
NB = 32
TCORE = NB * 128
D = 1024
I3 = 3072
NIC = 24
H, HS = 16, 64
QMAX = 32767.0
WQMAX = 7.0
EPS = 1e-5
NEG = -1.0e30


class Ct:
    def __init__(self, ctx, nc, name):
        self.h = ctx.enter_context(nc.semaphore(name))
        self.n = 0


class Eng:
    def __init__(self, e):
        self.e = e
        self.waited = {}

    def wait(self, deps):
        for ct, val in deps:
            if val is None or val <= 0:
                continue
            if self.waited.get(id(ct), -1) >= val:
                continue
            self.e.wait_ge(ct.h, val)
            self.waited[id(ct)] = val


def build_program():
    nc = bass.Bass("TRN2", target_bir_lowering=False, debug=False)

    x_d = nc.dram_tensor("x", [TCORE, D], F32, kind="ExternalInput").ap()
    wq_d = nc.dram_tensor("wq", [H, D, HS], F32, kind="ExternalInput").ap()
    wk_d = nc.dram_tensor("wk", [H, D, HS], F32, kind="ExternalInput").ap()
    wv_d = nc.dram_tensor("wv", [H, D, HS], F32, kind="ExternalInput").ap()
    pw_d = nc.dram_tensor("pw", [D, D], F32, kind="ExternalInput").ap()
    pb_d = nc.dram_tensor("pb", [1, D], F32, kind="ExternalInput").ap()
    wg_d = nc.dram_tensor("wg", [I3, D], F32, kind="ExternalInput").ap()
    wu_d = nc.dram_tensor("wu", [I3, D], F32, kind="ExternalInput").ap()
    wd_d = nc.dram_tensor("wd", [D, I3], F32, kind="ExternalInput").ap()
    out_d = nc.dram_tensor("out", [TCORE, D], F32, kind="ExternalOutput").ap()

    qk_s = nc.dram_tensor("qk_s", [NB, 4096, 128], F16).ap()
    v_s = nc.dram_tensor("v_s", [NB, 2048, 64], F16).ap()
    af_s = nc.dram_tensor("af_s", [NB, 128, D], F16).ap()
    up_s = nc.dram_tensor("up_s", [NIC, 128, TCORE], F16).ap()
    gt_s = nc.dram_tensor("gt_s", [NIC, 128, TCORE], F16).ap()
    o_s = nc.dram_tensor("o_s", [NIC, 128, TCORE], F16).ap()
    wdq_u = nc.dram_tensor("wdq_u", [NIC, 128, D], F16).ap()
    wdq_g = nc.dram_tensor("wdq_g", [NIC, 128, D], F16).ap()
    wdq_dd = nc.dram_tensor("wdq_dd", [8, 128, I3], F16).ap()
    wfm_u = nc.dram_tensor("wfm_u", [NIC, 128, 1024], F16).ap()
    wfm_g = nc.dram_tensor("wfm_g", [NIC, 128, 1024], F16).ap()
    wfm_d = nc.dram_tensor("wfm_d", [NIC, 128, 1024], F16).ap()
    pw16 = nc.dram_tensor("pw16", [8, 128, H, 128], F16).ap()
    l_s = nc.dram_tensor("l_s", [NB, 2048], F16).ap()
    rl_s = nc.dram_tensor("rl_s", [NB, 2048], F16).ap()
    cc_i = [nc.dram_tensor(f"cc{i}i", [1, 8], F32).ap() for i in range(3)]
    cc_o = [nc.dram_tensor(f"cc{i}o", [1, 8], F32).ap() for i in range(3)]

    ctx = ExitStack()
    scopes = {"A": ExitStack(), "B": ExitStack(), "C": ExitStack(),
              "D": ExitStack()}
    cms = {}

    def sb(name, shape, dtype, scope=None):
        if scope is None:
            cm = nc.sbuf_tensor(name, shape, dtype)
            t = ctx.enter_context(cm)
        elif scope == "manual":
            cm = nc.sbuf_tensor(name, shape, dtype)
            t = cm.__enter__()
        else:
            cm = nc.sbuf_tensor(name, shape, dtype, side="right")
            t = scopes[scope].enter_context(cm)
        cms[name] = cm
        return t

    idn32 = sb("idn32", [128, 128], F32)
    mtri = sb("mtri", [128, 128], F32)
    onesC = sb("onesC", [128, 128], F32)
    onesC16 = sb("onesC16", [128, 128], F16)
    stats = sb("stats", [128, 8, 10], F32)
    rmax = sb("rmax", [128, 8], F32)
    red8 = sb("red8", [8, 1], F32)
    sc8 = sb("sc8", [1, 16], F32)
    ss = sb("ss", [1, 16], F32)
    ssi = sb("ssi", [1, 2], I32)
    scb = sb("scb", [128, 8], F32)
    scb2 = sb("scb2", [128, 2], F32)
    cc_sb = sb("cc_sb", [1, 8], F32)
    tmpB = sb("tmpB", [128, 2], F32)
    tmpF = sb("tmpF", [128, 2], F32)
    WQKV = sb("WQKV", [128, 8, 3072], F16, "manual")
    xT = sb("xT", [128, 8, TCORE], F16, "manual")


    pbk = [ctx.enter_context(nc.psum_tensor(f"pbk{i}", [128, 1024], F32))
           for i in range(4)]

    sPE = Ct(ctx, nc, "sPE")
    sACT = Ct(ctx, nc, "sACT")
    sDVE = Ct(ctx, nc, "sDVE")
    sPOOL = Ct(ctx, nc, "sPOOL")
    sDMA = Ct(ctx, nc, "sDMA")
    sGD = Ct(ctx, nc, "sGD")
    sCC = Ct(ctx, nc, "sCC")

    prog = {k: [] for k in ("sync", "tensor", "scalar", "vector", "gpsimd")}
    ENG = {"tensor": sPE, "scalar": sACT, "vector": sDVE, "gpsimd": sPOOL}

    def emit(engine, builder, deps=()):
        sem = ENG[engine]
        sem.n += 1
        deps = [d for d in deps if d is not None and d != 0]
        def fn(E, builder=builder, deps=deps, sem=sem):
            E.wait(deps)
            ins = builder(E)
            ins.then_inc(sem.h, 1)
        prog[engine].append(fn)
        return sem.n

    dsems = {}

    def dsem(key):
        if key not in dsems:
            dsems[key] = Ct(ctx, nc, "dk_%s" % "_".join(str(x) for x in (
                key if isinstance(key, tuple) else (key,))))
        return dsems[key]

    def dma(engine, key, out, in_, deps=(), **kw):
        sem = dsem(key)
        sem.n += 16
        deps = [d for d in deps if d is not None and d != 0]
        def fn(E, out=out, in_=in_, deps=deps, kw=kw, sem=sem):
            E.wait(deps)
            ins = E.e.dma_start(out=out, in_=in_, **kw)
            ins.then_inc(sem.h, 16)
        prog[engine].append(fn)
        return (sem, sem.n)

    def dmat(key, out, in_, deps=()):
        sem = dsem(key)
        sem.n += 16
        deps = [d for d in deps if d is not None and d != 0]
        def fn(E, out=out, in_=in_, deps=deps, sem=sem):
            E.wait(deps)
            ins = E.e.dma_start_transpose(out, in_)
            ins.then_inc(sem.h, 16)
        prog["sync"].append(fn)
        return (sem, sem.n)

    def collective(cci, cco, deps):
        sCC.n += 1
        deps = [d for d in deps if d is not None]
        def fn(E, deps=deps):
            E.wait(deps)
            ins = E.e.collective_compute(
                "AllReduce", ALU.max, replica_groups=[list(range(NCORES))],
                ins=[cci.opt()], outs=[cco.opt()])
            ins.then_inc(sCC.h, 1)
        prog["gpsimd"].append(fn)
        return sCC.n

    d_iota = emit("gpsimd", lambda E: E.e.iota(
        iotaf[:], pattern=[[1, 128]], base=0, channel_multiplier=-1,
        allow_small_or_imprecise_dtypes=True))
    d_idn32 = emit("vector", lambda E: E.e.tensor_scalar(
        idn32[:], iotaf[:], 0.0, None, ALU.is_equal), deps=[(sPOOL, d_iota)])
    d_m01 = emit("vector", lambda E: E.e.tensor_scalar(
        mtri[:], iotaf[:], 0.0, None, ALU.is_ge), deps=[(sPOOL, d_iota)])
    d_mtri = emit("vector", lambda E: E.e.tensor_scalar(
        mtri[:], mtri[:], 1.0, -NEG, ALU.subtract, ALU.mult),
        deps=[(sDVE, d_m01)])
    d_ones = emit("vector", lambda E: E.e.memset(onesC[:], 1.0))
    d_ones = emit("vector", lambda E: E.e.memset(onesC16[:], 1.0),
                  deps=[(sDVE, d_ones)])
    d_rm0 = emit("vector", lambda E: E.e.memset(rmax[:], 0.0))
    d_st0 = emit("vector", lambda E: E.e.memset(stats[:], 1.0))
    d_rm1 = emit("vector", lambda E: E.e.tensor_scalar(
        rmax[:, 3:4], rmax[:, 3:4], 0.0, NEG, ALU.mult, ALU.add),
        deps=[(sDVE, d_rm0)])

    iotaf = sb("iotaf", [128, 128], F32, "A")
    xa = sb("xa", [128, 4, D], F32, "A")
    xn32 = sb("xn32", [128, 2, D], F32, "A")
    sqdmp = sb("sqdmp", [128, D], F32, "A")
    pw_f32 = sb("pw_f32", [128, 2, D], F32, "A")
    pw_cast = sb("pw_cast", [128, 2, D], F16, "A")
    WDH = 1536
    wch = sb("wch", [128, 2, WDH], F32, "A")
    dqmax = sb("dqmax", [128, 2, 48], F32, "A")
    dqs = sb("dqs", [128, 2, 48], F32, "A")
    dqsr = sb("dqsr", [128, 2, 48], F32, "A")
    dqz = sb("dqz", [128, 2, WDH], F32, "A")
    dqi = sb("dqi", [128, 2, WDH], I32, "A")
    dq16 = sb("dq16", [128, 2, WDH], F16, "A")
    ftA = sb("ftA", [128, 2, 12, 128], F16, "A")



    wqkv_done = 0
    for fc in range(8):
        for wsrc, base in ((wq_d, 0), (wk_d, 1024), (wv_d, 2048)):
            src = wsrc[:, fc * 128:(fc + 1) * 128, :].rearrange("i f d -> f i d")
            dst = WQKV[:, fc, base:base + 1024].rearrange("p (i d) -> p i d", d=HS)
            wqkv_done = dma("gpsimd", "wqkv", dst, src)

    pw_st = [0] * 8
    pw_cv = [0] * 8
    for ec in range(8):
        pg = ec % 2
        dl = dma("sync", ("pwf", pg), pw_f32[:, pg, :], pw_d[ec * 128:(ec + 1) * 128, :],
                 deps=[(sACT, pw_cv[ec - 2]) if ec >= 2 else None])
        cv = emit("scalar", lambda E, pg=pg: E.e.activation(
            pw_cast[:, pg, :], pw_f32[:, pg, :], AF.Copy),
            deps=[dl, pw_st[ec - 2] if ec >= 2 else None])
        pw_cv[ec] = cv
        pw_st[ec] = dma("sync", ("pw16", pg), pw16[ec, :, :, 0:64],
                        pw_cast[:, pg, :].rearrange("p (i d) -> p i d", d=64),
                        deps=[(sACT, cv)])

    trp = pbk[3]
    last_tr_copy = 0
    last_sq = 0
    last_xn = [0, 0]
    xa_free = [0] * 4
    amx_of = [0] * 4
    a_mx = d_rm1
    for g in range(8):
        xa_load = [0] * 4
        for j in range(4):
            b = g * 4 + j
            xa_load[j] = dma("sync", ("xa", j), xa[:, j, :], x_d[b * 128:(b + 1) * 128, :],
                             deps=[(sACT, xa_free[j]) if g > 0 else None])
        for j in range(4):
            last_sq = emit("scalar", lambda E, j=j: E.e.activation(
                sqdmp[:], xa[:, j, :], AF.Square, accum_out=stats[:, j, 0:1]),
                deps=[xa_load[j], (sACT, last_sq), (sDVE, amx_of[j])])
            emit("vector", lambda E, j=j: E.e.reduce_sum(
                stats[:, j, 1:2], xa[:, j, :], AX.X), deps=[xa_load[j]])
        vv = sDVE.n
        vv = emit("vector", lambda E: E.e.tensor_scalar(
            stats[:, :, 2:3].opt(), stats[:, :, 1:2].opt(), 1.0 / D, None,
            ALU.mult), deps=[(sDVE, vv)])
        vv = emit("vector", lambda E: E.e.tensor_scalar(
            stats[:, :, 3:4].opt(), stats[:, :, 0:1].opt(), 1.0 / D, None,
            ALU.mult), deps=[(sACT, last_sq), (sDVE, vv)])
        vv = emit("vector", lambda E: E.e.tensor_tensor(
            stats[:, :, 4:5].opt(), stats[:, :, 2:3].opt(),
            stats[:, :, 2:3].opt(), ALU.mult), deps=[(sDVE, vv)])
        vv = emit("vector", lambda E: E.e.tensor_tensor(
            stats[:, :, 5:6].opt(), stats[:, :, 3:4].opt(),
            stats[:, :, 4:5].opt(), ALU.subtract), deps=[(sDVE, vv)])
        vv = emit("vector", lambda E: E.e.tensor_scalar(
            stats[:, :, 5:6].opt(), stats[:, :, 5:6].opt(), 1.0, EPS, ALU.mult,
            ALU.add), deps=[(sDVE, vv)])
        vv = emit("vector", lambda E: E.e.memset(stats[:, :, 6:7].opt(), 1.0),
                  deps=[(sDVE, vv)])
        for it in range(7):
            vv = emit("vector", lambda E: E.e.tensor_tensor(
                stats[:, :, 7:8].opt(), stats[:, :, 6:7].opt(),
                stats[:, :, 6:7].opt(), ALU.mult), deps=[(sDVE, vv)])
            vv = emit("vector", lambda E: E.e.tensor_tensor(
                stats[:, :, 8:9].opt(), stats[:, :, 7:8].opt(),
                stats[:, :, 5:6].opt(), ALU.mult), deps=[(sDVE, vv)])
            vv = emit("vector", lambda E: E.e.tensor_scalar(
                stats[:, :, 8:9].opt(), stats[:, :, 8:9].opt(), -0.5, 1.5,
                ALU.mult, ALU.add), deps=[(sDVE, vv)])
            vv = emit("vector", lambda E: E.e.tensor_tensor(
                stats[:, :, 6:7].opt(), stats[:, :, 6:7].opt(),
                stats[:, :, 8:9].opt(), ALU.mult), deps=[(sDVE, vv)])
        vv = emit("vector", lambda E: E.e.scalar_tensor_tensor(
            stats[:, :, 9:10].opt(), stats[:, :, 2:3].opt(), -1.0,
            stats[:, :, 6:7].opt(), ALU.mult, ALU.mult), deps=[(sDVE, vv)])
        grp = vv
        for j in range(4):
            b = g * 4 + j
            pg = j % 2
            d_xn = emit("scalar", lambda E, j=j, pg=pg: E.e.activation(
                xn32[:, pg, :], xa[:, j, :], AF.Identity, bias=stats[:, j, 9:10],
                scale=stats[:, j, 6:7]),
                deps=[(sDVE, grp), (sDVE, last_xn[pg])])
            xa_free[j] = d_xn
            d_mx = emit("vector", lambda E, j=j, pg=pg: E.e.tensor_reduce(
                stats[:, j, 0:1], xn32[:, pg, :], AX.X, ALU.max,
                apply_absolute_value=True), deps=[(sACT, d_xn)])
            a_mx = emit("vector", lambda E, j=j: E.e.tensor_tensor(
                rmax[:, 0:1], rmax[:, 0:1], stats[:, j, 0:1], ALU.max),
                deps=[(sDVE, d_mx), (sDVE, a_mx)])
            amx_of[j] = a_mx
            trs = []
            for fc in range(8):
                tr = emit("tensor", lambda E, fc=fc, pg=pg: E.e.transpose(
                    trp[:, fc * 128:(fc + 1) * 128],
                    xn32[:, pg, fc * 128:(fc + 1) * 128], idn32[:]),
                    deps=[(sACT, d_xn), (sDVE, d_idn32), (sDVE, last_tr_copy)])
                trs.append(tr)
            for fc in range(8):
                last_tr_copy = emit("vector",
                    lambda E, fc=fc, b=b: E.e.tensor_copy(
                        xT[:, fc, b * 128:(b + 1) * 128],
                        trp[:, fc * 128:(fc + 1) * 128]),
                    deps=[(sPE, trs[fc])])
            last_xn[pg] = last_tr_copy
    a_xT = last_tr_copy

    dq_state = {"z": 0, "c": 0, "w": 0, "st": [0, 0], "ft": [0, 0], "fs": [0, 0]}

    def dq_chunk(src, width, wdq_row, wfm_tiles, k):
        pg = k % 2
        ng = width // 32
        nt = width // 128
        st_ = dq_state
        dl = dma("gpsimd", "wch", wch[:, pg, 0:width], src,
                 deps=[(sDVE, st_["z"])])
        r = emit("vector", lambda E: E.e.tensor_reduce(
            dqmax[:, pg, 0:ng], wch[:, pg, 0:width].rearrange(
                "p (g e) -> p g e", e=32),
            AX.X, ALU.max, apply_absolute_value=True), deps=[dl])
        r = emit("vector", lambda E: E.e.tensor_scalar(
            dqs[:, pg, 0:ng], dqmax[:, pg, 0:ng], 1.0 / WQMAX, 1e-12,
            ALU.mult, ALU.max), deps=[(sDVE, r)])
        r = emit("vector", lambda E: E.e.reciprocal(
            dqsr[:, pg, 0:ng], dqs[:, pg, 0:ng]), deps=[(sDVE, r)])
        z = emit("vector", lambda E: E.e.tensor_tensor(
            dqz[:, pg, 0:width].rearrange("p (g e) -> p g e", e=32),
            wch[:, pg, 0:width].rearrange("p (g e) -> p g e", e=32),
            dqsr[:, pg, 0:ng].unsqueeze(2).broadcast_to([128, ng, 32]),
            ALU.mult), deps=[(sDVE, r), (sACT, st_["c"])])
        st_["z"] = z
        c = emit("scalar", lambda E: E.e.activation(
            dqi[:, pg, 0:width], dqz[:, pg, 0:width], AF.Copy),
            deps=[(sDVE, z), (sDVE, st_["w"])])
        st_["c"] = c
        w = emit("vector", lambda E: E.e.tensor_tensor(
            dq16[:, pg, 0:width].rearrange("p (g e) -> p g e", e=32),
            dqi[:, pg, 0:width].rearrange("p (g e) -> p g e", e=32),
            dqs[:, pg, 0:ng].unsqueeze(2).broadcast_to([128, ng, 32]),
            ALU.mult), deps=[(sACT, c), st_["st"][pg]])
        st_["w"] = w
        stq = dma("gpsimd", ("wdq", pg), wdq_row, dq16[:, pg, 0:width],
                  deps=[(sDVE, w)])
        st_["st"][pg] = stq
        trs = []
        for t in range(nt):
            trs.append(dmat(("ftA", pg), ftA[:, pg, t, :],
                            wdq_row_tile(wdq_row, t),
                            deps=[stq, st_["fs"][pg]]))
        fst = dma("sync", ("wfm", pg), wfm_tiles,
                  ftA[:, pg, 0:nt, :], deps=[trs[-1]])
        st_["ft"][pg] = trs[-1]
        st_["fs"][pg] = fst
        return fst

    def wdq_row_tile(row_ap, t):
        return row_ap[:, t * 128:(t + 1) * 128]

    wfm_store = {}
    _k = 0
    for ic in range(NIC):
        for wt, wsrc, ddst, fdst in (("u", wu_d, wdq_u, wfm_u),
                                     ("g", wg_d, wdq_g, wfm_g)):
            wfm_store[(wt, ic)] = dq_chunk(
                wsrc[ic * 128:(ic + 1) * 128, :], 1024,
                ddst[ic, :, :],
                fdst[ic].rearrange("p (t e) -> p t e", e=128), _k)
            _k += 1
    wd_fst = {}
    for ec in range(8):
        for hv in range(2):
            wd_fst[(ec, hv)] = dq_chunk(
                wd_d[ec * 128:(ec + 1) * 128, hv * WDH:(hv + 1) * WDH], WDH,
                wdq_dd[ec, :, hv * WDH:(hv + 1) * WDH],
                wfm_d[hv * 12:(hv + 1) * 12, :,
                      ec * 128:(ec + 1) * 128].rearrange("i p e -> p i e"),
                _k)
            _k += 1

    def allreduce(idx, extra_deps):
        tr = emit("tensor", lambda E: E.e.transpose(
            pbk[3][0:8, 0:128], rmax[:, 0:8], idn32[:]),
            deps=list(extra_deps) + [(sDVE, d_idn32)])
        rd = emit("vector", lambda E: E.e.reduce_max(
            red8[:], pbk[3][0:8, 0:128], AX.X), deps=[(sPE, tr)])
        st = dma("gpsimd", "cci", cc_i[idx][0:1, 0:8].rearrange("a b -> b a"),
                 red8[:], deps=[(sDVE, rd)])
        ccv = collective(cc_i[idx], cc_o[idx], deps=[st])
        ld = dma("gpsimd", "ccsb", cc_sb[:], cc_o[idx], deps=[(sCC, ccv)])
        return tr, ld

    def bcast_cols(pairs, deps):
        last = 0
        for k, (src, dst) in enumerate(pairs):
            b1 = emit("tensor", lambda E, src=src: E.e.matmul(
                pbk[3][:, 512 + 16 * 0: 512 + 16 * 0 + 1] if False else
                pbk[3][:, 512:513], onesC[0:1, :], src.opt(),
                start=True, stop=True),
                deps=list(deps) + [(sDVE, d_ones), (sDVE, last)])
            last = emit("vector", lambda E, dst=dst: E.e.tensor_copy(
                dst, pbk[3][:, 512:513]), deps=[(sPE, b1)])
        return last

    ar1_tr, ar1_ld = allreduce(0, [(sDVE, a_mx), (sDVE, last_tr_copy)])
    v = emit("vector", lambda E: E.e.tensor_scalar(
        sc8[0:1, 8:9], cc_sb[0:1, 0:1], 1.0 / QMAX, 1e-12, ALU.mult, ALU.max),
        deps=[ar1_ld])
    v = emit("vector", lambda E: E.e.reciprocal(sc8[0:1, 9:10], sc8[0:1, 8:9]),
             deps=[(sDVE, v)])
    ar1_bc = bcast_cols([(sc8[0:1, 9:10], scb[:, 0:1]),
                         (sc8[0:1, 8:9], scb[:, 1:2])], [(sDVE, v)])

    scopes["A"].close()
    xqT = sb("xqT", [128, 8, TCORE], F16, "B")
    xq_i32 = sb("xq_i32", [128, 2, 1024], I32, "B")
    ftile = sb("ftile", [128, 2, 8, 128], F16, "B")
    stgF = sb("stgF", [128, 2, 2, 1024], F16, "B")


    if LIMIT < 1:
        return _finish(nc, prog, scopes, ctx, dsems, Eng, cms)
    xq_done = {}
    xq_prev = [0, 0]
    _kq = 0
    for q4 in range(4):
        for fc in range(8):
            pgq = _kq % 2
            _kq += 1
            c = emit("scalar", lambda E, fc=fc, q4=q4, pgq=pgq: E.e.activation(
                xq_i32[:, pgq, :], xT[:, fc, q4 * 1024:(q4 + 1) * 1024],
                AF.Identity, scale=scb[:, 0:1]),
                deps=[(sDVE, ar1_bc), (sDVE, a_xT), (sDVE, xq_prev[pgq])])
            xq_prev[pgq] = emit("vector",
                lambda E, fc=fc, q4=q4, pgq=pgq: E.e.tensor_scalar(
                    xqT[:, fc, q4 * 1024:(q4 + 1) * 1024], xq_i32[:, pgq, :],
                    scb[:, 1:2], None, ALU.mult), deps=[(sACT, c)])
            if q4 % 2 == 1:
                xq_done[(fc, q4 // 2)] = xq_prev[pgq]

    f1_seq = []
    for ic in range(NIC):
        f1_seq.append(("u", ic))
        f1_seq.append(("g", ic))

    if LIMIT < 2:
        return _finish(nc, prog, scopes, ctx, dsems, Eng, cms)
    def psF(wt, p):
        base = 0 if wt == "u" else 2
        t = pbk[base + p // 2]
        return t[:, (p % 2) * 512:(p % 2) * 512 + 512]

    f1_store = {}
    f1_evA = {}
    f1_evV = {}
    f1_pool = {}
    ftile_rd = [0, 0]
    stg_free = {}
    first_gate_pb3 = True
    seq_all = [(hh, wt, ic) for hh in range(2) for (wt, ic) in f1_seq]

    def emit_ftl(kk):
        hh, wt, ic = seq_all[kk]
        pg = kk % 2
        fsrc = wfm_u if wt == "u" else wfm_g
        return dma("sync", ("ftile", pg), ftile[:, pg, :, :],
                   fsrc[ic].rearrange("p (t e) -> p t e", e=128),
                   deps=[wfm_store[(wt, ic)], (sPE, ftile_rd[pg])])

    ftl_of = {0: emit_ftl(0)}
    for kk, (hh, wt, ic) in enumerate(seq_all):
        pg = kk % 2
        cpg = kk % 2
        sdram = up_s if wt == "u" else gt_s
        if kk + 1 < len(seq_all):
            ftl_of[kk + 1] = emit_ftl(kk + 1)
        ftl = ftl_of[kk]
        mm = 0
        for fc in range(8):
            for p in range(4):
                extra = []
                if wt == "g" and first_gate_pb3 and p >= 2:
                    extra.append((sDVE, ar1_bc))
                mm = emit("tensor", lambda E, fc=fc, p=p, pg=pg, wt=wt, hh=hh:
                    E.e.matmul(psF(wt, p), ftile[:, pg, fc, :],
                               xqT[:, fc, hh * 2048 + p * 512:hh * 2048 + (p + 1) * 512],
                               start=(fc == 0), stop=(fc == 7)),
                    deps=[ftl, (sDVE, xq_done[(fc, hh)]),
                          (sACT, f1_evA.get((wt, hh, ic - 1),
                                 f1_evA.get((wt, hh - 1, NIC - 1), 0))),
                          (sDVE, f1_evV.get((wt, hh, ic - 1),
                                 f1_evV.get((wt, hh - 1, NIC - 1), 0)))] + extra)
        if wt == "g":
            first_gate_pb3 = False
        ftile_rd[pg] = mm
        ev_a = ev_v = 0
        stv = 0
        col = 1 if wt == "u" else 2
        pr = 0
        for half in range(2):
            sdep = [stg_free.get((cpg, half), 0),
                    (sDVE, f1_pool.get((cpg, half), 0))]
            ev_a = emit("scalar", lambda E, half=half, wt=wt, cpg=cpg:
                E.e.activation(stgF[:, cpg, half, 0:512], psF(wt, 2 * half),
                               AF.Copy), deps=[(sPE, mm)] + sdep)
            ev_v = emit("vector", lambda E, half=half, wt=wt, cpg=cpg:
                E.e.tensor_copy(stgF[:, cpg, half, 512:1024],
                                psF(wt, 2 * half + 1)),
                deps=[(sPE, mm)] + sdep)
            stv = dma("sync", ("f1st", cpg, half),
                      sdram[ic, :, hh * 2048 + half * 1024:
                            hh * 2048 + (half + 1) * 1024],
                      stgF[:, cpg, half, :],
                      deps=[(sACT, ev_a), (sDVE, ev_v)])
            stg_free[(cpg, half)] = stv
            f1_store[(wt, hh, ic, half)] = stv
            pr = emit("vector", lambda E, half=half, cpg=cpg: E.e.tensor_reduce(
                tmpB[:, 0:1], stgF[:, cpg, half, :], AX.X, ALU.max,
                apply_absolute_value=True),
                deps=[(sACT, ev_a), (sDVE, ev_v), (sPE, ar1_tr),
                      (sDVE, pr)])
            pr = emit("vector", lambda E, col=col: E.e.tensor_tensor(
                rmax[:, col:col + 1], rmax[:, col:col + 1], tmpB[:, 0:1],
                ALU.max), deps=[(sDVE, pr)])
            if wt == "g":
                pr = emit("vector", lambda E, half=half, cpg=cpg: E.e.tensor_reduce(
                    tmpB[:, 1:2], stgF[:, cpg, half, :], AX.X, ALU.max),
                    deps=[(sDVE, pr)])
                pr = emit("vector", lambda E: E.e.tensor_tensor(
                    rmax[:, 3:4], rmax[:, 3:4], tmpB[:, 1:2], ALU.max),
                    deps=[(sDVE, pr)])
            f1_pool[(cpg, half)] = pr
        f1_evA[(wt, hh, ic)] = ev_a
        f1_evV[(wt, hh, ic)] = ev_v
    b_pe = sPE.n
    b_pool = sPOOL.n
    b_act = sACT.n
    b_dve = sDVE.n

    ar2_tr, ar2_ld = allreduce(1, [(sPOOL, b_pool), (sPE, b_pe),
                                   (sACT, b_act), (sDVE, b_dve)])
    v = emit("vector", lambda E: E.e.tensor_scalar(
        ss[0:1, 0:1], cc_sb[0:1, 1:2], 1.0 / QMAX, 1e-12, ALU.mult, ALU.max),
        deps=[ar2_ld])
    v = emit("vector", lambda E: E.e.tensor_scalar(
        ss[0:1, 1:2], cc_sb[0:1, 2:3], 1.0 / QMAX, 1e-12, ALU.mult, ALU.max),
        deps=[(sDVE, v)])
    v = emit("vector", lambda E: E.e.reciprocal(ss[0:1, 2:3], ss[0:1, 0:1]),
             deps=[(sDVE, v)])
    v = emit("vector", lambda E: E.e.reciprocal(ss[0:1, 3:4], ss[0:1, 1:2]),
             deps=[(sDVE, v)])
    v = emit("vector", lambda E: E.e.tensor_tensor(
        ss[0:1, 4:5], cc_sb[0:1, 3:4], ss[0:1, 3:4], ALU.mult), deps=[(sDVE, v)])
    v = emit("vector", lambda E: E.e.tensor_copy(ssi[0:1, 0:1], ss[0:1, 4:5]),
             deps=[(sDVE, v)])
    v = emit("vector", lambda E: E.e.tensor_tensor(
        ss[0:1, 5:6], ssi[0:1, 0:1], ss[0:1, 1:2], ALU.mult), deps=[(sDVE, v)])
    v = emit("scalar", lambda E: E.e.activation(
        ss[0:1, 6:7], ss[0:1, 5:6], AF.Tanh, scale=0.5), deps=[(sDVE, v)])
    v = emit("vector", lambda E: E.e.tensor_scalar(
        ssi[0:1, 1:2], ss[0:1, 6:7], 32768.0, 32768.0, ALU.mult, ALU.add),
        deps=[(sACT, v)])
    v = emit("vector", lambda E: E.e.tensor_scalar(
        ss[0:1, 7:8], ssi[0:1, 1:2], 1.0 / 65536.0, None, ALU.mult),
        deps=[(sDVE, v)])
    v = emit("vector", lambda E: E.e.tensor_tensor(
        ss[0:1, 8:9], ss[0:1, 5:6], ss[0:1, 7:8], ALU.mult), deps=[(sDVE, v)])
    v = emit("vector", lambda E: E.e.tensor_scalar(
        ss[0:1, 9:10], ss[0:1, 8:9], 1.0 / QMAX, 1e-12, ALU.mult, ALU.max),
        deps=[(sDVE, v)])
    v = emit("vector", lambda E: E.e.tensor_scalar(
        ss[0:1, 10:11], ss[0:1, 9:10], 65536.0, None, ALU.mult), deps=[(sDVE, v)])
    v = emit("vector", lambda E: E.e.reciprocal(ss[0:1, 11:12], ss[0:1, 10:11]),
             deps=[(sDVE, v)])
    v = emit("vector", lambda E: E.e.tensor_tensor(
        ss[0:1, 12:13], ss[0:1, 1:2], ss[0:1, 11:12], ALU.mult), deps=[(sDVE, v)])
    v = emit("vector", lambda E: E.e.tensor_tensor(
        ss[0:1, 13:14], ss[0:1, 9:10], ss[0:1, 0:1], ALU.mult), deps=[(sDVE, v)])
    v = emit("vector", lambda E: E.e.tensor_scalar(
        ss[0:1, 14:15], ss[0:1, 1:2], 0.5, None, ALU.mult), deps=[(sDVE, v)])
    ar2_bc = bcast_cols([(ss[0:1, 2:3], scb[:, 2:3]),
                         (ss[0:1, 3:4], scb[:, 3:4]),
                         (ss[0:1, 14:15], scb[:, 4:5]),
                         (ss[0:1, 12:13], scb[:, 5:6]),
                         (ss[0:1, 13:14], scb[:, 6:7])], [(sDVE, v)])

    scopes["B"].close()
    pbias = sb("pbias", [1, D], F32, "C")
    pwdmp = sb("pwdmp", [128, H, 8, 128], F16, "C")
    f2U = sb("f2U", [128, 3, 1024], F16, "C")
    f2G = sb("f2G", [128, 3, 1024], F16, "C")
    I16 = mybir.dt.int16
    f2ru = sb("f2ru", [128, 1024], I16, "C")
    f2rg = sb("f2rg", [128, 1024], I16, "C")
    f2t1 = sb("f2t1", [128, 1024], F32, "C")
    f2r2 = sb("f2r2", [128, 1024], I32, "C")
    f2o = sb("f2o", [128, 1024], F16, "C")
    stgQ = sb("stgQ", [128, 2, 1024], F16, "C")
    QKT = sb("QKT", [128, 4096], F16, "C")
    Vaug = sb("Vaug", [128, H, 65], F16, "C")
    PT = sb("PT", [128, 2, 512], F16, "C")
    OTn = sb("OTn", [64, 2048], F16, "C")
    scrB = sb("scrB", [128, 2048], F16, "C")
    lcol = sb("lcol", [128, 16], F16, "C")
    rcol = sb("rcol", [128, 16], F32, "C")
    stgAF = sb("stgAF", [128, D], F16, "C")

    d_vone = emit("vector", lambda E: E.e.memset(Vaug[:, :, 64:65], 1.0),
                  deps=[(sPE, b_pe), (sACT, b_act), (sDVE, b_dve)])
    d_pb = dma("gpsimd", "pb", pbias[:], pb_d, deps=[(sDVE, d_vone)])

    if LIMIT < 3:
        return _finish(nc, prog, scopes, ctx, dsems, Eng, cms)
    pw_tr = 0
    for i in range(H):
        for ec in range(8):
            pw_tr = dmat("pwdmp", pwdmp[:, i, ec, :], pw16[ec, :, i, :],
                         deps=[pw_st[ec]])
    pw_tr_all = pw_tr

    f2_list = [(ic, q) for q in range(4) for ic in range(NIC)]
    f2p = {}
    f2_store = {}
    f2_loaded = {}

    def f2_load(k):
        if k >= len(f2_list):
            return
        ic, q = f2_list[k]
        hh = q // 2
        qo = q * 1024
        sl = k % 3
        dl_u = dma("gpsimd", ("f2U", sl), f2U[:, sl, :],
                   up_s[ic, :, qo:qo + 1024],
                   deps=[f1_store[("u", hh, ic, q % 2)],
                         (sACT, f2p.get(("ru", sl), 0))])
        dl_g = dma("gpsimd", ("f2G", sl), f2G[:, sl, :],
                   gt_s[ic, :, qo:qo + 1024],
                   deps=[f1_store[("g", hh, ic, q % 2)],
                         (sACT, f2p.get(("rg", sl), 0))])
        f2_loaded[k] = (dl_u, dl_g)

    def f2_compute(k):
        if k >= len(f2_list):
            return
        ic, q = f2_list[k]
        qo = q * 1024
        sl = k % 3
        pg = k % 2
        dl_u, dl_g = f2_loaded[k]
        ru = emit("scalar", lambda E, sl=sl: E.e.activation(
            f2ru[:], f2U[:, sl, :], AF.Identity, scale=scb[:, 2:3]),
            deps=[dl_u, (sDVE, ar2_bc), (sDVE, f2p.get("t2", 0))])
        rg = emit("scalar", lambda E, sl=sl: E.e.activation(
            f2rg[:], f2G[:, sl, :], AF.Identity, scale=scb[:, 3:4]),
            deps=[dl_g, (sDVE, f2p.get("t1", 0))])
        f2p[("ru", sl)] = ru
        f2p[("rg", sl)] = rg
        tg = emit("scalar", lambda E: E.e.activation(
            f2t1[:], f2rg[:], AF.Tanh, scale=scb[:, 4:5]),
            deps=[(sACT, rg), (sDVE, f2p.get("mx", 0)),
                  (sDVE, f2p.get("t2", 0))])
        rs = emit("vector", lambda E: E.e.tensor_scalar(
            f2r2[:], f2t1[:], 32768.0, 32768.0, ALU.mult, ALU.add),
            deps=[(sACT, tg)])
        t1 = emit("vector", lambda E: E.e.tensor_tensor(
            f2t1[:], f2rg[:], f2r2[:], ALU.mult), deps=[(sDVE, rs)])
        f2p["t1"] = t1
        r2 = emit("vector", lambda E: E.e.tensor_scalar(
            f2r2[:], f2t1[:], scb[:, 5:6], None, ALU.mult), deps=[(sDVE, t1)])
        t2 = emit("vector", lambda E: E.e.tensor_tensor(
            f2t1[:], f2r2[:], f2ru[:], ALU.mult), deps=[(sDVE, r2)])
        f2p["t2"] = t2
        ost = emit("vector", lambda E: E.e.tensor_scalar(
            f2o[:], f2t1[:], scb[:, 6:7], None, ALU.mult),
            deps=[(sDVE, t2), f2p.get("st_dma", 0)])
        mx = emit("vector", lambda E: E.e.tensor_reduce(
            tmpF[:, 0:1], f2t1[:], AX.X, ALU.max, apply_absolute_value=True),
            deps=[(sDVE, ost), (sPE, ar2_tr)])
        mx = emit("vector", lambda E: E.e.tensor_tensor(
            rmax[:, 4:5], rmax[:, 4:5], tmpF[:, 0:1], ALU.max),
            deps=[(sDVE, mx)])
        f2p["mx"] = mx
        st = dma("gpsimd", "f2o", o_s[ic, :, qo:qo + 1024],
                 f2o[:], deps=[(sDVE, ost), (sDVE, mx)])
        f2p["st_dma"] = st
        f2_store[(ic, q)] = st

    blk = {}
    qst = {}
    psS_slot = [pbk[3][:, 0:512], pbk[3][:, 512:1024]]

    def pO_slot(p0, p1):
        if p0 < 1024:
            return pbk[1][0:65, p0:p1]
        return pbk[2][0:65, p0 - 1024:p1 - 1024]

    def pOn_slot(p):
        if p < 2:
            return pbk[1][0:64, p * 512:(p + 1) * 512]
        return pbk[2][0:64, (p - 2) * 512:(p - 1) * 512]

    def emit_qkv(b, pbk0_free_act, extra=()):
        prev = blk.get(b - 1, {})
        ev = 0
        sts = []
        dsts = [qk_s[b, 0:2048, 0:64].rearrange("(t i) d -> t i d", i=H),
                qk_s[b, 2048:4096, 0:64].rearrange("(t i) d -> t i d", i=H),
                v_s[b, :, :].rearrange("(t i) d -> t i d", i=H)]
        for r in range(3):
            rp = r % 2
            mm = 0
            for fc in range(8):
                deps = [(sDVE, a_xT), wqkv_done,
                        (sACT, pbk0_free_act), (sACT, ev)] + list(extra)
                mm = emit("tensor", lambda E, fc=fc, r=r, b=b: E.e.matmul(
                    pbk[0][:, 0:512],
                    xT[:, fc, b * 128:(b + 1) * 128],
                    WQKV[:, fc, r * 1024 + 0:r * 1024 + 512],
                    start=(fc == 0), stop=(fc == 7)), deps=deps)
                mm = emit("tensor", lambda E, fc=fc, r=r, b=b: E.e.matmul(
                    pbk[0][:, 512:1024],
                    xT[:, fc, b * 128:(b + 1) * 128],
                    WQKV[:, fc, r * 1024 + 512:r * 1024 + 1024],
                    start=(fc == 0), stop=(fc == 7)), deps=deps)
            pq = qst.get(b - 1, {})
            war = (sts[0] if r >= 2 else
                   (pq.get("st_k", 0) if r == 1 else pq.get("st_v", 0)))
            for half in range(2):
                ev = emit("scalar", lambda E, rp=rp, half=half: E.e.activation(
                    stgQ[:, rp, half * 512:(half + 1) * 512],
                    pbk[0][:, half * 512:(half + 1) * 512], AF.Copy),
                    deps=[(sPE, mm), war])
            sts.append(dma("sync", ("qkst", r), dsts[r],
                           stgQ[:, rp, :].rearrange("p (i d) -> p i d", d=HS),
                           deps=[(sACT, ev)]))
        st_q, st_k, st_v = sts
        tr = dmat("QKT", QKT[:], qk_s[b],
                  deps=[st_k, st_q, (sPE, blk.get(b - 1, {}).get("s_last", 0))])
        vrb = dma("sync", "Vaug", Vaug[:, :, 0:64],
                  v_s[b].rearrange("(j p) d -> p j d", p=128),
                  deps=[st_v, (sPE, blk.get(b - 1, {}).get("av", 0)),
                        (sDVE, d_vone)])
        qst[b] = dict(ev=ev, st_v=st_v, st_k=st_k, tr=tr, vrb=vrb)

    emit_qkv(0, 0, extra=[(sPE, b_pe), (sACT, b_act), (sDVE, ar2_bc),
                          (sDVE, b_dve)])

    for b in range(C_BLOCKS):
        pb_ = blk.get(b - 1, {})
        if b >= 1:
            for kf in range(3 * (b - 1), 3 * b):
                f2_compute(kf)
        for kf in range(3 * b, 3 * b + 3):
            f2_load(kf)
        tr = qst[b]["tr"]
        vrb = qst[b]["vrb"]
        exp_done = {}
        av_ring = {}
        npc = 0
        av = 0
        s_last = 0
        for j in range(H):
            lo = j * 128
            bnds = sorted(set([lo] + list(range(512 * (lo // 512 + 1), 2048, 512)) + [2048]))
            for (p0, p1) in zip(bnds[:-1], bnds[1:]):
                pi = npc % 2
                sdeps = [tr, (sACT, exp_done.get(pi, 0)),
                         (sDVE, pb_.get("evb", 0))]
                if b == 0 and npc < 2:
                    sdeps += [(sDVE, ar2_bc), (sDVE, b_dve)]
                smm = emit("tensor", lambda E, j=j, p0=p0, p1=p1, pi=pi:
                    E.e.matmul(psS_slot[pi][:, 0:p1 - p0],
                               QKT[0:64, 2048 + j * 128:2048 + (j + 1) * 128],
                               QKT[0:64, p0:p1], start=True, stop=True),
                    deps=sdeps)
                s_last = smm
                dv = 0
                if p0 == lo:
                    dv = emit("vector", lambda E, pi=pi: E.e.tensor_tensor(
                        psS_slot[pi][:, 0:128], psS_slot[pi][:, 0:128],
                        mtri[:], ALU.add), deps=[(sPE, smm), (sDVE, d_mtri)])
                edeps = [(sPE, smm), (sPE, av_ring.get(pi, 0))]
                if dv:
                    edeps.append((sDVE, dv))
                ex = emit("scalar", lambda E, pi=pi, p0=p0, p1=p1:
                    E.e.activation(PT[:, pi, 0:p1 - p0],
                                   psS_slot[pi][:, 0:p1 - p0], AF.Exp,
                                   scale=0.125), deps=edeps)
                exp_done[pi] = ex
                avdeps = [(sACT, ex), vrb]
                if j == 0:
                    avdeps += [(sDVE, pb_.get("otn", 0)),
                               (sDVE, pb_.get("lc", 0))]
                av = emit("tensor", lambda E, j=j, p0=p0, p1=p1, pi=pi:
                    E.e.matmul(pO_slot(p0, p1), Vaug[:, j, :],
                               PT[:, pi, 0:p1 - p0],
                               start=(j == 0), stop=(j == H - 1)), deps=avdeps)
                av_ring[pi] = av
                npc += 1
        av_done = av
        lc = 0
        for half in range(2):
            lc = emit("vector", lambda E, half=half: E.e.tensor_copy(
                scrB[64:65, half * 1024:(half + 1) * 1024],
                pbk[1 + half][64:65, 0:1024]),
                deps=[(sPE, av_done), pb_.get("l_st", 0),
                      (sPE, pb_.get("bcast", 0))])
        l_st = dma("sync", "l_s", l_s[b, :].unsqueeze(0), scrB[64:65, :],
                   deps=[(sDVE, lc)])
        l_ld = dma("sync", "lcol", lcol[:],
                   l_s[b, :].rearrange("(p k) -> p k", p=128),
                   deps=[l_st, (sDVE, pb_.get("rc", 0))])
        rc = emit("vector", lambda E: E.e.reciprocal(rcol[:], lcol[:]),
                  deps=[l_ld])
        rl_st = dma("gpsimd", "rl_s",
                    rl_s[b, :].rearrange("(p k) -> p k", p=128), rcol[:],
                    deps=[(sDVE, rc)])
        rl_ld = dma("sync", "rlrow", scrB[0:1, :], rl_s[b, :].unsqueeze(0),
                    deps=[rl_st, l_st, (sDVE, pb_.get("otn", 0))])
        if b + 1 < C_BLOCKS:
            emit_qkv(b + 1, pb_.get("af_ev", 0))
        otn = 0
        bca = 0
        last_evb = 0
        evb_of = {}
        for p in range(4):
            pi = p % 2
            bb = emit("tensor", lambda E, p=p, pi=pi: E.e.matmul(
                psS_slot[pi][0:64, 0:512], onesC16[0:1, 0:64],
                scrB[0:1, p * 512:(p + 1) * 512], start=True, stop=True),
                deps=[rl_ld, (sACT, exp_done.get(pi, 0)),
                      (sDVE, evb_of.get(p - 2, 0)), (sDVE, d_ones)])
            bca = bb
            evb = emit("vector", lambda E, p=p, pi=pi: E.e.tensor_copy(
                scrB[0:64, p * 512:(p + 1) * 512], psS_slot[pi][0:64, 0:512]),
                deps=[(sPE, bb), (sDVE, pb_.get("otn", 0))])
            evb_of[p] = evb
            last_evb = evb
            otn = emit("vector", lambda E, p=p: E.e.tensor_tensor(
                OTn[:, p * 512:(p + 1) * 512], pOn_slot(p),
                scrB[0:64, p * 512:(p + 1) * 512], ALU.mult),
                deps=[(sDVE, evb), (sPE, av_done), (sPE, pb_.get("proj", 0))])
        pm = 0
        for p in range(2):
            pm = emit("tensor", lambda E, p=p: E.e.matmul(
                pbk[0][:, p * 512:(p + 1) * 512], onesC[0:1, :],
                pbias[0:1, p * 512:(p + 1) * 512], start=True, stop=False),
                deps=[d_pb, (sDVE, otn), (sDVE, d_ones),
                      (sACT, qst.get(b + 1, {}).get("ev", 0)),
                      pw_tr_all])
        for i in range(H):
            for p in range(2):
                pm = emit("tensor", lambda E, i=i, p=p: E.e.matmul(
                    pbk[0][:, p * 512:(p + 1) * 512],
                    OTn[:].rearrange("d (t i) -> d i t", i=H)[:, i, :],
                    pwdmp[0:64, i, :, :].rearrange("d a b -> d (a b)")[:, p * 512:(p + 1) * 512],
                    start=False, stop=(i == H - 1)), deps=[(sDVE, otn)])
        prj = pm
        af_ev = emit("scalar", lambda E: E.e.activation(
            stgAF[:], pbk[0][:, :], AF.Copy),
            deps=[(sPE, prj), pb_.get("af_st", 0)])
        af_st = dma("sync", "stgAF", af_s[b], stgAF[:], deps=[(sACT, af_ev)])

        blk[b] = dict(av=av_done, otn=otn, proj=prj, af_ev=af_ev,
                      af_st=af_st, rc=rc, lc=lc, l_st=l_st, bcast=bca,
                      evb=last_evb, s_last=s_last)
    for kf in range(3 * (C_BLOCKS - 1), len(f2_list)):
        f2_compute(kf)
    c_pe = sPE.n
    c_dve = sDVE.n
    c_act = sACT.n
    c_pool = sPOOL.n

    ar3_tr, ar3_ld = allreduce(2, [(sDVE, c_dve), (sPE, c_pe),
                                   (sACT, c_act), (sPOOL, c_pool)])
    v = emit("vector", lambda E: E.e.tensor_tensor(
        ss[0:1, 15:16], cc_sb[0:1, 4:5], ss[0:1, 13:14], ALU.mult),
        deps=[ar3_ld])
    v = emit("vector", lambda E: E.e.tensor_scalar(
        ss[0:1, 15:16], ss[0:1, 15:16], 1.0 / QMAX, 1e-12, ALU.mult, ALU.max),
        deps=[(sDVE, v)])
    v = emit("vector", lambda E: E.e.reciprocal(sc8[0:1, 0:1], ss[0:1, 15:16]),
             deps=[(sDVE, v)])
    ar3_bc = bcast_cols([(sc8[0:1, 0:1], scb2[:, 0:1]),
                         (ss[0:1, 15:16], scb2[:, 1:2])], [(sDVE, v)])

    if LIMIT < 4:
        return _finish(nc, prog, scopes, ctx, dsems, Eng, cms)
    scopes["C"].close()
    cms["xT"].__exit__(None, None, None)
    cms["WQKV"].__exit__(None, None, None)
    WdT = sb("WdT", [128, NIC, 1024], F16, "D")
    oq = sb("oq", [128, 2, NIC, 1024], F16, "D")
    o_ld = sb("o_ld", [128, 2, 1024], F16, "D")
    o_i32 = sb("o_i32", [128, 2, 1024], I32, "D")
    outstg = sb("outstg", [128, 2, D], F32, "D")
    xres = sb("xres", [128, 2, D], F32, "D")
    afres = sb("afres", [128, 2, D], F16, "D")

    wdt_tr = 0
    for ic in range(NIC):
        wdt_tr = dma("sync", "WdT", WdT[:, ic, :], wfm_d[ic],
                     deps=[wd_fst[(7, 0)], wd_fst[(7, 1)],
                           (sPE, c_pe), (sACT, c_act), (sDVE, c_dve),
                           (sPOOL, c_pool)])
    wdt_all = wdt_tr

    oq_done = {}
    dn_done = {}
    res_done = {}
    out_dma = {}
    oq_cast = [0, 0]
    oq_sc = [0, 0]
    for g in range(4):
        gg = g % 2
        oqv = 0
        for ic in range(NIC):
            pg2 = ic % 2
            dl = dma("gpsimd", ("old", pg2), o_ld[:, pg2, :],
                     o_s[ic, :, g * 1024:(g + 1) * 1024],
                     deps=[f2_store[(ic, g)],
                           (sACT, oq_cast[pg2]), (sDVE, ar3_bc)])
            c = emit("scalar", lambda E, pg2=pg2: E.e.activation(
                o_i32[:, pg2, :], o_ld[:, pg2, :], AF.Identity, scale=scb2[:, 0:1]),
                deps=[dl, (sDVE, ar3_bc), (sDVE, oq_sc[pg2])])
            oq_cast[pg2] = c
            svv = emit("vector", lambda E, ic=ic, gg=gg, pg2=pg2: E.e.tensor_scalar(
                oq[:, gg, ic, :], o_i32[:, pg2, :], scb2[:, 1:2], None, ALU.mult),
                deps=[(sACT, c), (sPE, dn_done.get((g - 2, 7), 0))])
            oq_sc[pg2] = svv
            oqv = svv
        oq_done[g] = oqv
        for tt in range(8):
            t_abs = g * 8 + tt
            pgd = tt % 2
            mm = 0
            war = res_done.get((g, tt - 2), 0) if tt >= 2 else \
                res_done.get((g - 1, 6 + tt), 0)
            for ic in range(NIC):
                for p in range(2):
                    deps = [(sDVE, oq_done[g]), wdt_all,
                            (sDVE, war)]
                    if g == 0 and tt < 2:
                        deps += [(sACT, c_act), (sDVE, c_dve), (sPE, c_pe)]
                    mm = emit("tensor", lambda E, ic=ic, p=p, tt=tt, gg=gg, pgd=pgd:
                        E.e.matmul(pbk[pgd][:, p * 512:(p + 1) * 512],
                                   oq[:, gg, ic, tt * 128:(tt + 1) * 128],
                                   WdT[:, ic, p * 512:(p + 1) * 512],
                                   start=(ic == 0), stop=(ic == NIC - 1)),
                        deps=deps)
            dn_done[(g, tt)] = mm
            xl = dma("sync", ("xres", pgd), xres[:, pgd, :],
                     x_d[t_abs * 128:(t_abs + 1) * 128, :],
                     deps=[(sDVE, war), (sDVE, ar3_bc)])
            al = dma("sync", ("afres", pgd), afres[:, pgd, :], af_s[t_abs],
                     deps=[blk[t_abs]["af_st"], (sDVE, war),
                           (sDVE, ar3_bc)])
            r1 = emit("vector", lambda E, pgd=pgd: E.e.tensor_tensor(
                outstg[:, pgd, :], pbk[pgd][:, :], afres[:, pgd, :], ALU.add),
                deps=[(sPE, mm), al, out_dma.get((g, tt - 2),
                      out_dma.get((g - 1, 6 + tt), 0))])
            r2 = emit("vector", lambda E, pgd=pgd: E.e.tensor_tensor(
                outstg[:, pgd, :], outstg[:, pgd, :], xres[:, pgd, :], ALU.add),
                deps=[(sDVE, r1), xl])
            res_done[(g, tt)] = r2
            out_dma[(g, tt)] = dma("sync", ("out", pgd),
                out_d[t_abs * 128:(t_abs + 1) * 128, :],
                outstg[:, pgd, :], deps=[(sDVE, r2)])

    scopes["D"].close()
    return _finish(nc, prog, scopes, ctx, dsems, Eng, cms)


def _finish(nc, prog, scopes, ctx, dsems, Eng, cms):
    for sc in scopes.values():
        try:
            sc.close()
        except Exception:
            pass
    for nm in ("xT", "WQKV"):
        try:
            cms[nm].__exit__(None, None, None)
        except Exception:
            pass
    finals = [(ct, ct.n) for ct in dsems.values() if ct.n > 0]

    with nc.Block() as block:
        @block.sync
        def _(sync):
            E = Eng(sync)
            for fn in prog["sync"]:
                fn(E)
            for ct, v in finals:
                sync.wait_ge(ct.h, v)

        @block.tensor
        def _(tensor):
            E = Eng(tensor)
            for fn in prog["tensor"]:
                fn(E)

        @block.scalar
        def _(scalar):
            E = Eng(scalar)
            for fn in prog["scalar"]:
                fn(E)

        @block.vector
        def _(vector):
            E = Eng(vector)
            for fn in prog["vector"]:
                fn(E)

        @block.gpsimd
        def _(gpsimd):
            E = Eng(gpsimd)
            for fn in prog["gpsimd"]:
                fn(E)

    ctx.close()
    return nc


_CACHE = {}


def kernel(L0_attn_x, L0_attn_Wq, L0_attn_Wk, L0_attn_Wv, L0_attn_proj_w,
           L0_attn_proj_b, L0_ffn_w_gate, L0_ffn_w_up, L0_ffn_w_down):
    if "nc" not in _CACHE:
        _CACHE["nc"] = build_program()
    nc = _CACHE["nc"]
    x = np.ascontiguousarray(np.asarray(L0_attn_x, dtype=np.float32))
    shared = dict(
        wq=np.ascontiguousarray(np.asarray(L0_attn_Wq, dtype=np.float32)),
        wk=np.ascontiguousarray(np.asarray(L0_attn_Wk, dtype=np.float32)),
        wv=np.ascontiguousarray(np.asarray(L0_attn_Wv, dtype=np.float32)),
        pw=np.ascontiguousarray(np.asarray(L0_attn_proj_w, dtype=np.float32)),
        pb=np.ascontiguousarray(np.asarray(L0_attn_proj_b, dtype=np.float32)).reshape(1, D),
        wg=np.ascontiguousarray(np.asarray(L0_ffn_w_gate, dtype=np.float32)),
        wu=np.ascontiguousarray(np.asarray(L0_ffn_w_up, dtype=np.float32)),
        wd=np.ascontiguousarray(np.asarray(L0_ffn_w_down, dtype=np.float32)),
    )
    in_maps = []
    for k in range(NCORES):
        m = dict(shared)
        m["x"] = np.ascontiguousarray(x[2 * k:2 * k + 2].reshape(TCORE, D))
        in_maps.append(m)
    res = run_bass_kernel_spmd(nc, in_maps, core_ids=list(range(NCORES)))
    parts = [np.asarray(r["out"]).reshape(2, 2048, D) for r in res.results]
    return np.concatenate(parts, axis=0)


# version 15
# speedup vs baseline: 1.4787x; 1.1826x over previous
from contextlib import ExitStack

import numpy as np
import concourse.bass as bass
import concourse.mybir as mybir
from concourse.bass_utils import run_bass_kernel_spmd

F32 = mybir.dt.float32
F16 = mybir.dt.float16
I32 = mybir.dt.int32
AF = mybir.ActivationFunctionType
ALU = mybir.AluOpType
AX = mybir.AxisListType

NCORES = 8
LIMIT = 99
C_BLOCKS = 32
C_STEP = 9
NB = 32
TCORE = NB * 128
D = 1024
I3 = 3072
NIC = 24
H, HS = 16, 64
QMAX = 32767.0
WQMAX = 7.0
EPS = 1e-5
NEG = -1.0e30


class Ct:
    def __init__(self, ctx, nc, name):
        self.h = ctx.enter_context(nc.semaphore(name))
        self.n = 0


class Eng:
    def __init__(self, e):
        self.e = e
        self.waited = {}

    def wait(self, deps):
        for ct, val in deps:
            if val is None or val <= 0:
                continue
            if self.waited.get(id(ct), -1) >= val:
                continue
            self.e.wait_ge(ct.h, val)
            self.waited[id(ct)] = val


def build_program():
    nc = bass.Bass("TRN2", target_bir_lowering=False, debug=False)

    x_d = nc.dram_tensor("x", [TCORE, D], F32, kind="ExternalInput").ap()
    wq_d = nc.dram_tensor("wq", [H, D, HS], F32, kind="ExternalInput").ap()
    wk_d = nc.dram_tensor("wk", [H, D, HS], F32, kind="ExternalInput").ap()
    wv_d = nc.dram_tensor("wv", [H, D, HS], F32, kind="ExternalInput").ap()
    pw_d = nc.dram_tensor("pw", [D, D], F32, kind="ExternalInput").ap()
    pb_d = nc.dram_tensor("pb", [1, D], F32, kind="ExternalInput").ap()
    wg_d = nc.dram_tensor("wg", [I3, D], F32, kind="ExternalInput").ap()
    wu_d = nc.dram_tensor("wu", [I3, D], F32, kind="ExternalInput").ap()
    wd_d = nc.dram_tensor("wd", [D, I3], F32, kind="ExternalInput").ap()
    out_d = nc.dram_tensor("out", [TCORE, D], F32, kind="ExternalOutput").ap()

    qk_s = nc.dram_tensor("qk_s", [NB, 4096, 128], F16).ap()
    v_s = nc.dram_tensor("v_s", [NB, 2048, 64], F16).ap()
    af_s = nc.dram_tensor("af_s", [NB, 128, D], F16).ap()
    up_s = nc.dram_tensor("up_s", [NIC, 128, TCORE], F16).ap()
    gt_s = nc.dram_tensor("gt_s", [NIC, 128, TCORE], F16).ap()
    o_s = nc.dram_tensor("o_s", [NIC, 128, TCORE], F16).ap()
    wdq_u = nc.dram_tensor("wdq_u", [NIC, 128, D], F16).ap()
    wdq_g = nc.dram_tensor("wdq_g", [NIC, 128, D], F16).ap()
    wdq_dd = nc.dram_tensor("wdq_dd", [8, 128, I3], F16).ap()
    wfm_u = nc.dram_tensor("wfm_u", [NIC, 128, 1024], F16).ap()
    wfm_g = nc.dram_tensor("wfm_g", [NIC, 128, 1024], F16).ap()
    wfm_d = nc.dram_tensor("wfm_d", [NIC, 128, 1024], F16).ap()
    pw16 = nc.dram_tensor("pw16", [8, 128, H, 128], F16).ap()
    l_s = nc.dram_tensor("l_s", [NB, 2048], F16).ap()
    rl_s = nc.dram_tensor("rl_s", [NB, 2048], F16).ap()
    cc_i = [nc.dram_tensor(f"cc{i}i", [1, 8], F32).ap() for i in range(3)]
    cc_o = [nc.dram_tensor(f"cc{i}o", [1, 8], F32).ap() for i in range(3)]

    ctx = ExitStack()
    scopes = {"A": ExitStack(), "B": ExitStack(), "C": ExitStack(),
              "D": ExitStack()}
    cms = {}

    def sb(name, shape, dtype, scope=None):
        if scope is None:
            cm = nc.sbuf_tensor(name, shape, dtype)
            t = ctx.enter_context(cm)
        elif scope == "manual":
            cm = nc.sbuf_tensor(name, shape, dtype)
            t = cm.__enter__()
        else:
            cm = nc.sbuf_tensor(name, shape, dtype, side="right")
            t = scopes[scope].enter_context(cm)
        cms[name] = cm
        return t

    idn32 = sb("idn32", [128, 128], F32)
    mtri = sb("mtri", [128, 128], F32)
    onesC = sb("onesC", [128, 128], F32)
    onesC16 = sb("onesC16", [128, 128], F16)
    stats = sb("stats", [128, 8, 10], F32)
    rmax = sb("rmax", [128, 8], F32)
    red8 = sb("red8", [8, 1], F32)
    sc8 = sb("sc8", [1, 16], F32)
    ss = sb("ss", [1, 16], F32)
    ssi = sb("ssi", [1, 2], I32)
    scb = sb("scb", [128, 8], F32)
    scb2 = sb("scb2", [128, 2], F32)
    cc_sb = sb("cc_sb", [1, 8], F32)
    tmpB = sb("tmpB", [128, 2], F32)
    tmpF = sb("tmpF", [128, 2], F32)
    WQKV = sb("WQKV", [128, 8, 3072], F16, "manual")
    xT = sb("xT", [128, 8, TCORE], F16, "manual")


    pbk = [ctx.enter_context(nc.psum_tensor(f"pbk{i}", [128, 1024], F32))
           for i in range(4)]

    sPE = Ct(ctx, nc, "sPE")
    sACT = Ct(ctx, nc, "sACT")
    sDVE = Ct(ctx, nc, "sDVE")
    sPOOL = Ct(ctx, nc, "sPOOL")
    sDMA = Ct(ctx, nc, "sDMA")
    sGD = Ct(ctx, nc, "sGD")
    sCC = Ct(ctx, nc, "sCC")

    prog = {k: [] for k in ("sync", "tensor", "scalar", "vector", "gpsimd")}
    ENG = {"tensor": sPE, "scalar": sACT, "vector": sDVE, "gpsimd": sPOOL}

    def emit(engine, builder, deps=()):
        sem = ENG[engine]
        sem.n += 1
        deps = [d for d in deps if d is not None and d != 0]
        def fn(E, builder=builder, deps=deps, sem=sem):
            E.wait(deps)
            ins = builder(E)
            ins.then_inc(sem.h, 1)
        prog[engine].append(fn)
        return sem.n

    dsems = {}

    def dsem(key):
        if key not in dsems:
            dsems[key] = Ct(ctx, nc, "dk_%s" % "_".join(str(x) for x in (
                key if isinstance(key, tuple) else (key,))))
        return dsems[key]

    def dma(engine, key, out, in_, deps=(), **kw):
        sem = dsem(key)
        sem.n += 16
        deps = [d for d in deps if d is not None and d != 0]
        def fn(E, out=out, in_=in_, deps=deps, kw=kw, sem=sem):
            E.wait(deps)
            ins = E.e.dma_start(out=out, in_=in_, **kw)
            ins.then_inc(sem.h, 16)
        prog[engine].append(fn)
        return (sem, sem.n)

    def dmat(key, out, in_, deps=()):
        sem = dsem(key)
        sem.n += 16
        deps = [d for d in deps if d is not None and d != 0]
        def fn(E, out=out, in_=in_, deps=deps, sem=sem):
            E.wait(deps)
            ins = E.e.dma_start_transpose(out, in_)
            ins.then_inc(sem.h, 16)
        prog["sync"].append(fn)
        return (sem, sem.n)

    def collective(cci, cco, deps):
        sCC.n += 1
        deps = [d for d in deps if d is not None]
        def fn(E, deps=deps):
            E.wait(deps)
            ins = E.e.collective_compute(
                "AllReduce", ALU.max, replica_groups=[list(range(NCORES))],
                ins=[cci.opt()], outs=[cco.opt()])
            ins.then_inc(sCC.h, 1)
        prog["gpsimd"].append(fn)
        return sCC.n

    d_iota = emit("gpsimd", lambda E: E.e.iota(
        iotaf[:], pattern=[[1, 128]], base=0, channel_multiplier=-1,
        allow_small_or_imprecise_dtypes=True))
    d_idn32 = emit("vector", lambda E: E.e.tensor_scalar(
        idn32[:], iotaf[:], 0.0, None, ALU.is_equal), deps=[(sPOOL, d_iota)])
    d_m01 = emit("vector", lambda E: E.e.tensor_scalar(
        mtri[:], iotaf[:], 0.0, None, ALU.is_ge), deps=[(sPOOL, d_iota)])
    d_mtri = emit("vector", lambda E: E.e.tensor_scalar(
        mtri[:], mtri[:], 1.0, -NEG, ALU.subtract, ALU.mult),
        deps=[(sDVE, d_m01)])
    d_ones = emit("vector", lambda E: E.e.memset(onesC[:], 1.0))
    d_ones = emit("vector", lambda E: E.e.memset(onesC16[:], 1.0),
                  deps=[(sDVE, d_ones)])
    d_rm0 = emit("vector", lambda E: E.e.memset(rmax[:], 0.0))
    d_st0 = emit("vector", lambda E: E.e.memset(stats[:], 1.0))
    d_rm1 = emit("vector", lambda E: E.e.tensor_scalar(
        rmax[:, 3:4], rmax[:, 3:4], 0.0, NEG, ALU.mult, ALU.add),
        deps=[(sDVE, d_rm0)])

    iotaf = sb("iotaf", [128, 128], F32, "A")
    xa = sb("xa", [128, 4, D], F32, "A")
    xn32 = sb("xn32", [128, 2, D], F32, "A")
    sqdmp = sb("sqdmp", [128, D], F32, "A")
    pw_f32 = sb("pw_f32", [128, 2, D], F32, "A")
    pw_cast = sb("pw_cast", [128, 2, D], F16, "A")
    WDH = 1536
    wch = sb("wch", [128, 2, WDH], F32, "A")
    dqmax = sb("dqmax", [128, 2, 48], F32, "A")
    dqs = sb("dqs", [128, 2, 48], F32, "A")
    dqsr = sb("dqsr", [128, 2, 48], F32, "A")
    dqz = sb("dqz", [128, 2, WDH], F32, "A")
    dqi = sb("dqi", [128, 2, WDH], I32, "A")
    dq16 = sb("dq16", [128, 2, WDH], F16, "A")
    ftA = sb("ftA", [128, 2, 12, 128], F16, "A")



    wqkv_done = 0
    for fc in range(8):
        for wsrc, base in ((wq_d, 0), (wk_d, 1024), (wv_d, 2048)):
            src = wsrc[:, fc * 128:(fc + 1) * 128, :].rearrange("i f d -> f i d")
            dst = WQKV[:, fc, base:base + 1024].rearrange("p (i d) -> p i d", d=HS)
            wqkv_done = dma("gpsimd", "wqkv", dst, src)

    pw_st = [0] * 8
    pw_cv = [0] * 8
    for ec in range(8):
        pg = ec % 2
        dl = dma("sync", ("pwf", pg), pw_f32[:, pg, :], pw_d[ec * 128:(ec + 1) * 128, :],
                 deps=[(sACT, pw_cv[ec - 2]) if ec >= 2 else None])
        cv = emit("scalar", lambda E, pg=pg: E.e.activation(
            pw_cast[:, pg, :], pw_f32[:, pg, :], AF.Copy),
            deps=[dl, pw_st[ec - 2] if ec >= 2 else None])
        pw_cv[ec] = cv
        pw_st[ec] = dma("sync", ("pw16", pg), pw16[ec, :, :, 0:64],
                        pw_cast[:, pg, :].rearrange("p (i d) -> p i d", d=64),
                        deps=[(sACT, cv)])

    trp = pbk[3]
    last_tr_copy = 0
    last_sq = 0
    last_xn = [0, 0]
    xa_free = [0] * 4
    amx_of = [0] * 4
    a_mx = d_rm1
    for g in range(8):
        xa_load = [0] * 4
        for j in range(4):
            b = g * 4 + j
            xa_load[j] = dma("sync", ("xa", j), xa[:, j, :], x_d[b * 128:(b + 1) * 128, :],
                             deps=[(sACT, xa_free[j]) if g > 0 else None])
        for j in range(4):
            last_sq = emit("scalar", lambda E, j=j: E.e.activation(
                sqdmp[:], xa[:, j, :], AF.Square, accum_out=stats[:, j, 0:1]),
                deps=[xa_load[j], (sACT, last_sq), (sDVE, amx_of[j])])
            emit("vector", lambda E, j=j: E.e.reduce_sum(
                stats[:, j, 1:2], xa[:, j, :], AX.X), deps=[xa_load[j]])
        vv = sDVE.n
        vv = emit("vector", lambda E: E.e.tensor_scalar(
            stats[:, :, 2:3].opt(), stats[:, :, 1:2].opt(), 1.0 / D, None,
            ALU.mult), deps=[(sDVE, vv)])
        vv = emit("vector", lambda E: E.e.tensor_scalar(
            stats[:, :, 3:4].opt(), stats[:, :, 0:1].opt(), 1.0 / D, None,
            ALU.mult), deps=[(sACT, last_sq), (sDVE, vv)])
        vv = emit("vector", lambda E: E.e.tensor_tensor(
            stats[:, :, 4:5].opt(), stats[:, :, 2:3].opt(),
            stats[:, :, 2:3].opt(), ALU.mult), deps=[(sDVE, vv)])
        vv = emit("vector", lambda E: E.e.tensor_tensor(
            stats[:, :, 5:6].opt(), stats[:, :, 3:4].opt(),
            stats[:, :, 4:5].opt(), ALU.subtract), deps=[(sDVE, vv)])
        vv = emit("vector", lambda E: E.e.tensor_scalar(
            stats[:, :, 5:6].opt(), stats[:, :, 5:6].opt(), 1.0, EPS, ALU.mult,
            ALU.add), deps=[(sDVE, vv)])
        vv = emit("vector", lambda E: E.e.memset(stats[:, :, 6:7].opt(), 1.0),
                  deps=[(sDVE, vv)])
        for it in range(7):
            vv = emit("vector", lambda E: E.e.tensor_tensor(
                stats[:, :, 7:8].opt(), stats[:, :, 6:7].opt(),
                stats[:, :, 6:7].opt(), ALU.mult), deps=[(sDVE, vv)])
            vv = emit("vector", lambda E: E.e.tensor_tensor(
                stats[:, :, 8:9].opt(), stats[:, :, 7:8].opt(),
                stats[:, :, 5:6].opt(), ALU.mult), deps=[(sDVE, vv)])
            vv = emit("vector", lambda E: E.e.tensor_scalar(
                stats[:, :, 8:9].opt(), stats[:, :, 8:9].opt(), -0.5, 1.5,
                ALU.mult, ALU.add), deps=[(sDVE, vv)])
            vv = emit("vector", lambda E: E.e.tensor_tensor(
                stats[:, :, 6:7].opt(), stats[:, :, 6:7].opt(),
                stats[:, :, 8:9].opt(), ALU.mult), deps=[(sDVE, vv)])
        vv = emit("vector", lambda E: E.e.scalar_tensor_tensor(
            stats[:, :, 9:10].opt(), stats[:, :, 2:3].opt(), -1.0,
            stats[:, :, 6:7].opt(), ALU.mult, ALU.mult), deps=[(sDVE, vv)])
        grp = vv
        for j in range(4):
            b = g * 4 + j
            pg = j % 2
            d_xn = emit("scalar", lambda E, j=j, pg=pg: E.e.activation(
                xn32[:, pg, :], xa[:, j, :], AF.Identity, bias=stats[:, j, 9:10],
                scale=stats[:, j, 6:7]),
                deps=[(sDVE, grp), (sDVE, last_xn[pg])])
            xa_free[j] = d_xn
            d_mx = emit("vector", lambda E, j=j, pg=pg: E.e.tensor_reduce(
                stats[:, j, 0:1], xn32[:, pg, :], AX.X, ALU.max,
                apply_absolute_value=True), deps=[(sACT, d_xn)])
            a_mx = emit("vector", lambda E, j=j: E.e.tensor_tensor(
                rmax[:, 0:1], rmax[:, 0:1], stats[:, j, 0:1], ALU.max),
                deps=[(sDVE, d_mx), (sDVE, a_mx)])
            amx_of[j] = a_mx
            trs = []
            for fc in range(8):
                tr = emit("tensor", lambda E, fc=fc, pg=pg: E.e.transpose(
                    trp[:, fc * 128:(fc + 1) * 128],
                    xn32[:, pg, fc * 128:(fc + 1) * 128], idn32[:]),
                    deps=[(sACT, d_xn), (sDVE, d_idn32), (sDVE, last_tr_copy)])
                trs.append(tr)
            for fc in range(8):
                last_tr_copy = emit("vector",
                    lambda E, fc=fc, b=b: E.e.tensor_copy(
                        xT[:, fc, b * 128:(b + 1) * 128],
                        trp[:, fc * 128:(fc + 1) * 128]),
                    deps=[(sPE, trs[fc])])
            last_xn[pg] = last_tr_copy
    a_xT = last_tr_copy

    dq_state = {"z": 0, "c": 0, "w": 0, "st": [0, 0], "ft": [0, 0], "fs": [0, 0]}

    def dq_chunk(src, width, wdq_row, wfm_tiles, k):
        pg = k % 2
        ng = width // 32
        nt = width // 128
        st_ = dq_state
        dl = dma("gpsimd", "wch", wch[:, pg, 0:width], src,
                 deps=[(sDVE, st_["z"])])
        r = emit("vector", lambda E: E.e.tensor_reduce(
            dqmax[:, pg, 0:ng], wch[:, pg, 0:width].rearrange(
                "p (g e) -> p g e", e=32),
            AX.X, ALU.max, apply_absolute_value=True), deps=[dl])
        r = emit("vector", lambda E: E.e.tensor_scalar(
            dqs[:, pg, 0:ng], dqmax[:, pg, 0:ng], 1.0 / WQMAX, 1e-12,
            ALU.mult, ALU.max), deps=[(sDVE, r)])
        r = emit("vector", lambda E: E.e.reciprocal(
            dqsr[:, pg, 0:ng], dqs[:, pg, 0:ng]), deps=[(sDVE, r)])
        z = emit("vector", lambda E: E.e.tensor_tensor(
            dqz[:, pg, 0:width].rearrange("p (g e) -> p g e", e=32),
            wch[:, pg, 0:width].rearrange("p (g e) -> p g e", e=32),
            dqsr[:, pg, 0:ng].unsqueeze(2).broadcast_to([128, ng, 32]),
            ALU.mult), deps=[(sDVE, r), (sACT, st_["c"])])
        st_["z"] = z
        c = emit("scalar", lambda E: E.e.activation(
            dqi[:, pg, 0:width], dqz[:, pg, 0:width], AF.Copy),
            deps=[(sDVE, z), (sDVE, st_["w"])])
        st_["c"] = c
        w = emit("vector", lambda E: E.e.tensor_tensor(
            dq16[:, pg, 0:width].rearrange("p (g e) -> p g e", e=32),
            dqi[:, pg, 0:width].rearrange("p (g e) -> p g e", e=32),
            dqs[:, pg, 0:ng].unsqueeze(2).broadcast_to([128, ng, 32]),
            ALU.mult), deps=[(sACT, c), st_["st"][pg]])
        st_["w"] = w
        stq = dma("gpsimd", ("wdq", pg), wdq_row, dq16[:, pg, 0:width],
                  deps=[(sDVE, w)])
        st_["st"][pg] = stq
        trs = []
        for t in range(nt):
            trs.append(dmat(("ftA", pg), ftA[:, pg, t, :],
                            wdq_row_tile(wdq_row, t),
                            deps=[stq, st_["fs"][pg]]))
        fst = dma("sync", ("wfm", pg), wfm_tiles,
                  ftA[:, pg, 0:nt, :], deps=[trs[-1]])
        st_["ft"][pg] = trs[-1]
        st_["fs"][pg] = fst
        return fst

    def wdq_row_tile(row_ap, t):
        return row_ap[:, t * 128:(t + 1) * 128]

    wfm_store = {}
    _k = 0
    for ic in range(NIC):
        for wt, wsrc, ddst, fdst in (("u", wu_d, wdq_u, wfm_u),
                                     ("g", wg_d, wdq_g, wfm_g)):
            wfm_store[(wt, ic)] = dq_chunk(
                wsrc[ic * 128:(ic + 1) * 128, :], 1024,
                ddst[ic, :, :],
                fdst[ic].rearrange("p (t e) -> p t e", e=128), _k)
            _k += 1
    wd_fst = {}
    for ec in range(8):
        for hv in range(2):
            wd_fst[(ec, hv)] = dq_chunk(
                wd_d[ec * 128:(ec + 1) * 128, hv * WDH:(hv + 1) * WDH], WDH,
                wdq_dd[ec, :, hv * WDH:(hv + 1) * WDH],
                wfm_d[hv * 12:(hv + 1) * 12, :,
                      ec * 128:(ec + 1) * 128].rearrange("i p e -> p i e"),
                _k)
            _k += 1

    def allreduce(idx, extra_deps):
        tr = emit("tensor", lambda E: E.e.transpose(
            pbk[3][0:8, 0:128], rmax[:, 0:8], idn32[:]),
            deps=list(extra_deps) + [(sDVE, d_idn32)])
        rd = emit("vector", lambda E: E.e.reduce_max(
            red8[:], pbk[3][0:8, 0:128], AX.X), deps=[(sPE, tr)])
        st = dma("gpsimd", "cci", cc_i[idx][0:1, 0:8].rearrange("a b -> b a"),
                 red8[:], deps=[(sDVE, rd)])
        ccv = collective(cc_i[idx], cc_o[idx], deps=[st])
        ld = dma("gpsimd", "ccsb", cc_sb[:], cc_o[idx], deps=[(sCC, ccv)])
        return tr, ld

    def bcast_cols(pairs, deps):
        last = 0
        for k, (src, dst) in enumerate(pairs):
            b1 = emit("tensor", lambda E, src=src: E.e.matmul(
                pbk[3][:, 512 + 16 * 0: 512 + 16 * 0 + 1] if False else
                pbk[3][:, 512:513], onesC[0:1, :], src.opt(),
                start=True, stop=True),
                deps=list(deps) + [(sDVE, d_ones), (sDVE, last)])
            last = emit("vector", lambda E, dst=dst: E.e.tensor_copy(
                dst, pbk[3][:, 512:513]), deps=[(sPE, b1)])
        return last

    ar1_tr, ar1_ld = allreduce(0, [(sDVE, a_mx), (sDVE, last_tr_copy)])
    v = emit("vector", lambda E: E.e.tensor_scalar(
        sc8[0:1, 8:9], cc_sb[0:1, 0:1], 1.0 / QMAX, 1e-12, ALU.mult, ALU.max),
        deps=[ar1_ld])
    v = emit("vector", lambda E: E.e.reciprocal(sc8[0:1, 9:10], sc8[0:1, 8:9]),
             deps=[(sDVE, v)])
    ar1_bc = bcast_cols([(sc8[0:1, 9:10], scb[:, 0:1]),
                         (sc8[0:1, 8:9], scb[:, 1:2])], [(sDVE, v)])

    scopes["A"].close()
    xqT = sb("xqT", [128, 8, TCORE], F16, "B")
    xq_i32 = sb("xq_i32", [128, 2, 1024], I32, "B")
    ftile = sb("ftile", [128, 2, 8, 128], F16, "B")
    stgF = sb("stgF", [128, 2, 2, 1024], F16, "B")


    if LIMIT < 1:
        return _finish(nc, prog, scopes, ctx, dsems, Eng, cms)
    xq_done = {}
    xq_prev = [0, 0]
    _kq = 0
    for q4 in range(4):
        for fc in range(8):
            pgq = _kq % 2
            _kq += 1
            c = emit("scalar", lambda E, fc=fc, q4=q4, pgq=pgq: E.e.activation(
                xq_i32[:, pgq, :], xT[:, fc, q4 * 1024:(q4 + 1) * 1024],
                AF.Identity, scale=scb[:, 0:1]),
                deps=[(sDVE, ar1_bc), (sDVE, a_xT), (sDVE, xq_prev[pgq])])
            xq_prev[pgq] = emit("vector",
                lambda E, fc=fc, q4=q4, pgq=pgq: E.e.tensor_scalar(
                    xqT[:, fc, q4 * 1024:(q4 + 1) * 1024], xq_i32[:, pgq, :],
                    scb[:, 1:2], None, ALU.mult), deps=[(sACT, c)])
            if q4 % 2 == 1:
                xq_done[(fc, q4 // 2)] = xq_prev[pgq]

    f1_seq = []
    for ic in range(NIC):
        f1_seq.append(("u", ic))
        f1_seq.append(("g", ic))

    if LIMIT < 2:
        return _finish(nc, prog, scopes, ctx, dsems, Eng, cms)
    def psF(wt, p):
        base = 0 if wt == "u" else 2
        t = pbk[base + p // 2]
        return t[:, (p % 2) * 512:(p % 2) * 512 + 512]

    f1_store = {}
    f1_evA = {}
    f1_evV = {}
    f1_pool = {}
    ftile_rd = [0, 0]
    stg_free = {}
    first_gate_pb3 = True
    seq_all = [(hh, wt, ic) for hh in range(2) for (wt, ic) in f1_seq]

    def emit_ftl(kk):
        hh, wt, ic = seq_all[kk]
        pg = kk % 2
        fsrc = wfm_u if wt == "u" else wfm_g
        return dma("sync", ("ftile", pg), ftile[:, pg, :, :],
                   fsrc[ic].rearrange("p (t e) -> p t e", e=128),
                   deps=[wfm_store[(wt, ic)], (sPE, ftile_rd[pg])])

    ftl_of = {0: emit_ftl(0)}
    for kk, (hh, wt, ic) in enumerate(seq_all):
        pg = kk % 2
        cpg = kk % 2
        sdram = up_s if wt == "u" else gt_s
        if kk + 1 < len(seq_all):
            ftl_of[kk + 1] = emit_ftl(kk + 1)
        ftl = ftl_of[kk]
        mm = 0
        for fc in range(8):
            for p in range(4):
                extra = []
                if wt == "g" and first_gate_pb3 and p >= 2:
                    extra.append((sDVE, ar1_bc))
                mm = emit("tensor", lambda E, fc=fc, p=p, pg=pg, wt=wt, hh=hh:
                    E.e.matmul(psF(wt, p), ftile[:, pg, fc, :],
                               xqT[:, fc, hh * 2048 + p * 512:hh * 2048 + (p + 1) * 512],
                               start=(fc == 0), stop=(fc == 7)),
                    deps=[ftl, (sDVE, xq_done[(fc, hh)]),
                          (sACT, f1_evA.get((wt, hh, ic - 1),
                                 f1_evA.get((wt, hh - 1, NIC - 1), 0))),
                          (sDVE, f1_evV.get((wt, hh, ic - 1),
                                 f1_evV.get((wt, hh - 1, NIC - 1), 0)))] + extra)
        if wt == "g":
            first_gate_pb3 = False
        ftile_rd[pg] = mm
        ev_a = ev_v = 0
        stv = 0
        col = 1 if wt == "u" else 2
        pr = 0
        for half in range(2):
            sdep = [stg_free.get((cpg, half), 0),
                    (sDVE, f1_pool.get((cpg, half), 0))]
            ev_a = emit("scalar", lambda E, half=half, wt=wt, cpg=cpg:
                E.e.activation(stgF[:, cpg, half, 0:512], psF(wt, 2 * half),
                               AF.Copy), deps=[(sPE, mm)] + sdep)
            ev_v = emit("vector", lambda E, half=half, wt=wt, cpg=cpg:
                E.e.tensor_copy(stgF[:, cpg, half, 512:1024],
                                psF(wt, 2 * half + 1)),
                deps=[(sPE, mm)] + sdep)
            stv = dma("sync", ("f1st", cpg, half),
                      sdram[ic, :, hh * 2048 + half * 1024:
                            hh * 2048 + (half + 1) * 1024],
                      stgF[:, cpg, half, :],
                      deps=[(sACT, ev_a), (sDVE, ev_v)])
            stg_free[(cpg, half)] = stv
            f1_store[(wt, hh, ic, half)] = stv
            pr = emit("vector", lambda E, half=half, cpg=cpg: E.e.tensor_reduce(
                tmpB[:, 0:1], stgF[:, cpg, half, :], AX.X, ALU.max,
                apply_absolute_value=True),
                deps=[(sACT, ev_a), (sDVE, ev_v), (sPE, ar1_tr),
                      (sDVE, pr)])
            pr = emit("vector", lambda E, col=col: E.e.tensor_tensor(
                rmax[:, col:col + 1], rmax[:, col:col + 1], tmpB[:, 0:1],
                ALU.max), deps=[(sDVE, pr)])
            if wt == "g":
                pr = emit("vector", lambda E, half=half, cpg=cpg: E.e.tensor_reduce(
                    tmpB[:, 1:2], stgF[:, cpg, half, :], AX.X, ALU.max),
                    deps=[(sDVE, pr)])
                pr = emit("vector", lambda E: E.e.tensor_tensor(
                    rmax[:, 3:4], rmax[:, 3:4], tmpB[:, 1:2], ALU.max),
                    deps=[(sDVE, pr)])
            f1_pool[(cpg, half)] = pr
        f1_evA[(wt, hh, ic)] = ev_a
        f1_evV[(wt, hh, ic)] = ev_v
    b_pe = sPE.n
    b_pool = sPOOL.n
    b_act = sACT.n
    b_dve = sDVE.n

    ar2_tr, ar2_ld = allreduce(1, [(sPOOL, b_pool), (sPE, b_pe),
                                   (sACT, b_act), (sDVE, b_dve)])
    v = emit("vector", lambda E: E.e.tensor_scalar(
        ss[0:1, 0:1], cc_sb[0:1, 1:2], 1.0 / QMAX, 1e-12, ALU.mult, ALU.max),
        deps=[ar2_ld])
    v = emit("vector", lambda E: E.e.tensor_scalar(
        ss[0:1, 1:2], cc_sb[0:1, 2:3], 1.0 / QMAX, 1e-12, ALU.mult, ALU.max),
        deps=[(sDVE, v)])
    v = emit("vector", lambda E: E.e.reciprocal(ss[0:1, 2:3], ss[0:1, 0:1]),
             deps=[(sDVE, v)])
    v = emit("vector", lambda E: E.e.reciprocal(ss[0:1, 3:4], ss[0:1, 1:2]),
             deps=[(sDVE, v)])
    v = emit("vector", lambda E: E.e.tensor_tensor(
        ss[0:1, 4:5], cc_sb[0:1, 3:4], ss[0:1, 3:4], ALU.mult), deps=[(sDVE, v)])
    v = emit("vector", lambda E: E.e.tensor_copy(ssi[0:1, 0:1], ss[0:1, 4:5]),
             deps=[(sDVE, v)])
    v = emit("vector", lambda E: E.e.tensor_tensor(
        ss[0:1, 5:6], ssi[0:1, 0:1], ss[0:1, 1:2], ALU.mult), deps=[(sDVE, v)])
    v = emit("scalar", lambda E: E.e.activation(
        ss[0:1, 6:7], ss[0:1, 5:6], AF.Tanh, scale=0.5), deps=[(sDVE, v)])
    v = emit("vector", lambda E: E.e.tensor_scalar(
        ssi[0:1, 1:2], ss[0:1, 6:7], 32768.0, 32768.0, ALU.mult, ALU.add),
        deps=[(sACT, v)])
    v = emit("vector", lambda E: E.e.tensor_scalar(
        ss[0:1, 7:8], ssi[0:1, 1:2], 1.0 / 65536.0, None, ALU.mult),
        deps=[(sDVE, v)])
    v = emit("vector", lambda E: E.e.tensor_tensor(
        ss[0:1, 8:9], ss[0:1, 5:6], ss[0:1, 7:8], ALU.mult), deps=[(sDVE, v)])
    v = emit("vector", lambda E: E.e.tensor_scalar(
        ss[0:1, 9:10], ss[0:1, 8:9], 1.0 / QMAX, 1e-12, ALU.mult, ALU.max),
        deps=[(sDVE, v)])
    v = emit("vector", lambda E: E.e.tensor_scalar(
        ss[0:1, 10:11], ss[0:1, 9:10], 65536.0, None, ALU.mult), deps=[(sDVE, v)])
    v = emit("vector", lambda E: E.e.reciprocal(ss[0:1, 11:12], ss[0:1, 10:11]),
             deps=[(sDVE, v)])
    v = emit("vector", lambda E: E.e.tensor_tensor(
        ss[0:1, 12:13], ss[0:1, 1:2], ss[0:1, 11:12], ALU.mult), deps=[(sDVE, v)])
    v = emit("vector", lambda E: E.e.tensor_tensor(
        ss[0:1, 13:14], ss[0:1, 9:10], ss[0:1, 0:1], ALU.mult), deps=[(sDVE, v)])
    v = emit("vector", lambda E: E.e.tensor_scalar(
        ss[0:1, 14:15], ss[0:1, 1:2], 0.5, None, ALU.mult), deps=[(sDVE, v)])
    ar2_bc = bcast_cols([(ss[0:1, 2:3], scb[:, 2:3]),
                         (ss[0:1, 3:4], scb[:, 3:4]),
                         (ss[0:1, 14:15], scb[:, 4:5]),
                         (ss[0:1, 12:13], scb[:, 5:6]),
                         (ss[0:1, 13:14], scb[:, 6:7])], [(sDVE, v)])

    scopes["B"].close()
    pbias = sb("pbias", [1, D], F32, "C")
    pwdmp = sb("pwdmp", [128, H, 8, 128], F16, "C")
    f2U = sb("f2U", [128, 3, 1024], F16, "C")
    f2G = sb("f2G", [128, 3, 1024], F16, "C")
    I16 = mybir.dt.int16
    f2ru = sb("f2ru", [128, 1024], I16, "C")
    f2rg = sb("f2rg", [128, 1024], I16, "C")
    f2t1 = sb("f2t1", [128, 1024], F32, "C")
    f2r2 = sb("f2r2", [128, 1024], I32, "C")
    f2o = sb("f2o", [128, 1024], F16, "C")
    stgQ = sb("stgQ", [128, 2, 1024], F16, "C")
    QKT = sb("QKT", [128, 4096], F16, "C")
    Vaug = sb("Vaug", [128, H, 65], F16, "C")
    PT = sb("PT", [128, 2, 512], F16, "C")
    OTn = sb("OTn", [64, 2048], F16, "C")
    scrB = sb("scrB", [128, 2048], F16, "C")
    lcol = sb("lcol", [128, 16], F16, "C")
    rcol = sb("rcol", [128, 16], F32, "C")
    stgAF = sb("stgAF", [128, D], F16, "C")

    d_vone = emit("vector", lambda E: E.e.memset(Vaug[:, :, 64:65], 1.0),
                  deps=[(sPE, b_pe), (sACT, b_act), (sDVE, b_dve)])
    d_pb = dma("gpsimd", "pb", pbias[:], pb_d, deps=[(sDVE, d_vone)])

    if LIMIT < 3:
        return _finish(nc, prog, scopes, ctx, dsems, Eng, cms)
    pw_tr = 0
    for i in range(H):
        for ec in range(8):
            pw_tr = dmat("pwdmp", pwdmp[:, i, ec, :], pw16[ec, :, i, :],
                         deps=[pw_st[ec]])
    pw_tr_all = pw_tr

    f2_list = [(ic, q) for q in range(4) for ic in range(NIC)]
    f2p = {}
    f2_store = {}
    f2_loaded = {}

    def f2_load(k):
        if k >= len(f2_list):
            return
        ic, q = f2_list[k]
        hh = q // 2
        qo = q * 1024
        sl = k % 3
        dl_u = dma("gpsimd", ("f2U", sl), f2U[:, sl, :],
                   up_s[ic, :, qo:qo + 1024],
                   deps=[f1_store[("u", hh, ic, q % 2)],
                         (sACT, f2p.get(("ru", sl), 0))])
        dl_g = dma("gpsimd", ("f2G", sl), f2G[:, sl, :],
                   gt_s[ic, :, qo:qo + 1024],
                   deps=[f1_store[("g", hh, ic, q % 2)],
                         (sACT, f2p.get(("rg", sl), 0))])
        f2_loaded[k] = (dl_u, dl_g)

    def f2_compute(k):
        if k >= len(f2_list):
            return
        ic, q = f2_list[k]
        qo = q * 1024
        sl = k % 3
        pg = k % 2
        dl_u, dl_g = f2_loaded[k]
        ru = emit("scalar", lambda E, sl=sl: E.e.activation(
            f2ru[:], f2U[:, sl, :], AF.Identity, scale=scb[:, 2:3]),
            deps=[dl_u, (sDVE, ar2_bc), (sDVE, f2p.get("t2", 0))])
        rg = emit("scalar", lambda E, sl=sl: E.e.activation(
            f2rg[:], f2G[:, sl, :], AF.Identity, scale=scb[:, 3:4]),
            deps=[dl_g, (sDVE, f2p.get("t1", 0))])
        f2p[("ru", sl)] = ru
        f2p[("rg", sl)] = rg
        tg = emit("scalar", lambda E: E.e.activation(
            f2t1[:], f2rg[:], AF.Tanh, scale=scb[:, 4:5]),
            deps=[(sACT, rg), (sDVE, f2p.get("mx", 0)),
                  (sDVE, f2p.get("t2", 0))])
        rs = emit("vector", lambda E: E.e.tensor_scalar(
            f2r2[:], f2t1[:], 32768.0, 32768.0, ALU.mult, ALU.add),
            deps=[(sACT, tg)])
        t1 = emit("vector", lambda E: E.e.tensor_tensor(
            f2t1[:], f2rg[:], f2r2[:], ALU.mult), deps=[(sDVE, rs)])
        f2p["t1"] = t1
        r2 = emit("vector", lambda E: E.e.tensor_scalar(
            f2r2[:], f2t1[:], scb[:, 5:6], None, ALU.mult), deps=[(sDVE, t1)])
        t2 = emit("vector", lambda E: E.e.tensor_tensor(
            f2t1[:], f2r2[:], f2ru[:], ALU.mult), deps=[(sDVE, r2)])
        f2p["t2"] = t2
        ost = emit("vector", lambda E: E.e.tensor_scalar(
            f2o[:], f2t1[:], scb[:, 6:7], None, ALU.mult),
            deps=[(sDVE, t2), f2p.get("st_dma", 0)])
        mx = emit("vector", lambda E: E.e.tensor_reduce(
            tmpF[:, 0:1], f2t1[:], AX.X, ALU.max, apply_absolute_value=True),
            deps=[(sDVE, ost), (sPE, ar2_tr)])
        mx = emit("vector", lambda E: E.e.tensor_tensor(
            rmax[:, 4:5], rmax[:, 4:5], tmpF[:, 0:1], ALU.max),
            deps=[(sDVE, mx)])
        f2p["mx"] = mx
        st = dma("gpsimd", "f2o", o_s[ic, :, qo:qo + 1024],
                 f2o[:], deps=[(sDVE, ost), (sDVE, mx)])
        f2p["st_dma"] = st
        f2_store[(ic, q)] = st

    blk = {}
    qst = {}
    psS_slot = [pbk[3][:, 0:512], pbk[3][:, 512:1024]]

    def pO_slot(p0, p1):
        if p0 < 1024:
            return pbk[1][0:65, p0:p1]
        return pbk[2][0:65, p0 - 1024:p1 - 1024]

    def pOn_slot(p):
        if p < 2:
            return pbk[1][0:64, p * 512:(p + 1) * 512]
        return pbk[2][0:64, (p - 2) * 512:(p - 1) * 512]

    def emit_qkv(b, pbk0_free_act, extra=()):
        prev = blk.get(b - 1, {})
        ev = 0
        sts = []
        dsts = [qk_s[b, 0:2048, 0:64].rearrange("(t i) d -> t i d", i=H),
                qk_s[b, 2048:4096, 0:64].rearrange("(t i) d -> t i d", i=H),
                v_s[b, :, :].rearrange("(t i) d -> t i d", i=H)]
        for r in range(3):
            rp = r % 2
            mm = 0
            for fc in range(8):
                deps = [(sDVE, a_xT), wqkv_done,
                        (sACT, pbk0_free_act), (sACT, ev)] + list(extra)
                mm = emit("tensor", lambda E, fc=fc, r=r, b=b: E.e.matmul(
                    pbk[0][:, 0:512],
                    xT[:, fc, b * 128:(b + 1) * 128],
                    WQKV[:, fc, r * 1024 + 0:r * 1024 + 512],
                    start=(fc == 0), stop=(fc == 7)), deps=deps)
                mm = emit("tensor", lambda E, fc=fc, r=r, b=b: E.e.matmul(
                    pbk[0][:, 512:1024],
                    xT[:, fc, b * 128:(b + 1) * 128],
                    WQKV[:, fc, r * 1024 + 512:r * 1024 + 1024],
                    start=(fc == 0), stop=(fc == 7)), deps=deps)
            pq = qst.get(b - 1, {})
            war = (sts[0] if r >= 2 else
                   (pq.get("st_k", 0) if r == 1 else pq.get("st_v", 0)))
            for half in range(2):
                ev = emit("scalar", lambda E, rp=rp, half=half: E.e.activation(
                    stgQ[:, rp, half * 512:(half + 1) * 512],
                    pbk[0][:, half * 512:(half + 1) * 512], AF.Copy),
                    deps=[(sPE, mm), war])
            sts.append(dma("sync", ("qkst", r), dsts[r],
                           stgQ[:, rp, :].rearrange("p (i d) -> p i d", d=HS),
                           deps=[(sACT, ev)]))
        st_q, st_k, st_v = sts
        tr = dmat("QKT", QKT[:], qk_s[b],
                  deps=[st_k, st_q, (sPE, blk.get(b - 1, {}).get("s_last", 0))])
        vrb = dma("sync", "Vaug", Vaug[:, :, 0:64],
                  v_s[b].rearrange("(j p) d -> p j d", p=128),
                  deps=[st_v, (sPE, blk.get(b - 1, {}).get("av", 0)),
                        (sDVE, d_vone)])
        qst[b] = dict(ev=ev, st_v=st_v, st_k=st_k, tr=tr, vrb=vrb)

    emit_qkv(0, 0, extra=[(sPE, b_pe), (sACT, b_act), (sDVE, ar2_bc),
                          (sDVE, b_dve)])

    for b in range(C_BLOCKS):
        pb_ = blk.get(b - 1, {})
        if b >= 1:
            for kf in range(3 * (b - 1), 3 * b):
                f2_compute(kf)
        for kf in range(3 * b, 3 * b + 3):
            f2_load(kf)
        tr = qst[b]["tr"]
        vrb = qst[b]["vrb"]
        exp_done = {}
        av_ring = {}
        npc = 0
        av = 0
        s_last = 0
        pend_av = None
        for j in range(H):
            lo = j * 128
            bnds = sorted(set([lo] + list(range(512 * (lo // 512 + 1), 2048, 512)) + [2048]))
            for (p0, p1) in zip(bnds[:-1], bnds[1:]):
                pi = npc % 2
                sdeps = [tr, (sACT, exp_done.get(pi, 0)),
                         (sDVE, pb_.get("evb", 0))]
                if b == 0 and npc < 2:
                    sdeps += [(sDVE, ar2_bc), (sDVE, b_dve)]
                smm = emit("tensor", lambda E, j=j, p0=p0, p1=p1, pi=pi:
                    E.e.matmul(psS_slot[pi][:, 0:p1 - p0],
                               QKT[0:64, 2048 + j * 128:2048 + (j + 1) * 128],
                               QKT[0:64, p0:p1], start=True, stop=True),
                    deps=sdeps)
                s_last = smm
                if pend_av is not None:
                    fn, avdeps, ppi = pend_av
                    av = emit("tensor", fn, deps=avdeps)
                    av_ring[ppi] = av
                dv = 0
                if p0 == lo:
                    dv = emit("vector", lambda E, pi=pi: E.e.tensor_tensor(
                        psS_slot[pi][:, 0:128], psS_slot[pi][:, 0:128],
                        mtri[:], ALU.add), deps=[(sPE, smm), (sDVE, d_mtri)])
                edeps = [(sPE, smm), (sPE, av_ring.get(pi, 0))]
                if dv:
                    edeps.append((sDVE, dv))
                ex = emit("scalar", lambda E, pi=pi, p0=p0, p1=p1:
                    E.e.activation(PT[:, pi, 0:p1 - p0],
                                   psS_slot[pi][:, 0:p1 - p0], AF.Exp,
                                   scale=0.125), deps=edeps)
                exp_done[pi] = ex
                avdeps = [(sACT, ex), vrb]
                if j == 0:
                    avdeps += [(sDVE, pb_.get("otn", 0)),
                               (sDVE, pb_.get("lc", 0))]
                pend_av = (lambda E, j=j, p0=p0, p1=p1, pi=pi:
                    E.e.matmul(pO_slot(p0, p1), Vaug[:, j, :],
                               PT[:, pi, 0:p1 - p0],
                               start=(j == 0), stop=(j == H - 1)),
                    avdeps, pi)
                npc += 1
        fn, avdeps, ppi = pend_av
        av = emit("tensor", fn, deps=avdeps)
        av_ring[ppi] = av
        av_done = av
        lc = 0
        for half in range(2):
            lc = emit("vector", lambda E, half=half: E.e.tensor_copy(
                scrB[64:65, half * 1024:(half + 1) * 1024],
                pbk[1 + half][64:65, 0:1024]),
                deps=[(sPE, av_done), pb_.get("l_st", 0),
                      (sPE, pb_.get("bcast", 0))])
        l_st = dma("sync", "l_s", l_s[b, :].unsqueeze(0), scrB[64:65, :],
                   deps=[(sDVE, lc)])
        l_ld = dma("sync", "lcol", lcol[:],
                   l_s[b, :].rearrange("(p k) -> p k", p=128),
                   deps=[l_st, (sDVE, pb_.get("rc", 0))])
        rc = emit("vector", lambda E: E.e.reciprocal(rcol[:], lcol[:]),
                  deps=[l_ld])
        rl_st = dma("gpsimd", "rl_s",
                    rl_s[b, :].rearrange("(p k) -> p k", p=128), rcol[:],
                    deps=[(sDVE, rc)])
        rl_ld = dma("sync", "rlrow", scrB[0:1, :], rl_s[b, :].unsqueeze(0),
                    deps=[rl_st, l_st, (sDVE, pb_.get("otn", 0))])
        if b + 1 < C_BLOCKS:
            emit_qkv(b + 1, pb_.get("af_ev", 0))
        otn = 0
        bca = 0
        last_evb = 0
        evb_of = {}
        for p in range(4):
            pi = p % 2
            bb = emit("tensor", lambda E, p=p, pi=pi: E.e.matmul(
                psS_slot[pi][0:64, 0:512], onesC16[0:1, 0:64],
                scrB[0:1, p * 512:(p + 1) * 512], start=True, stop=True),
                deps=[rl_ld, (sACT, exp_done.get(pi, 0)),
                      (sDVE, evb_of.get(p - 2, 0)), (sDVE, d_ones)])
            bca = bb
            evb = emit("vector", lambda E, p=p, pi=pi: E.e.tensor_copy(
                scrB[0:64, p * 512:(p + 1) * 512], psS_slot[pi][0:64, 0:512]),
                deps=[(sPE, bb), (sDVE, pb_.get("otn", 0))])
            evb_of[p] = evb
            last_evb = evb
            otn = emit("vector", lambda E, p=p: E.e.tensor_tensor(
                OTn[:, p * 512:(p + 1) * 512], pOn_slot(p),
                scrB[0:64, p * 512:(p + 1) * 512], ALU.mult),
                deps=[(sDVE, evb), (sPE, av_done), (sPE, pb_.get("proj", 0))])
        pm = 0
        for p in range(2):
            pm = emit("tensor", lambda E, p=p: E.e.matmul(
                pbk[0][:, p * 512:(p + 1) * 512], onesC[0:1, :],
                pbias[0:1, p * 512:(p + 1) * 512], start=True, stop=False),
                deps=[d_pb, (sDVE, otn), (sDVE, d_ones),
                      (sACT, qst.get(b + 1, {}).get("ev", 0)),
                      pw_tr_all])
        for i in range(H):
            for p in range(2):
                pm = emit("tensor", lambda E, i=i, p=p: E.e.matmul(
                    pbk[0][:, p * 512:(p + 1) * 512],
                    OTn[:].rearrange("d (t i) -> d i t", i=H)[:, i, :],
                    pwdmp[0:64, i, :, :].rearrange("d a b -> d (a b)")[:, p * 512:(p + 1) * 512],
                    start=False, stop=(i == H - 1)), deps=[(sDVE, otn)])
        prj = pm
        af_ev = emit("scalar", lambda E: E.e.activation(
            stgAF[:], pbk[0][:, :], AF.Copy),
            deps=[(sPE, prj), pb_.get("af_st", 0)])
        af_st = dma("sync", "stgAF", af_s[b], stgAF[:], deps=[(sACT, af_ev)])

        blk[b] = dict(av=av_done, otn=otn, proj=prj, af_ev=af_ev,
                      af_st=af_st, rc=rc, lc=lc, l_st=l_st, bcast=bca,
                      evb=last_evb, s_last=s_last)
    for kf in range(3 * (C_BLOCKS - 1), len(f2_list)):
        f2_compute(kf)
    c_pe = sPE.n
    c_dve = sDVE.n
    c_act = sACT.n
    c_pool = sPOOL.n

    ar3_tr, ar3_ld = allreduce(2, [(sDVE, c_dve), (sPE, c_pe),
                                   (sACT, c_act), (sPOOL, c_pool)])
    v = emit("vector", lambda E: E.e.tensor_tensor(
        ss[0:1, 15:16], cc_sb[0:1, 4:5], ss[0:1, 13:14], ALU.mult),
        deps=[ar3_ld])
    v = emit("vector", lambda E: E.e.tensor_scalar(
        ss[0:1, 15:16], ss[0:1, 15:16], 1.0 / QMAX, 1e-12, ALU.mult, ALU.max),
        deps=[(sDVE, v)])
    v = emit("vector", lambda E: E.e.reciprocal(sc8[0:1, 0:1], ss[0:1, 15:16]),
             deps=[(sDVE, v)])
    ar3_bc = bcast_cols([(sc8[0:1, 0:1], scb2[:, 0:1]),
                         (ss[0:1, 15:16], scb2[:, 1:2])], [(sDVE, v)])

    if LIMIT < 4:
        return _finish(nc, prog, scopes, ctx, dsems, Eng, cms)
    scopes["C"].close()
    cms["xT"].__exit__(None, None, None)
    cms["WQKV"].__exit__(None, None, None)
    WdT = sb("WdT", [128, NIC, 1024], F16, "D")
    oq = sb("oq", [128, 2, NIC, 1024], F16, "D")
    o_ld = sb("o_ld", [128, 2, 1024], F16, "D")
    o_i32 = sb("o_i32", [128, 2, 1024], I32, "D")
    outstg = sb("outstg", [128, 2, D], F32, "D")
    xres = sb("xres", [128, 2, D], F32, "D")
    afres = sb("afres", [128, 2, D], F16, "D")

    wdt_tr = 0
    for ic in range(NIC):
        wdt_tr = dma("sync", "WdT", WdT[:, ic, :], wfm_d[ic],
                     deps=[wd_fst[(7, 0)], wd_fst[(7, 1)],
                           (sPE, c_pe), (sACT, c_act), (sDVE, c_dve),
                           (sPOOL, c_pool)])
    wdt_all = wdt_tr

    oq_done = {}
    dn_done = {}
    res_done = {}
    out_dma = {}
    oq_cast = [0, 0]
    oq_sc = [0, 0]
    for g in range(4):
        gg = g % 2
        oqv = 0
        for ic in range(NIC):
            pg2 = ic % 2
            dl = dma("gpsimd", ("old", pg2), o_ld[:, pg2, :],
                     o_s[ic, :, g * 1024:(g + 1) * 1024],
                     deps=[f2_store[(ic, g)],
                           (sACT, oq_cast[pg2]), (sDVE, ar3_bc)])
            c = emit("scalar", lambda E, pg2=pg2: E.e.activation(
                o_i32[:, pg2, :], o_ld[:, pg2, :], AF.Identity, scale=scb2[:, 0:1]),
                deps=[dl, (sDVE, ar3_bc), (sDVE, oq_sc[pg2])])
            oq_cast[pg2] = c
            svv = emit("vector", lambda E, ic=ic, gg=gg, pg2=pg2: E.e.tensor_scalar(
                oq[:, gg, ic, :], o_i32[:, pg2, :], scb2[:, 1:2], None, ALU.mult),
                deps=[(sACT, c), (sPE, dn_done.get((g - 2, 7), 0))])
            oq_sc[pg2] = svv
            oqv = svv
        oq_done[g] = oqv
        for tt in range(8):
            t_abs = g * 8 + tt
            pgd = tt % 2
            mm = 0
            war = res_done.get((g, tt - 2), 0) if tt >= 2 else \
                res_done.get((g - 1, 6 + tt), 0)
            for ic in range(NIC):
                for p in range(2):
                    deps = [(sDVE, oq_done[g]), wdt_all,
                            (sDVE, war)]
                    if g == 0 and tt < 2:
                        deps += [(sACT, c_act), (sDVE, c_dve), (sPE, c_pe)]
                    mm = emit("tensor", lambda E, ic=ic, p=p, tt=tt, gg=gg, pgd=pgd:
                        E.e.matmul(pbk[pgd][:, p * 512:(p + 1) * 512],
                                   oq[:, gg, ic, tt * 128:(tt + 1) * 128],
                                   WdT[:, ic, p * 512:(p + 1) * 512],
                                   start=(ic == 0), stop=(ic == NIC - 1)),
                        deps=deps)
            dn_done[(g, tt)] = mm
            xl = dma("sync", ("xres", pgd), xres[:, pgd, :],
                     x_d[t_abs * 128:(t_abs + 1) * 128, :],
                     deps=[(sDVE, war), (sDVE, ar3_bc)])
            al = dma("sync", ("afres", pgd), afres[:, pgd, :], af_s[t_abs],
                     deps=[blk[t_abs]["af_st"], (sDVE, war),
                           (sDVE, ar3_bc)])
            r1 = emit("vector", lambda E, pgd=pgd: E.e.tensor_tensor(
                outstg[:, pgd, :], pbk[pgd][:, :], afres[:, pgd, :], ALU.add),
                deps=[(sPE, mm), al, out_dma.get((g, tt - 2),
                      out_dma.get((g - 1, 6 + tt), 0))])
            r2 = emit("vector", lambda E, pgd=pgd: E.e.tensor_tensor(
                outstg[:, pgd, :], outstg[:, pgd, :], xres[:, pgd, :], ALU.add),
                deps=[(sDVE, r1), xl])
            res_done[(g, tt)] = r2
            out_dma[(g, tt)] = dma("sync", ("out", pgd),
                out_d[t_abs * 128:(t_abs + 1) * 128, :],
                outstg[:, pgd, :], deps=[(sDVE, r2)])

    scopes["D"].close()
    return _finish(nc, prog, scopes, ctx, dsems, Eng, cms)


def _finish(nc, prog, scopes, ctx, dsems, Eng, cms):
    for sc in scopes.values():
        try:
            sc.close()
        except Exception:
            pass
    for nm in ("xT", "WQKV"):
        try:
            cms[nm].__exit__(None, None, None)
        except Exception:
            pass
    finals = [(ct, ct.n) for ct in dsems.values() if ct.n > 0]

    with nc.Block() as block:
        @block.sync
        def _(sync):
            E = Eng(sync)
            for fn in prog["sync"]:
                fn(E)
            for ct, v in finals:
                sync.wait_ge(ct.h, v)

        @block.tensor
        def _(tensor):
            E = Eng(tensor)
            for fn in prog["tensor"]:
                fn(E)

        @block.scalar
        def _(scalar):
            E = Eng(scalar)
            for fn in prog["scalar"]:
                fn(E)

        @block.vector
        def _(vector):
            E = Eng(vector)
            for fn in prog["vector"]:
                fn(E)

        @block.gpsimd
        def _(gpsimd):
            E = Eng(gpsimd)
            for fn in prog["gpsimd"]:
                fn(E)

    ctx.close()
    return nc


_CACHE = {}


def kernel(L0_attn_x, L0_attn_Wq, L0_attn_Wk, L0_attn_Wv, L0_attn_proj_w,
           L0_attn_proj_b, L0_ffn_w_gate, L0_ffn_w_up, L0_ffn_w_down):
    if "nc" not in _CACHE:
        _CACHE["nc"] = build_program()
    nc = _CACHE["nc"]
    x = np.ascontiguousarray(np.asarray(L0_attn_x, dtype=np.float32))
    shared = dict(
        wq=np.ascontiguousarray(np.asarray(L0_attn_Wq, dtype=np.float32)),
        wk=np.ascontiguousarray(np.asarray(L0_attn_Wk, dtype=np.float32)),
        wv=np.ascontiguousarray(np.asarray(L0_attn_Wv, dtype=np.float32)),
        pw=np.ascontiguousarray(np.asarray(L0_attn_proj_w, dtype=np.float32)),
        pb=np.ascontiguousarray(np.asarray(L0_attn_proj_b, dtype=np.float32)).reshape(1, D),
        wg=np.ascontiguousarray(np.asarray(L0_ffn_w_gate, dtype=np.float32)),
        wu=np.ascontiguousarray(np.asarray(L0_ffn_w_up, dtype=np.float32)),
        wd=np.ascontiguousarray(np.asarray(L0_ffn_w_down, dtype=np.float32)),
    )
    in_maps = []
    for k in range(NCORES):
        m = dict(shared)
        m["x"] = np.ascontiguousarray(x[2 * k:2 * k + 2].reshape(TCORE, D))
        in_maps.append(m)
    res = run_bass_kernel_spmd(nc, in_maps, core_ids=list(range(NCORES)))
    parts = [np.asarray(r["out"]).reshape(2, 2048, D) for r in res.results]
    return np.concatenate(parts, axis=0)


# version 18
# speedup vs baseline: 1.4938x; 1.0102x over previous
from contextlib import ExitStack

import numpy as np
import concourse.bass as bass
import concourse.mybir as mybir
from concourse.bass_utils import run_bass_kernel_spmd

F32 = mybir.dt.float32
F16 = mybir.dt.float16
I32 = mybir.dt.int32
AF = mybir.ActivationFunctionType
ALU = mybir.AluOpType
AX = mybir.AxisListType

NCORES = 8
LIMIT = 99
C_BLOCKS = 32
C_STEP = 9
NB = 32
TCORE = NB * 128
D = 1024
I3 = 3072
NIC = 24
H, HS = 16, 64
QMAX = 32767.0
WQMAX = 7.0
EPS = 1e-5
NEG = -1.0e30


class Ct:
    def __init__(self, ctx, nc, name):
        self.h = ctx.enter_context(nc.semaphore(name))
        self.n = 0


class Eng:
    def __init__(self, e):
        self.e = e
        self.waited = {}

    def wait(self, deps):
        for ct, val in deps:
            if val is None or val <= 0:
                continue
            if self.waited.get(id(ct), -1) >= val:
                continue
            self.e.wait_ge(ct.h, val)
            self.waited[id(ct)] = val


def build_program():
    nc = bass.Bass("TRN2", target_bir_lowering=False, debug=False)

    x_d = nc.dram_tensor("x", [TCORE, D], F32, kind="ExternalInput").ap()
    wq_d = nc.dram_tensor("wq", [H, D, HS], F32, kind="ExternalInput").ap()
    wk_d = nc.dram_tensor("wk", [H, D, HS], F32, kind="ExternalInput").ap()
    wv_d = nc.dram_tensor("wv", [H, D, HS], F32, kind="ExternalInput").ap()
    pw_d = nc.dram_tensor("pw", [D, D], F32, kind="ExternalInput").ap()
    pb_d = nc.dram_tensor("pb", [1, D], F32, kind="ExternalInput").ap()
    wg_d = nc.dram_tensor("wg", [I3, D], F32, kind="ExternalInput").ap()
    wu_d = nc.dram_tensor("wu", [I3, D], F32, kind="ExternalInput").ap()
    wd_d = nc.dram_tensor("wd", [D, I3], F32, kind="ExternalInput").ap()
    out_d = nc.dram_tensor("out", [TCORE, D], F32, kind="ExternalOutput").ap()

    qk_s = nc.dram_tensor("qk_s", [NB, 4096, 128], F16).ap()
    v_s = nc.dram_tensor("v_s", [NB, 2048, 64], F16).ap()
    af_s = nc.dram_tensor("af_s", [NB, 128, D], F16).ap()
    up_s = nc.dram_tensor("up_s", [NIC, 128, TCORE], F16).ap()
    gt_s = nc.dram_tensor("gt_s", [NIC, 128, TCORE], F16).ap()
    o_s = nc.dram_tensor("o_s", [NIC, 128, TCORE], F16).ap()
    wdq_u = nc.dram_tensor("wdq_u", [NIC, 128, D], F16).ap()
    wdq_g = nc.dram_tensor("wdq_g", [NIC, 128, D], F16).ap()
    wdq_dd = nc.dram_tensor("wdq_dd", [8, 128, I3], F16).ap()
    wfm_u = nc.dram_tensor("wfm_u", [NIC, 128, 1024], F16).ap()
    wfm_g = nc.dram_tensor("wfm_g", [NIC, 128, 1024], F16).ap()
    wfm_d = nc.dram_tensor("wfm_d", [NIC, 128, 1024], F16).ap()
    pw16 = nc.dram_tensor("pw16", [8, 128, H, 128], F16).ap()
    l_s = nc.dram_tensor("l_s", [NB, 2048], F16).ap()
    rl_s = nc.dram_tensor("rl_s", [NB, 2048], F16).ap()
    cc_i = [nc.dram_tensor(f"cc{i}i", [1, 8], F32).ap() for i in range(3)]
    cc_o = [nc.dram_tensor(f"cc{i}o", [1, 8], F32).ap() for i in range(3)]

    ctx = ExitStack()
    scopes = {"A": ExitStack(), "B": ExitStack(), "C": ExitStack(),
              "D": ExitStack()}
    cms = {}

    def sb(name, shape, dtype, scope=None):
        if scope is None:
            cm = nc.sbuf_tensor(name, shape, dtype)
            t = ctx.enter_context(cm)
        elif scope == "manual":
            cm = nc.sbuf_tensor(name, shape, dtype)
            t = cm.__enter__()
        else:
            cm = nc.sbuf_tensor(name, shape, dtype, side="right")
            t = scopes[scope].enter_context(cm)
        cms[name] = cm
        return t

    idn32 = sb("idn32", [128, 128], F32)
    mtri = sb("mtri", [128, 128], F32)
    onesC = sb("onesC", [128, 128], F32)
    onesC16 = sb("onesC16", [128, 128], F16)
    stats = sb("stats", [128, 8, 10], F32)
    rmax = sb("rmax", [128, 8], F32)
    red8 = sb("red8", [8, 1], F32)
    sc8 = sb("sc8", [1, 16], F32)
    ss = sb("ss", [1, 16], F32)
    ssi = sb("ssi", [1, 2], I32)
    scb = sb("scb", [128, 8], F32)
    scb2 = sb("scb2", [128, 2], F32)
    cc_sb = sb("cc_sb", [1, 8], F32)
    tmpB = sb("tmpB", [128, 2], F32)
    tmpF = sb("tmpF", [128, 2], F32)
    WQKV = sb("WQKV", [128, 8, 3072], F16, "manual")
    xT = sb("xT", [128, 8, TCORE], F16, "manual")


    pbk = [ctx.enter_context(nc.psum_tensor(f"pbk{i}", [128, 1024], F32))
           for i in range(4)]

    sPE = Ct(ctx, nc, "sPE")
    sACT = Ct(ctx, nc, "sACT")
    sDVE = Ct(ctx, nc, "sDVE")
    sPOOL = Ct(ctx, nc, "sPOOL")
    sDMA = Ct(ctx, nc, "sDMA")
    sGD = Ct(ctx, nc, "sGD")
    sCC = Ct(ctx, nc, "sCC")

    prog = {k: [] for k in ("sync", "tensor", "scalar", "vector", "gpsimd")}
    ENG = {"tensor": sPE, "scalar": sACT, "vector": sDVE, "gpsimd": sPOOL}

    def emit(engine, builder, deps=()):
        sem = ENG[engine]
        sem.n += 1
        deps = [d for d in deps if d is not None and d != 0]
        def fn(E, builder=builder, deps=deps, sem=sem):
            E.wait(deps)
            ins = builder(E)
            ins.then_inc(sem.h, 1)
        prog[engine].append(fn)
        return sem.n

    dsems = {}

    def dsem(key):
        if key not in dsems:
            dsems[key] = Ct(ctx, nc, "dk_%s" % "_".join(str(x) for x in (
                key if isinstance(key, tuple) else (key,))))
        return dsems[key]

    def dma(engine, key, out, in_, deps=(), **kw):
        sem = dsem(key)
        sem.n += 16
        deps = [d for d in deps if d is not None and d != 0]
        def fn(E, out=out, in_=in_, deps=deps, kw=kw, sem=sem):
            E.wait(deps)
            ins = E.e.dma_start(out=out, in_=in_, **kw)
            ins.then_inc(sem.h, 16)
        prog[engine].append(fn)
        return (sem, sem.n)

    def dmat(key, out, in_, deps=()):
        sem = dsem(key)
        sem.n += 16
        deps = [d for d in deps if d is not None and d != 0]
        def fn(E, out=out, in_=in_, deps=deps, sem=sem):
            E.wait(deps)
            ins = E.e.dma_start_transpose(out, in_)
            ins.then_inc(sem.h, 16)
        prog["sync"].append(fn)
        return (sem, sem.n)

    def collective(cci, cco, deps):
        sCC.n += 1
        deps = [d for d in deps if d is not None]
        def fn(E, deps=deps):
            E.wait(deps)
            ins = E.e.collective_compute(
                "AllReduce", ALU.max, replica_groups=[list(range(NCORES))],
                ins=[cci.opt()], outs=[cco.opt()])
            ins.then_inc(sCC.h, 1)
        prog["gpsimd"].append(fn)
        return sCC.n

    d_iota = emit("gpsimd", lambda E: E.e.iota(
        iotaf[:], pattern=[[1, 128]], base=0, channel_multiplier=-1,
        allow_small_or_imprecise_dtypes=True))
    d_idn32 = emit("vector", lambda E: E.e.tensor_scalar(
        idn32[:], iotaf[:], 0.0, None, ALU.is_equal), deps=[(sPOOL, d_iota)])
    d_m01 = emit("vector", lambda E: E.e.tensor_scalar(
        mtri[:], iotaf[:], 0.0, None, ALU.is_ge), deps=[(sPOOL, d_iota)])
    d_mtri = emit("vector", lambda E: E.e.tensor_scalar(
        mtri[:], mtri[:], 1.0, -NEG, ALU.subtract, ALU.mult),
        deps=[(sDVE, d_m01)])
    d_ones = emit("vector", lambda E: E.e.memset(onesC[:], 1.0))
    d_ones = emit("vector", lambda E: E.e.memset(onesC16[:], 1.0),
                  deps=[(sDVE, d_ones)])
    d_rm0 = emit("vector", lambda E: E.e.memset(rmax[:], 0.0))
    d_st0 = emit("vector", lambda E: E.e.memset(stats[:], 1.0))
    d_rm1 = emit("vector", lambda E: E.e.tensor_scalar(
        rmax[:, 3:4], rmax[:, 3:4], 0.0, NEG, ALU.mult, ALU.add),
        deps=[(sDVE, d_rm0)])

    iotaf = sb("iotaf", [128, 128], F32, "A")
    xa = sb("xa", [128, 4, D], F32, "A")
    xn32 = sb("xn32", [128, 2, D], F32, "A")
    sqdmp = sb("sqdmp", [128, D], F32, "A")
    pw_f32 = sb("pw_f32", [128, 2, D], F32, "A")
    pw_cast = sb("pw_cast", [128, 2, D], F16, "A")
    WDH = 1536
    wch = sb("wch", [128, 2, WDH], F32, "A")
    dqmax = sb("dqmax", [128, 2, 48], F32, "A")
    dqs = sb("dqs", [128, 2, 48], F32, "A")
    dqsr = sb("dqsr", [128, 2, 48], F32, "A")
    dqz = sb("dqz", [128, 2, WDH], F32, "A")
    dqi = sb("dqi", [128, 2, WDH], I32, "A")
    dq16 = sb("dq16", [128, 2, WDH], F16, "A")
    ftA = sb("ftA", [128, 2, 12, 128], F16, "A")



    wqkv_done = 0
    for fc in range(8):
        for wsrc, base in ((wq_d, 0), (wk_d, 1024), (wv_d, 2048)):
            src = wsrc[:, fc * 128:(fc + 1) * 128, :].rearrange("i f d -> f i d")
            dst = WQKV[:, fc, base:base + 1024].rearrange("p (i d) -> p i d", d=HS)
            wqkv_done = dma("gpsimd", "wqkv", dst, src)

    pw_st = [0] * 8
    pw_cv = [0] * 8
    for ec in range(8):
        pg = ec % 2
        dl = dma("sync", ("pwf", pg), pw_f32[:, pg, :], pw_d[ec * 128:(ec + 1) * 128, :],
                 deps=[(sACT, pw_cv[ec - 2]) if ec >= 2 else None])
        cv = emit("scalar", lambda E, pg=pg: E.e.activation(
            pw_cast[:, pg, :], pw_f32[:, pg, :], AF.Copy),
            deps=[dl, pw_st[ec - 2] if ec >= 2 else None])
        pw_cv[ec] = cv
        pw_st[ec] = dma("sync", ("pw16", pg), pw16[ec, :, :, 0:64],
                        pw_cast[:, pg, :].rearrange("p (i d) -> p i d", d=64),
                        deps=[(sACT, cv)])

    trp = pbk[3]
    last_tr_copy = 0
    last_sq = 0
    last_xn = [0, 0]
    xa_free = [0] * 4
    amx_of = [0] * 4
    a_mx = d_rm1
    for g in range(8):
        xa_load = [0] * 4
        for j in range(4):
            b = g * 4 + j
            xa_load[j] = dma("sync", ("xa", j), xa[:, j, :], x_d[b * 128:(b + 1) * 128, :],
                             deps=[(sACT, xa_free[j]) if g > 0 else None])
        for j in range(4):
            last_sq = emit("scalar", lambda E, j=j: E.e.activation(
                sqdmp[:], xa[:, j, :], AF.Square, accum_out=stats[:, j, 0:1]),
                deps=[xa_load[j], (sACT, last_sq), (sDVE, amx_of[j])])
            emit("vector", lambda E, j=j: E.e.reduce_sum(
                stats[:, j, 1:2], xa[:, j, :], AX.X), deps=[xa_load[j]])
        vv = sDVE.n
        vv = emit("vector", lambda E: E.e.tensor_scalar(
            stats[:, :, 2:3].opt(), stats[:, :, 1:2].opt(), 1.0 / D, None,
            ALU.mult), deps=[(sDVE, vv)])
        vv = emit("vector", lambda E: E.e.tensor_scalar(
            stats[:, :, 3:4].opt(), stats[:, :, 0:1].opt(), 1.0 / D, None,
            ALU.mult), deps=[(sACT, last_sq), (sDVE, vv)])
        vv = emit("vector", lambda E: E.e.tensor_tensor(
            stats[:, :, 4:5].opt(), stats[:, :, 2:3].opt(),
            stats[:, :, 2:3].opt(), ALU.mult), deps=[(sDVE, vv)])
        vv = emit("vector", lambda E: E.e.tensor_tensor(
            stats[:, :, 5:6].opt(), stats[:, :, 3:4].opt(),
            stats[:, :, 4:5].opt(), ALU.subtract), deps=[(sDVE, vv)])
        vv = emit("vector", lambda E: E.e.tensor_scalar(
            stats[:, :, 5:6].opt(), stats[:, :, 5:6].opt(), 1.0, EPS, ALU.mult,
            ALU.add), deps=[(sDVE, vv)])
        vv = emit("vector", lambda E: E.e.memset(stats[:, :, 6:7].opt(), 1.0),
                  deps=[(sDVE, vv)])
        for it in range(7):
            vv = emit("vector", lambda E: E.e.tensor_tensor(
                stats[:, :, 7:8].opt(), stats[:, :, 6:7].opt(),
                stats[:, :, 6:7].opt(), ALU.mult), deps=[(sDVE, vv)])
            vv = emit("vector", lambda E: E.e.tensor_tensor(
                stats[:, :, 8:9].opt(), stats[:, :, 7:8].opt(),
                stats[:, :, 5:6].opt(), ALU.mult), deps=[(sDVE, vv)])
            vv = emit("vector", lambda E: E.e.tensor_scalar(
                stats[:, :, 8:9].opt(), stats[:, :, 8:9].opt(), -0.5, 1.5,
                ALU.mult, ALU.add), deps=[(sDVE, vv)])
            vv = emit("vector", lambda E: E.e.tensor_tensor(
                stats[:, :, 6:7].opt(), stats[:, :, 6:7].opt(),
                stats[:, :, 8:9].opt(), ALU.mult), deps=[(sDVE, vv)])
        vv = emit("vector", lambda E: E.e.scalar_tensor_tensor(
            stats[:, :, 9:10].opt(), stats[:, :, 2:3].opt(), -1.0,
            stats[:, :, 6:7].opt(), ALU.mult, ALU.mult), deps=[(sDVE, vv)])
        grp = vv
        for j in range(4):
            b = g * 4 + j
            pg = j % 2
            d_xn = emit("scalar", lambda E, j=j, pg=pg: E.e.activation(
                xn32[:, pg, :], xa[:, j, :], AF.Identity, bias=stats[:, j, 9:10],
                scale=stats[:, j, 6:7]),
                deps=[(sDVE, grp), (sDVE, last_xn[pg])])
            xa_free[j] = d_xn
            d_mx = emit("vector", lambda E, j=j, pg=pg: E.e.tensor_reduce(
                stats[:, j, 0:1], xn32[:, pg, :], AX.X, ALU.max,
                apply_absolute_value=True), deps=[(sACT, d_xn)])
            a_mx = emit("vector", lambda E, j=j: E.e.tensor_tensor(
                rmax[:, 0:1], rmax[:, 0:1], stats[:, j, 0:1], ALU.max),
                deps=[(sDVE, d_mx), (sDVE, a_mx)])
            amx_of[j] = a_mx
            trs = []
            for fc in range(8):
                tr = emit("tensor", lambda E, fc=fc, pg=pg: E.e.transpose(
                    trp[:, fc * 128:(fc + 1) * 128],
                    xn32[:, pg, fc * 128:(fc + 1) * 128], idn32[:]),
                    deps=[(sACT, d_xn), (sDVE, d_idn32), (sDVE, last_tr_copy)])
                trs.append(tr)
            for fc in range(8):
                last_tr_copy = emit("vector",
                    lambda E, fc=fc, b=b: E.e.tensor_copy(
                        xT[:, fc, b * 128:(b + 1) * 128],
                        trp[:, fc * 128:(fc + 1) * 128]),
                    deps=[(sPE, trs[fc])])
            last_xn[pg] = last_tr_copy
    a_xT = last_tr_copy

    dq_state = {"z": 0, "c": 0, "w": 0, "st": [0, 0], "ft": [0, 0], "fs": [0, 0]}

    def dq_chunk(src, width, wdq_row, wfm_tiles, k):
        pg = k % 2
        ng = width // 32
        nt = width // 128
        st_ = dq_state
        dl = dma("gpsimd", "wch", wch[:, pg, 0:width], src,
                 deps=[(sDVE, st_["z"])])
        r = emit("vector", lambda E: E.e.tensor_reduce(
            dqmax[:, pg, 0:ng], wch[:, pg, 0:width].rearrange(
                "p (g e) -> p g e", e=32),
            AX.X, ALU.max, apply_absolute_value=True), deps=[dl])
        r = emit("vector", lambda E: E.e.tensor_scalar(
            dqs[:, pg, 0:ng], dqmax[:, pg, 0:ng], 1.0 / WQMAX, 1e-12,
            ALU.mult, ALU.max), deps=[(sDVE, r)])
        r = emit("vector", lambda E: E.e.reciprocal(
            dqsr[:, pg, 0:ng], dqs[:, pg, 0:ng]), deps=[(sDVE, r)])
        z = emit("vector", lambda E: E.e.tensor_tensor(
            dqz[:, pg, 0:width].rearrange("p (g e) -> p g e", e=32),
            wch[:, pg, 0:width].rearrange("p (g e) -> p g e", e=32),
            dqsr[:, pg, 0:ng].unsqueeze(2).broadcast_to([128, ng, 32]),
            ALU.mult), deps=[(sDVE, r), (sACT, st_["c"])])
        st_["z"] = z
        c = emit("scalar", lambda E: E.e.activation(
            dqi[:, pg, 0:width], dqz[:, pg, 0:width], AF.Copy),
            deps=[(sDVE, z), (sDVE, st_["w"])])
        st_["c"] = c
        w = emit("vector", lambda E: E.e.tensor_tensor(
            dq16[:, pg, 0:width].rearrange("p (g e) -> p g e", e=32),
            dqi[:, pg, 0:width].rearrange("p (g e) -> p g e", e=32),
            dqs[:, pg, 0:ng].unsqueeze(2).broadcast_to([128, ng, 32]),
            ALU.mult), deps=[(sACT, c), st_["st"][pg]])
        st_["w"] = w
        stq = dma("gpsimd", ("wdq", pg), wdq_row, dq16[:, pg, 0:width],
                  deps=[(sDVE, w)])
        st_["st"][pg] = stq
        trs = []
        for t in range(nt):
            trs.append(dmat(("ftA", pg), ftA[:, pg, t, :],
                            wdq_row_tile(wdq_row, t),
                            deps=[stq, st_["fs"][pg]]))
        fst = dma("sync", ("wfm", pg), wfm_tiles,
                  ftA[:, pg, 0:nt, :], deps=[trs[-1]])
        st_["ft"][pg] = trs[-1]
        st_["fs"][pg] = fst
        return fst

    def wdq_row_tile(row_ap, t):
        return row_ap[:, t * 128:(t + 1) * 128]

    wfm_store = {}
    _k = 0
    for ic in range(NIC):
        for wt, wsrc, ddst, fdst in (("u", wu_d, wdq_u, wfm_u),
                                     ("g", wg_d, wdq_g, wfm_g)):
            wfm_store[(wt, ic)] = dq_chunk(
                wsrc[ic * 128:(ic + 1) * 128, :], 1024,
                ddst[ic, :, :],
                fdst[ic].rearrange("p (t e) -> p t e", e=128), _k)
            _k += 1
    wd_fst = {}
    for ec in range(8):
        for hv in range(2):
            wd_fst[(ec, hv)] = dq_chunk(
                wd_d[ec * 128:(ec + 1) * 128, hv * WDH:(hv + 1) * WDH], WDH,
                wdq_dd[ec, :, hv * WDH:(hv + 1) * WDH],
                wfm_d[hv * 12:(hv + 1) * 12, :,
                      ec * 128:(ec + 1) * 128].rearrange("i p e -> p i e"),
                _k)
            _k += 1

    def allreduce(idx, extra_deps):
        tr = emit("tensor", lambda E: E.e.transpose(
            pbk[3][0:8, 0:128], rmax[:, 0:8], idn32[:]),
            deps=list(extra_deps) + [(sDVE, d_idn32)])
        rd = emit("vector", lambda E: E.e.reduce_max(
            red8[:], pbk[3][0:8, 0:128], AX.X), deps=[(sPE, tr)])
        st = dma("gpsimd", "cci", cc_i[idx][0:1, 0:8].rearrange("a b -> b a"),
                 red8[:], deps=[(sDVE, rd)])
        ccv = collective(cc_i[idx], cc_o[idx], deps=[st])
        ld = dma("gpsimd", "ccsb", cc_sb[:], cc_o[idx], deps=[(sCC, ccv)])
        return tr, ld

    def bcast_cols(pairs, deps):
        last = 0
        for k, (src, dst) in enumerate(pairs):
            b1 = emit("tensor", lambda E, src=src: E.e.matmul(
                pbk[3][:, 512 + 16 * 0: 512 + 16 * 0 + 1] if False else
                pbk[3][:, 512:513], onesC[0:1, :], src.opt(),
                start=True, stop=True),
                deps=list(deps) + [(sDVE, d_ones), (sDVE, last)])
            last = emit("vector", lambda E, dst=dst: E.e.tensor_copy(
                dst, pbk[3][:, 512:513]), deps=[(sPE, b1)])
        return last

    ar1_tr, ar1_ld = allreduce(0, [(sDVE, a_mx), (sDVE, last_tr_copy)])
    v = emit("vector", lambda E: E.e.tensor_scalar(
        sc8[0:1, 8:9], cc_sb[0:1, 0:1], 1.0 / QMAX, 1e-12, ALU.mult, ALU.max),
        deps=[ar1_ld])
    v = emit("vector", lambda E: E.e.reciprocal(sc8[0:1, 9:10], sc8[0:1, 8:9]),
             deps=[(sDVE, v)])
    ar1_bc = bcast_cols([(sc8[0:1, 9:10], scb[:, 0:1]),
                         (sc8[0:1, 8:9], scb[:, 1:2])], [(sDVE, v)])

    scopes["A"].close()
    xqT = sb("xqT", [128, 8, TCORE], F16, "B")
    xq_i32 = sb("xq_i32", [128, 2, 1024], I32, "B")
    ftile = sb("ftile", [128, 2, 8, 128], F16, "B")
    stgF = sb("stgF", [128, 2, 2, 1024], F16, "B")


    if LIMIT < 1:
        return _finish(nc, prog, scopes, ctx, dsems, Eng, cms)
    xq_done = {}
    xq_prev = [0, 0]
    _kq = 0
    for q4 in range(4):
        for fc in range(8):
            pgq = _kq % 2
            _kq += 1
            c = emit("scalar", lambda E, fc=fc, q4=q4, pgq=pgq: E.e.activation(
                xq_i32[:, pgq, :], xT[:, fc, q4 * 1024:(q4 + 1) * 1024],
                AF.Identity, scale=scb[:, 0:1]),
                deps=[(sDVE, ar1_bc), (sDVE, a_xT), (sDVE, xq_prev[pgq])])
            xq_prev[pgq] = emit("vector",
                lambda E, fc=fc, q4=q4, pgq=pgq: E.e.tensor_scalar(
                    xqT[:, fc, q4 * 1024:(q4 + 1) * 1024], xq_i32[:, pgq, :],
                    scb[:, 1:2], None, ALU.mult), deps=[(sACT, c)])
            if q4 % 2 == 1:
                xq_done[(fc, q4 // 2)] = xq_prev[pgq]

    f1_seq = []
    for ic in range(NIC):
        f1_seq.append(("u", ic))
        f1_seq.append(("g", ic))

    if LIMIT < 2:
        return _finish(nc, prog, scopes, ctx, dsems, Eng, cms)
    def psF(wt, p):
        base = 0 if wt == "u" else 2
        t = pbk[base + p // 2]
        return t[:, (p % 2) * 512:(p % 2) * 512 + 512]

    f1_store = {}
    f1_evA = {}
    f1_evV = {}
    f1_pool = {}
    ftile_rd = [0, 0]
    stg_free = {}
    first_gate_pb3 = True
    seq_all = [(hh, wt, ic) for hh in range(2) for (wt, ic) in f1_seq]

    def emit_ftl(kk):
        hh, wt, ic = seq_all[kk]
        pg = kk % 2
        fsrc = wfm_u if wt == "u" else wfm_g
        return dma("sync", ("ftile", pg), ftile[:, pg, :, :],
                   fsrc[ic].rearrange("p (t e) -> p t e", e=128),
                   deps=[wfm_store[(wt, ic)], (sPE, ftile_rd[pg])])

    ftl_of = {0: emit_ftl(0)}
    for kk, (hh, wt, ic) in enumerate(seq_all):
        pg = kk % 2
        cpg = kk % 2
        sdram = up_s if wt == "u" else gt_s
        if kk + 1 < len(seq_all):
            ftl_of[kk + 1] = emit_ftl(kk + 1)
        ftl = ftl_of[kk]
        mm = 0
        for fc in range(8):
            for p in range(4):
                extra = []
                if wt == "g" and first_gate_pb3 and p >= 2:
                    extra.append((sDVE, ar1_bc))
                mm = emit("tensor", lambda E, fc=fc, p=p, pg=pg, wt=wt, hh=hh:
                    E.e.matmul(psF(wt, p), ftile[:, pg, fc, :],
                               xqT[:, fc, hh * 2048 + p * 512:hh * 2048 + (p + 1) * 512],
                               start=(fc == 0), stop=(fc == 7)),
                    deps=[ftl, (sDVE, xq_done[(fc, hh)]),
                          (sACT, f1_evA.get((wt, hh, ic - 1),
                                 f1_evA.get((wt, hh - 1, NIC - 1), 0))),
                          (sDVE, f1_evV.get((wt, hh, ic - 1),
                                 f1_evV.get((wt, hh - 1, NIC - 1), 0)))] + extra)
        if wt == "g":
            first_gate_pb3 = False
        ftile_rd[pg] = mm
        ev_a = ev_v = 0
        stv = 0
        col = 1 if wt == "u" else 2
        pr = 0
        for half in range(2):
            sdep = [stg_free.get((cpg, half), 0),
                    (sDVE, f1_pool.get((cpg, half), 0))]
            ev_a = emit("scalar", lambda E, half=half, wt=wt, cpg=cpg:
                E.e.activation(stgF[:, cpg, half, 0:512], psF(wt, 2 * half),
                               AF.Copy), deps=[(sPE, mm)] + sdep)
            ev_v = emit("vector", lambda E, half=half, wt=wt, cpg=cpg:
                E.e.tensor_copy(stgF[:, cpg, half, 512:1024],
                                psF(wt, 2 * half + 1)),
                deps=[(sPE, mm)] + sdep)
            stv = dma("sync", ("f1st", cpg, half),
                      sdram[ic, :, hh * 2048 + half * 1024:
                            hh * 2048 + (half + 1) * 1024],
                      stgF[:, cpg, half, :],
                      deps=[(sACT, ev_a), (sDVE, ev_v)])
            stg_free[(cpg, half)] = stv
            f1_store[(wt, hh, ic, half)] = stv
            pr = emit("vector", lambda E, half=half, cpg=cpg: E.e.tensor_reduce(
                tmpB[:, 0:1], stgF[:, cpg, half, :], AX.X, ALU.max,
                apply_absolute_value=True),
                deps=[(sACT, ev_a), (sDVE, ev_v), (sPE, ar1_tr),
                      (sDVE, pr)])
            pr = emit("vector", lambda E, col=col: E.e.tensor_tensor(
                rmax[:, col:col + 1], rmax[:, col:col + 1], tmpB[:, 0:1],
                ALU.max), deps=[(sDVE, pr)])
            if wt == "g":
                pr = emit("vector", lambda E, half=half, cpg=cpg: E.e.tensor_reduce(
                    tmpB[:, 1:2], stgF[:, cpg, half, :], AX.X, ALU.max),
                    deps=[(sDVE, pr)])
                pr = emit("vector", lambda E: E.e.tensor_tensor(
                    rmax[:, 3:4], rmax[:, 3:4], tmpB[:, 1:2], ALU.max),
                    deps=[(sDVE, pr)])
            f1_pool[(cpg, half)] = pr
        f1_evA[(wt, hh, ic)] = ev_a
        f1_evV[(wt, hh, ic)] = ev_v
    b_pe = sPE.n
    b_pool = sPOOL.n
    b_act = sACT.n
    b_dve = sDVE.n

    ar2_tr, ar2_ld = allreduce(1, [(sPOOL, b_pool), (sPE, b_pe),
                                   (sACT, b_act), (sDVE, b_dve)])
    v = emit("vector", lambda E: E.e.tensor_scalar(
        ss[0:1, 0:1], cc_sb[0:1, 1:2], 1.0 / QMAX, 1e-12, ALU.mult, ALU.max),
        deps=[ar2_ld])
    v = emit("vector", lambda E: E.e.tensor_scalar(
        ss[0:1, 1:2], cc_sb[0:1, 2:3], 1.0 / QMAX, 1e-12, ALU.mult, ALU.max),
        deps=[(sDVE, v)])
    v = emit("vector", lambda E: E.e.reciprocal(ss[0:1, 2:3], ss[0:1, 0:1]),
             deps=[(sDVE, v)])
    v = emit("vector", lambda E: E.e.reciprocal(ss[0:1, 3:4], ss[0:1, 1:2]),
             deps=[(sDVE, v)])
    v = emit("vector", lambda E: E.e.tensor_tensor(
        ss[0:1, 4:5], cc_sb[0:1, 3:4], ss[0:1, 3:4], ALU.mult), deps=[(sDVE, v)])
    v = emit("vector", lambda E: E.e.tensor_copy(ssi[0:1, 0:1], ss[0:1, 4:5]),
             deps=[(sDVE, v)])
    v = emit("vector", lambda E: E.e.tensor_tensor(
        ss[0:1, 5:6], ssi[0:1, 0:1], ss[0:1, 1:2], ALU.mult), deps=[(sDVE, v)])
    v = emit("scalar", lambda E: E.e.activation(
        ss[0:1, 6:7], ss[0:1, 5:6], AF.Tanh, scale=0.5), deps=[(sDVE, v)])
    v = emit("vector", lambda E: E.e.tensor_scalar(
        ssi[0:1, 1:2], ss[0:1, 6:7], 32768.0, 32768.0, ALU.mult, ALU.add),
        deps=[(sACT, v)])
    v = emit("vector", lambda E: E.e.tensor_scalar(
        ss[0:1, 7:8], ssi[0:1, 1:2], 1.0 / 65536.0, None, ALU.mult),
        deps=[(sDVE, v)])
    v = emit("vector", lambda E: E.e.tensor_tensor(
        ss[0:1, 8:9], ss[0:1, 5:6], ss[0:1, 7:8], ALU.mult), deps=[(sDVE, v)])
    v = emit("vector", lambda E: E.e.tensor_scalar(
        ss[0:1, 9:10], ss[0:1, 8:9], 1.0 / QMAX, 1e-12, ALU.mult, ALU.max),
        deps=[(sDVE, v)])
    v = emit("vector", lambda E: E.e.tensor_scalar(
        ss[0:1, 10:11], ss[0:1, 9:10], 65536.0, None, ALU.mult), deps=[(sDVE, v)])
    v = emit("vector", lambda E: E.e.reciprocal(ss[0:1, 11:12], ss[0:1, 10:11]),
             deps=[(sDVE, v)])
    v = emit("vector", lambda E: E.e.tensor_tensor(
        ss[0:1, 12:13], ss[0:1, 1:2], ss[0:1, 11:12], ALU.mult), deps=[(sDVE, v)])
    v = emit("vector", lambda E: E.e.tensor_tensor(
        ss[0:1, 13:14], ss[0:1, 9:10], ss[0:1, 0:1], ALU.mult), deps=[(sDVE, v)])
    v = emit("vector", lambda E: E.e.tensor_scalar(
        ss[0:1, 14:15], ss[0:1, 1:2], 0.5, None, ALU.mult), deps=[(sDVE, v)])
    ar2_bc = bcast_cols([(ss[0:1, 2:3], scb[:, 2:3]),
                         (ss[0:1, 3:4], scb[:, 3:4]),
                         (ss[0:1, 14:15], scb[:, 4:5]),
                         (ss[0:1, 12:13], scb[:, 5:6]),
                         (ss[0:1, 13:14], scb[:, 6:7])], [(sDVE, v)])

    scopes["B"].close()
    pbias = sb("pbias", [1, D], F16, "C")
    pwdmp = sb("pwdmp", [128, H, 8, 128], F16, "C")
    f2U = sb("f2U", [128, 2, 1024], F16, "C")
    f2G = sb("f2G", [128, 2, 1024], F16, "C")
    I16 = mybir.dt.int16
    f2ru = sb("f2ru", [128, 1024], I16, "C")
    f2rg = sb("f2rg", [128, 1024], I16, "C")
    f2t1 = sb("f2t1", [128, 1024], F32, "C")
    f2r2 = sb("f2r2", [128, 1024], I32, "C")
    f2o = sb("f2o", [128, 1024], F16, "C")
    stgQ = sb("stgQ", [128, 2, 1024], F16, "C")
    QKT = sb("QKT", [128, 2, 4096], F16, "C")
    Vaug = sb("Vaug", [128, H, 65], F16, "C")
    PT = sb("PT", [128, 2, 512], F16, "C")
    OTn = sb("OTn", [64, 2048], F16, "C")
    scrB = sb("scrB", [128, 2048], F16, "C")
    lcol = sb("lcol", [128, 16], F16, "C")
    rcol = sb("rcol", [128, 16], F32, "C")
    stgAF = sb("stgAF", [128, D], F16, "C")

    d_vone = emit("vector", lambda E: E.e.memset(Vaug[:, :, 64:65], 1.0),
                  deps=[(sPE, b_pe), (sACT, b_act), (sDVE, b_dve)])
    d_pb = dma("gpsimd", "pb", pbias[:], pb_d, deps=[(sDVE, d_vone)])

    if LIMIT < 3:
        return _finish(nc, prog, scopes, ctx, dsems, Eng, cms)
    pw_tr = 0
    for i in range(H):
        for ec in range(8):
            pw_tr = dmat("pwdmp", pwdmp[:, i, ec, :], pw16[ec, :, i, :],
                         deps=[pw_st[ec]])
    pw_tr_all = pw_tr

    f2_list = [(ic, q) for q in range(4) for ic in range(NIC)]
    f2p = {}
    f2_store = {}
    f2_loaded = {}

    def f2_load(k):
        if k >= len(f2_list):
            return
        ic, q = f2_list[k]
        hh = q // 2
        qo = q * 1024
        sl = k % 2
        dl_u = dma("gpsimd", ("f2U", sl), f2U[:, sl, :],
                   up_s[ic, :, qo:qo + 1024],
                   deps=[f1_store[("u", hh, ic, q % 2)],
                         (sACT, f2p.get(("ru", sl), 0))])
        dl_g = dma("gpsimd", ("f2G", sl), f2G[:, sl, :],
                   gt_s[ic, :, qo:qo + 1024],
                   deps=[f1_store[("g", hh, ic, q % 2)],
                         (sACT, f2p.get(("rg", sl), 0))])
        f2_loaded[k] = (dl_u, dl_g)

    def f2_compute(k):
        if k >= len(f2_list):
            return
        ic, q = f2_list[k]
        qo = q * 1024
        sl = k % 2
        pg = k % 2
        dl_u, dl_g = f2_loaded[k]
        ru = emit("scalar", lambda E, sl=sl: E.e.activation(
            f2ru[:], f2U[:, sl, :], AF.Identity, scale=scb[:, 2:3]),
            deps=[dl_u, (sDVE, ar2_bc), (sDVE, f2p.get("t2", 0))])
        rg = emit("scalar", lambda E, sl=sl: E.e.activation(
            f2rg[:], f2G[:, sl, :], AF.Identity, scale=scb[:, 3:4]),
            deps=[dl_g, (sDVE, f2p.get("t1", 0))])
        f2p[("ru", sl)] = ru
        f2p[("rg", sl)] = rg
        tg = emit("scalar", lambda E: E.e.activation(
            f2t1[:], f2rg[:], AF.Tanh, scale=scb[:, 4:5]),
            deps=[(sACT, rg), (sDVE, f2p.get("mx", 0)),
                  (sDVE, f2p.get("t2", 0))])
        rs = emit("vector", lambda E: E.e.tensor_scalar(
            f2r2[:], f2t1[:], 32768.0, 32768.0, ALU.mult, ALU.add),
            deps=[(sACT, tg)])
        t1 = emit("vector", lambda E: E.e.tensor_tensor(
            f2t1[:], f2rg[:], f2r2[:], ALU.mult), deps=[(sDVE, rs)])
        f2p["t1"] = t1
        r2 = emit("vector", lambda E: E.e.tensor_scalar(
            f2r2[:], f2t1[:], scb[:, 5:6], None, ALU.mult), deps=[(sDVE, t1)])
        t2 = emit("vector", lambda E: E.e.tensor_tensor(
            f2t1[:], f2r2[:], f2ru[:], ALU.mult), deps=[(sDVE, r2)])
        f2p["t2"] = t2
        ost = emit("vector", lambda E: E.e.tensor_scalar(
            f2o[:], f2t1[:], scb[:, 6:7], None, ALU.mult),
            deps=[(sDVE, t2), f2p.get("st_dma", 0)])
        mx = emit("vector", lambda E: E.e.tensor_reduce(
            tmpF[:, 0:1], f2t1[:], AX.X, ALU.max, apply_absolute_value=True),
            deps=[(sDVE, ost), (sPE, ar2_tr)])
        mx = emit("vector", lambda E: E.e.tensor_tensor(
            rmax[:, 4:5], rmax[:, 4:5], tmpF[:, 0:1], ALU.max),
            deps=[(sDVE, mx)])
        f2p["mx"] = mx
        st = dma("gpsimd", "f2o", o_s[ic, :, qo:qo + 1024],
                 f2o[:], deps=[(sDVE, ost), (sDVE, mx)])
        f2p["st_dma"] = st
        f2_store[(ic, q)] = st

    blk = {}
    qst = {}
    psS_slot = [pbk[3][:, 0:512], pbk[3][:, 512:1024]]

    def pO_slot(p0, p1):
        if p0 < 1024:
            return pbk[1][0:65, p0:p1]
        return pbk[2][0:65, p0 - 1024:p1 - 1024]

    def pOn_slot(p):
        if p < 2:
            return pbk[1][0:64, p * 512:(p + 1) * 512]
        return pbk[2][0:64, (p - 2) * 512:(p - 1) * 512]

    def emit_qkv(b, pbk0_free_act, extra=()):
        prev = blk.get(b - 1, {})
        ev = 0
        sts = []
        dsts = [qk_s[b, 0:2048, 0:64].rearrange("(t i) d -> t i d", i=H),
                qk_s[b, 2048:4096, 0:64].rearrange("(t i) d -> t i d", i=H),
                v_s[b, :, :].rearrange("(t i) d -> t i d", i=H)]
        for r in range(3):
            rp = r % 2
            mm = 0
            for fc in range(8):
                deps = [(sDVE, a_xT), wqkv_done,
                        (sACT, pbk0_free_act), (sACT, ev)] + list(extra)
                mm = emit("tensor", lambda E, fc=fc, r=r, b=b: E.e.matmul(
                    pbk[0][:, 0:512],
                    xT[:, fc, b * 128:(b + 1) * 128],
                    WQKV[:, fc, r * 1024 + 0:r * 1024 + 512],
                    start=(fc == 0), stop=(fc == 7)), deps=deps)
                mm = emit("tensor", lambda E, fc=fc, r=r, b=b: E.e.matmul(
                    pbk[0][:, 512:1024],
                    xT[:, fc, b * 128:(b + 1) * 128],
                    WQKV[:, fc, r * 1024 + 512:r * 1024 + 1024],
                    start=(fc == 0), stop=(fc == 7)), deps=deps)
            pq = qst.get(b - 1, {})
            war = (sts[0] if r >= 2 else
                   (pq.get("st_k", 0) if r == 1 else pq.get("st_v", 0)))
            for half in range(2):
                ev = emit("scalar", lambda E, rp=rp, half=half: E.e.activation(
                    stgQ[:, rp, half * 512:(half + 1) * 512],
                    pbk[0][:, half * 512:(half + 1) * 512], AF.Copy),
                    deps=[(sPE, mm), war])
            sts.append(dma("sync", ("qkst", r), dsts[r],
                           stgQ[:, rp, :].rearrange("p (i d) -> p i d", d=HS),
                           deps=[(sACT, ev)]))
        st_q, st_k, st_v = sts
        tr = dmat(("QKT", b % 2), QKT[:, b % 2, :], qk_s[b],
                  deps=[st_k, st_q, (sPE, blk.get(b - 2, {}).get("s_last", 0))])
        vrb = dma("sync", "Vaug", Vaug[:, :, 0:64],
                  v_s[b].rearrange("(j p) d -> p j d", p=128),
                  deps=[st_v, (sPE, blk.get(b - 1, {}).get("av", 0)),
                        (sDVE, d_vone)])
        qst[b] = dict(ev=ev, st_v=st_v, st_k=st_k, tr=tr, vrb=vrb)

    emit_qkv(0, 0, extra=[(sPE, b_pe), (sACT, b_act), (sDVE, ar2_bc),
                          (sDVE, b_dve)])

    for b in range(C_BLOCKS):
        pb_ = blk.get(b - 1, {})
        if b >= 1:
            for i in range(3):
                f2_compute(3 * (b - 1) + i)
                f2_load(3 * (b - 1) + i + 2)
        else:
            f2_load(0)
            f2_load(1)
        tr = qst[b]["tr"]
        vrb = qst[b]["vrb"]
        exp_done = {}
        av_ring = {}
        npc = 0
        av = 0
        s_last = 0
        pend_av = None
        for j in range(H):
            lo = j * 128
            bnds = sorted(set([lo] + list(range(512 * (lo // 512 + 1), 2048, 512)) + [2048]))
            for (p0, p1) in zip(bnds[:-1], bnds[1:]):
                pi = npc % 2
                sdeps = [tr, (sACT, exp_done.get(pi, 0)),
                         (sDVE, pb_.get("evb", 0))]
                if b == 0 and npc < 2:
                    sdeps += [(sDVE, ar2_bc), (sDVE, b_dve)]
                smm = emit("tensor", lambda E, j=j, p0=p0, p1=p1, pi=pi, b=b:
                    E.e.matmul(psS_slot[pi][:, 0:p1 - p0],
                               QKT[0:64, b % 2, 2048 + j * 128:2048 + (j + 1) * 128],
                               QKT[0:64, b % 2, p0:p1], start=True, stop=True),
                    deps=sdeps)
                s_last = smm
                if pend_av is not None:
                    fn, avdeps, ppi = pend_av
                    av = emit("tensor", fn, deps=avdeps)
                    av_ring[ppi] = av
                dv = 0
                if p0 == lo:
                    dv = emit("vector", lambda E, pi=pi: E.e.tensor_tensor(
                        psS_slot[pi][:, 0:128], psS_slot[pi][:, 0:128],
                        mtri[:], ALU.add), deps=[(sPE, smm), (sDVE, d_mtri)])
                edeps = [(sPE, smm), (sPE, av_ring.get(pi, 0))]
                if dv:
                    edeps.append((sDVE, dv))
                ex = emit("scalar", lambda E, pi=pi, p0=p0, p1=p1:
                    E.e.activation(PT[:, pi, 0:p1 - p0],
                                   psS_slot[pi][:, 0:p1 - p0], AF.Exp,
                                   scale=0.125), deps=edeps)
                exp_done[pi] = ex
                avdeps = [(sACT, ex), vrb]
                if j == 0:
                    avdeps += [(sDVE, pb_.get("otn", 0)),
                               (sDVE, pb_.get("lc", 0))]
                pend_av = (lambda E, j=j, p0=p0, p1=p1, pi=pi:
                    E.e.matmul(pO_slot(p0, p1), Vaug[:, j, :],
                               PT[:, pi, 0:p1 - p0],
                               start=(j == 0), stop=(j == H - 1)),
                    avdeps, pi)
                npc += 1
        fn, avdeps, ppi = pend_av
        av = emit("tensor", fn, deps=avdeps)
        av_ring[ppi] = av
        av_done = av
        lc = 0
        for half in range(2):
            lc = emit("vector", lambda E, half=half: E.e.tensor_copy(
                scrB[64:65, half * 1024:(half + 1) * 1024],
                pbk[1 + half][64:65, 0:1024]),
                deps=[(sPE, av_done), pb_.get("l_st", 0),
                      (sPE, pb_.get("bcast", 0))])
        l_st = dma("sync", "l_s", l_s[b, :].unsqueeze(0), scrB[64:65, :],
                   deps=[(sDVE, lc)])
        l_ld = dma("sync", "lcol", lcol[:],
                   l_s[b, :].rearrange("(p k) -> p k", p=128),
                   deps=[l_st, (sDVE, pb_.get("rc", 0))])
        rc = emit("vector", lambda E: E.e.reciprocal(rcol[:], lcol[:]),
                  deps=[l_ld])
        rl_st = dma("gpsimd", "rl_s",
                    rl_s[b, :].rearrange("(p k) -> p k", p=128), rcol[:],
                    deps=[(sDVE, rc)])
        rl_ld = dma("sync", "rlrow", scrB[0:1, :], rl_s[b, :].unsqueeze(0),
                    deps=[rl_st, l_st, (sDVE, pb_.get("otn", 0))])
        if b + 1 < C_BLOCKS:
            emit_qkv(b + 1, pb_.get("af_ev", 0))
        otn = 0
        bca = 0
        last_evb = 0
        evb_of = {}
        for p in range(4):
            pi = p % 2
            bb = emit("tensor", lambda E, p=p, pi=pi: E.e.matmul(
                psS_slot[pi][0:64, 0:512], onesC16[0:1, 0:64],
                scrB[0:1, p * 512:(p + 1) * 512], start=True, stop=True),
                deps=[rl_ld, (sACT, exp_done.get(pi, 0)),
                      (sDVE, evb_of.get(p - 2, 0)), (sDVE, d_ones)])
            bca = bb
            evb = emit("vector", lambda E, p=p, pi=pi: E.e.tensor_copy(
                scrB[0:64, p * 512:(p + 1) * 512], psS_slot[pi][0:64, 0:512]),
                deps=[(sPE, bb), (sDVE, pb_.get("otn", 0))])
            evb_of[p] = evb
            last_evb = evb
            otn = emit("vector", lambda E, p=p: E.e.tensor_tensor(
                OTn[:, p * 512:(p + 1) * 512], pOn_slot(p),
                scrB[0:64, p * 512:(p + 1) * 512], ALU.mult),
                deps=[(sDVE, evb), (sPE, av_done), (sPE, pb_.get("proj", 0))])
        pm = 0
        for p in range(2):
            pm = emit("tensor", lambda E, p=p: E.e.matmul(
                pbk[0][:, p * 512:(p + 1) * 512], onesC16[0:1, :],
                pbias[0:1, p * 512:(p + 1) * 512], start=True, stop=False),
                deps=[d_pb, (sDVE, otn), (sDVE, d_ones),
                      (sACT, qst.get(b + 1, {}).get("ev", 0)),
                      pw_tr_all])
        for i in range(H):
            for p in range(2):
                pm = emit("tensor", lambda E, i=i, p=p: E.e.matmul(
                    pbk[0][:, p * 512:(p + 1) * 512],
                    OTn[:].rearrange("d (t i) -> d i t", i=H)[:, i, :],
                    pwdmp[0:64, i, :, :].rearrange("d a b -> d (a b)")[:, p * 512:(p + 1) * 512],
                    start=False, stop=(i == H - 1)), deps=[(sDVE, otn)])
        prj = pm
        af_ev = emit("scalar", lambda E: E.e.activation(
            stgAF[:], pbk[0][:, :], AF.Copy),
            deps=[(sPE, prj), pb_.get("af_st", 0)])
        af_st = dma("sync", "stgAF", af_s[b], stgAF[:], deps=[(sACT, af_ev)])

        blk[b] = dict(av=av_done, otn=otn, proj=prj, af_ev=af_ev,
                      af_st=af_st, rc=rc, lc=lc, l_st=l_st, bcast=bca,
                      evb=last_evb, s_last=s_last)
    for kf in range(3 * (C_BLOCKS - 1), len(f2_list)):
        f2_compute(kf)
        if kf + 2 < len(f2_list):
            f2_load(kf + 2)
    c_pe = sPE.n
    c_dve = sDVE.n
    c_act = sACT.n
    c_pool = sPOOL.n

    ar3_tr, ar3_ld = allreduce(2, [(sDVE, c_dve), (sPE, c_pe),
                                   (sACT, c_act), (sPOOL, c_pool)])
    v = emit("vector", lambda E: E.e.tensor_tensor(
        ss[0:1, 15:16], cc_sb[0:1, 4:5], ss[0:1, 13:14], ALU.mult),
        deps=[ar3_ld])
    v = emit("vector", lambda E: E.e.tensor_scalar(
        ss[0:1, 15:16], ss[0:1, 15:16], 1.0 / QMAX, 1e-12, ALU.mult, ALU.max),
        deps=[(sDVE, v)])
    v = emit("vector", lambda E: E.e.reciprocal(sc8[0:1, 0:1], ss[0:1, 15:16]),
             deps=[(sDVE, v)])
    ar3_bc = bcast_cols([(sc8[0:1, 0:1], scb2[:, 0:1]),
                         (ss[0:1, 15:16], scb2[:, 1:2])], [(sDVE, v)])

    if LIMIT < 4:
        return _finish(nc, prog, scopes, ctx, dsems, Eng, cms)
    scopes["C"].close()
    cms["xT"].__exit__(None, None, None)
    cms["WQKV"].__exit__(None, None, None)
    WdT = sb("WdT", [128, NIC, 1024], F16, "D")
    oq = sb("oq", [128, 2, NIC, 1024], F16, "D")
    o_ld = sb("o_ld", [128, 2, 1024], F16, "D")
    o_i32 = sb("o_i32", [128, 2, 1024], I32, "D")
    outstg = sb("outstg", [128, 2, D], F32, "D")
    xres = sb("xres", [128, 2, D], F32, "D")
    afres = sb("afres", [128, 2, D], F16, "D")

    wdt_tr = 0
    for ic in range(NIC):
        wdt_tr = dma("sync", "WdT", WdT[:, ic, :], wfm_d[ic],
                     deps=[wd_fst[(7, 0)], wd_fst[(7, 1)],
                           (sPE, c_pe), (sACT, c_act), (sDVE, c_dve),
                           (sPOOL, c_pool)])
    wdt_all = wdt_tr

    oq_done = {}
    dn_done = {}
    res_done = {}
    out_dma = {}
    oq_cast = [0, 0]
    oq_sc = [0, 0]
    for g in range(4):
        gg = g % 2
        oqv = 0
        for ic in range(NIC):
            pg2 = ic % 2
            dl = dma("gpsimd", ("old", pg2), o_ld[:, pg2, :],
                     o_s[ic, :, g * 1024:(g + 1) * 1024],
                     deps=[f2_store[(ic, g)],
                           (sACT, oq_cast[pg2]), (sDVE, ar3_bc)])
            c = emit("scalar", lambda E, pg2=pg2: E.e.activation(
                o_i32[:, pg2, :], o_ld[:, pg2, :], AF.Identity, scale=scb2[:, 0:1]),
                deps=[dl, (sDVE, ar3_bc), (sDVE, oq_sc[pg2])])
            oq_cast[pg2] = c
            svv = emit("vector", lambda E, ic=ic, gg=gg, pg2=pg2: E.e.tensor_scalar(
                oq[:, gg, ic, :], o_i32[:, pg2, :], scb2[:, 1:2], None, ALU.mult),
                deps=[(sACT, c), (sPE, dn_done.get((g - 2, 7), 0))])
            oq_sc[pg2] = svv
            oqv = svv
        oq_done[g] = oqv
        for tt in range(8):
            t_abs = g * 8 + tt
            pgd = tt % 2
            mm = 0
            war = res_done.get((g, tt - 2), 0) if tt >= 2 else \
                res_done.get((g - 1, 6 + tt), 0)
            for ic in range(NIC):
                for p in range(2):
                    deps = [(sDVE, oq_done[g]), wdt_all,
                            (sDVE, war)]
                    if g == 0 and tt < 2:
                        deps += [(sACT, c_act), (sDVE, c_dve), (sPE, c_pe)]
                    mm = emit("tensor", lambda E, ic=ic, p=p, tt=tt, gg=gg, pgd=pgd:
                        E.e.matmul(pbk[pgd][:, p * 512:(p + 1) * 512],
                                   oq[:, gg, ic, tt * 128:(tt + 1) * 128],
                                   WdT[:, ic, p * 512:(p + 1) * 512],
                                   start=(ic == 0), stop=(ic == NIC - 1)),
                        deps=deps)
            dn_done[(g, tt)] = mm
            xl = dma("sync", ("xres", pgd), xres[:, pgd, :],
                     x_d[t_abs * 128:(t_abs + 1) * 128, :],
                     deps=[(sDVE, war), (sDVE, ar3_bc)])
            al = dma("sync", ("afres", pgd), afres[:, pgd, :], af_s[t_abs],
                     deps=[blk[t_abs]["af_st"], (sDVE, war),
                           (sDVE, ar3_bc)])
            r1 = emit("vector", lambda E, pgd=pgd: E.e.tensor_tensor(
                outstg[:, pgd, :], pbk[pgd][:, :], afres[:, pgd, :], ALU.add),
                deps=[(sPE, mm), al, out_dma.get((g, tt - 2),
                      out_dma.get((g - 1, 6 + tt), 0))])
            r2 = emit("vector", lambda E, pgd=pgd: E.e.tensor_tensor(
                outstg[:, pgd, :], outstg[:, pgd, :], xres[:, pgd, :], ALU.add),
                deps=[(sDVE, r1), xl])
            res_done[(g, tt)] = r2
            out_dma[(g, tt)] = dma("sync", ("out", pgd),
                out_d[t_abs * 128:(t_abs + 1) * 128, :],
                outstg[:, pgd, :], deps=[(sDVE, r2)])

    scopes["D"].close()
    return _finish(nc, prog, scopes, ctx, dsems, Eng, cms)


def _finish(nc, prog, scopes, ctx, dsems, Eng, cms):
    for sc in scopes.values():
        try:
            sc.close()
        except Exception:
            pass
    for nm in ("xT", "WQKV"):
        try:
            cms[nm].__exit__(None, None, None)
        except Exception:
            pass
    finals = [(ct, ct.n) for ct in dsems.values() if ct.n > 0]

    with nc.Block() as block:
        @block.sync
        def _(sync):
            E = Eng(sync)
            for fn in prog["sync"]:
                fn(E)
            for ct, v in finals:
                sync.wait_ge(ct.h, v)

        @block.tensor
        def _(tensor):
            E = Eng(tensor)
            for fn in prog["tensor"]:
                fn(E)

        @block.scalar
        def _(scalar):
            E = Eng(scalar)
            for fn in prog["scalar"]:
                fn(E)

        @block.vector
        def _(vector):
            E = Eng(vector)
            for fn in prog["vector"]:
                fn(E)

        @block.gpsimd
        def _(gpsimd):
            E = Eng(gpsimd)
            for fn in prog["gpsimd"]:
                fn(E)

    ctx.close()
    return nc


_CACHE = {}


def kernel(L0_attn_x, L0_attn_Wq, L0_attn_Wk, L0_attn_Wv, L0_attn_proj_w,
           L0_attn_proj_b, L0_ffn_w_gate, L0_ffn_w_up, L0_ffn_w_down):
    if "nc" not in _CACHE:
        _CACHE["nc"] = build_program()
    nc = _CACHE["nc"]
    x = np.ascontiguousarray(np.asarray(L0_attn_x, dtype=np.float32))
    shared = dict(
        wq=np.ascontiguousarray(np.asarray(L0_attn_Wq, dtype=np.float32)),
        wk=np.ascontiguousarray(np.asarray(L0_attn_Wk, dtype=np.float32)),
        wv=np.ascontiguousarray(np.asarray(L0_attn_Wv, dtype=np.float32)),
        pw=np.ascontiguousarray(np.asarray(L0_attn_proj_w, dtype=np.float32)),
        pb=np.ascontiguousarray(np.asarray(L0_attn_proj_b, dtype=np.float32)).reshape(1, D),
        wg=np.ascontiguousarray(np.asarray(L0_ffn_w_gate, dtype=np.float32)),
        wu=np.ascontiguousarray(np.asarray(L0_ffn_w_up, dtype=np.float32)),
        wd=np.ascontiguousarray(np.asarray(L0_ffn_w_down, dtype=np.float32)),
    )
    in_maps = []
    for k in range(NCORES):
        m = dict(shared)
        m["x"] = np.ascontiguousarray(x[2 * k:2 * k + 2].reshape(TCORE, D))
        in_maps.append(m)
    res = run_bass_kernel_spmd(nc, in_maps, core_ids=list(range(NCORES)))
    parts = [np.asarray(r["out"]).reshape(2, 2048, D) for r in res.results]
    return np.concatenate(parts, axis=0)
